# version 06
# baseline (speedup 1.0000x reference)
import os
import numpy as np
import ml_dtypes
from contextlib import ExitStack

PARANOID = os.environ.get("MK_PARANOID", "0") == "1"
RELAX = set(int(x) for x in os.environ.get("MK_RELAX", "1,2,3,4").split(",") if x)
RELAX3 = set(os.environ.get("MK_RELAX3", ""))
DEBUG = os.environ.get("MK_DEBUG", "0") == "1"
S2CUT = int(os.environ.get("MK_S2_CUT", "9"))
S3CUT = int(os.environ.get("MK_S3_CUT", "9"))

import concourse.bass as bass
import concourse.mybir as mybir
from concourse.bass_utils import run_bass_kernel_spmd

F32 = mybir.dt.float32
BF16 = mybir.dt.bfloat16
I8 = mybir.dt.int8
AF = mybir.ActivationFunctionType
ALU = mybir.AluOpType
AX = mybir.AxisListType.X

B, T, D = 2, 8192, 1024
TPC = 2048
NCH = D // 128
NG, GQ = 4, 512
NHEADS, DH = 16, 64
NEG = -1e30



def make_inputs(inp):
    x = np.asarray(inp["L0_attn_x"])
    maps = []
    ones_row = np.ones((1, 128), np.float32)
    sel01 = np.zeros((2, 2, 128), np.float32)
    sel01[0, 0, :] = 1.0
    sel01[1, 1, :] = 1.0
    ji = np.arange(128)
    m_tri = np.tile(np.where(ji[:, None] <= ji[None, :], 0.0, NEG), (1, 4)).astype(ml_dtypes.bfloat16)
    wqkvT = np.ascontiguousarray(np.asarray(inp["L0_attn_w_qkv"]).T)
    woutT = np.ascontiguousarray(np.asarray(inp["L0_attn_w_out"]).T)
    wbase = np.ascontiguousarray(np.asarray(inp["L0_ffn_W_base"]))
    bbase = np.ascontiguousarray(np.asarray(inp["L0_ffn_b_base"]))
    wd1 = np.ascontiguousarray(np.asarray(inp["L0_ffn_Wd1"]).transpose(1, 0, 2).reshape(D, 8 * 256))
    bd1 = np.ascontiguousarray(np.asarray(inp["L0_ffn_bd1"]).reshape(-1))
    wd2 = np.ascontiguousarray(np.asarray(inp["L0_ffn_Wd2"]).reshape(8 * 256, D))
    bd2 = np.ascontiguousarray(np.asarray(inp["L0_ffn_bd2"]).reshape(-1))
    adn = np.ascontiguousarray(np.asarray(inp["L0_ffn_Wa_down"]).transpose(1, 0, 2).reshape(D, 512))
    badn = np.ascontiguousarray(np.asarray(inp["L0_ffn_ba_down"]).reshape(-1))
    aup = np.ascontiguousarray(np.asarray(inp["L0_ffn_Wa_up"]).reshape(512, D))
    pnw = np.ascontiguousarray(np.asarray(inp["L1_attn_pre_norm_w"]))
    fnw = np.ascontiguousarray(np.asarray(inp["L1_attn_ffn_norm_w"]))
    convw = np.ascontiguousarray(np.asarray(inp["L1_attn_conv_w"]).reshape(D, 4))
    convb = np.ascontiguousarray(np.asarray(inp["L1_attn_conv_b"]))
    waT = np.ascontiguousarray(np.asarray(inp["L1_attn_w_a"]).T)
    wiT = np.ascontiguousarray(np.asarray(inp["L1_attn_w_i"]).T)
    wvT = np.ascontiguousarray(np.asarray(inp["L1_attn_w_v"]).T)
    wopT = np.ascontiguousarray(np.asarray(inp["L1_attn_out_proj_w"]).T)
    dbias = np.ascontiguousarray(np.asarray(inp["L1_attn_decay_bias"]))
    wgu = np.asarray(inp["L1_attn_w_gate_up"]).T
    wguT = np.empty_like(wgu)
    for j in range(32):
        wguT[:, 256 * j:256 * j + 128] = wgu[:, 128 * j:128 * (j + 1)]
        wguT[:, 256 * j + 128:256 * j + 256] = wgu[:, 4096 + 128 * j:4096 + 128 * (j + 1)]
    wguT = np.ascontiguousarray(wguT)
    wdnT = np.ascontiguousarray(np.asarray(inp["L1_attn_w_down"]).T)
    wqT = np.ascontiguousarray(np.asarray(inp["L1_ffn_weight_int8"]).T)
    wsc = np.ascontiguousarray(np.asarray(inp["L1_ffn_weight_scale"]).reshape(-1))
    qbias = np.ascontiguousarray(np.asarray(inp["L1_ffn_bias"]))
    bup = np.ascontiguousarray(np.asarray(inp["L0_ffn_ba_up"]).reshape(8, 8, 128).transpose(2, 1, 0).reshape(128, 64))
    for c in range(8):
        b, t0 = c // 4, (c % 4) * TPC
        prevhot = np.zeros((8, 1), np.float32)
        if c % 4 > 0:
            prevhot[c - 1, 0] = 1.0
        predselT = np.zeros((8, 3), np.float32)
        for i in range(3):
            if c % 4 - 1 - i >= 0:
                predselT[c - 1 - i, i] = 1.0
        x_f = np.ascontiguousarray(x[b, t0:t0 + TPC, :].T)
        xprev = (np.ascontiguousarray(x[b, t0 - 128:t0, :].T) if t0 >= 128
                 else np.zeros((D, 128), np.float32))
        xstr = np.ascontiguousarray(x[b, 0::64, :].T)
        qg = t0 + np.arange(TPC)[None, :]
        ki = np.arange(128)[:, None]
        kg = (qg // 128) * 128 - 128 + ki
        allow_band = (kg >= 0) & (kg <= qg) & (((qg - kg) <= 128) | (kg % 64 == 0))
        m_band = np.where(allow_band, 0.0, NEG).astype(ml_dtypes.bfloat16)
        q0sub = (qg // 128) * 128
        allow_str = (64 * np.arange(128)[:, None]) < (q0sub - 128)
        m_str = np.where(allow_str, 0.0, NEG).astype(ml_dtypes.bfloat16)
        maps.append({
            "x_f": x_f, "xprev": xprev, "xstr": xstr,
            "m_tri": m_tri, "m_band": m_band, "m_str": m_str,
            "ones_row": ones_row, "sel01": sel01,
            "wqkvT": wqkvT, "woutT": woutT,
            "wbase": wbase, "bbase": bbase, "wd1": wd1, "bd1": bd1, "wd2": wd2,
            "bd2": bd2, "adn": adn, "badn": badn, "aup": aup, "bup": bup,
            "prevhot": prevhot, "predselT": predselT,
            "pnw": pnw, "fnw": fnw, "convw": convw, "convb": convb,
            "waT": waT, "wiT": wiT, "wvT": wvT, "wopT": wopT, "dbias": dbias,
            "wguT": wguT, "wdnT": wdnT,
            "wqT": wqT, "wsc": wsc, "qbias": qbias,
        })
    return maps



class MK:
    def __init__(self, nc, block, es, upto):
        self.nc, self.block, self.es, self.upto = nc, block, es, upto
        self.h, self.n = {}, {}
        for name in ("dL", "dG", "dS", "T", "V", "SC", "CC", "dO"):
            self.h[name] = es.enter_context(nc.semaphore(name))
            self.n[name] = 0
        self.pb = [es.enter_context(nc.psum_tensor(f"pb{i}", [128, 512], F32)) for i in range(8)]

    def inc(self, instr, name, amt=None):
        if amt is None:
            amt = 16 if name.startswith("d") else 1
        instr.then_inc(self.h[name], amt)
        self.n[name] += amt

    def wait(self, eng, name, val=None):
        if PARANOID or not getattr(self, "relaxed", True):
            for nm in self.h:
                if nm != "dO":
                    eng.wait_ge(self.h[nm], self.n[nm])
            return
        eng.wait_ge(self.h[name], self.n[name] if val is None else val)

    def wcomp(self, eng):
        for nm in ("T", "V", "SC"):
            self.wait(eng, nm)

    _uid = 0

    def sb(self, name, shape, dt=F32, es=None):
        MK._uid += 1
        return (es or self.es).enter_context(self.nc.sbuf_tensor(f"{name}_u{MK._uid}", shape, dt))

    def sec(self, eng):
        return getattr(self.block, eng)

    @staticmethod
    def split(v, src32, hi, lo):
        v.tensor_copy(hi, src32)
        v.tensor_tensor(lo, src32, hi, ALU.subtract)

    def declare_io(self):
        nc = self.nc
        self.hb = {}

        def hin(name, shape, dt=F32):
            self.hb[name] = nc.dram_tensor(name, list(shape), dt, kind="ExternalInput").ap()

        hin("x_f", (D, TPC)); hin("xprev", (D, 128)); hin("xstr", (D, 128))
        hin("m_tri", (128, 512), BF16); hin("m_band", (128, TPC), BF16); hin("m_str", (128, TPC), BF16)
        hin("ones_row", (1, 128)); hin("sel01", (2, 2, 128))
        hin("wqkvT", (D, 3 * D)); hin("woutT", (D, D))
        hin("wbase", (D, D)); hin("bbase", (D,)); hin("wd1", (D, 2048)); hin("bd1", (2048,))
        hin("wd2", (2048, D)); hin("bd2", (8192,)); hin("adn", (D, 512)); hin("badn", (512,))
        hin("aup", (512, D)); hin("bup", (128, 64))
        hin("pnw", (D,)); hin("fnw", (D,)); hin("convw", (D, 4)); hin("convb", (D,))
        hin("waT", (D, D)); hin("wiT", (D, D)); hin("wvT", (D, D)); hin("wopT", (D, D))
        hin("dbias", (D,)); hin("wguT", (D, 8192)); hin("wdnT", (4096, D))
        hin("prevhot", (8, 1)); hin("predselT", (8, 3))
        hin("wqT", (D, D), I8); hin("wsc", (D,)); hin("qbias", (D,))
        self.out_h = nc.dram_tensor("out", [D, TPC], F32, kind="ExternalOutput").ap()
        if DEBUG:
            self.dbg = {}
            for nm, shape in [("dbg_xn", (D, 640)), ("dbg_q", (D, 512)), ("dbg_k", (D, 640)),
                              ("dbg_v", (640, D)), ("dbg_kst", (D, 128)), ("dbg_vst", (128, D)),
                              ("dbg_att", (128, 8 * 512)), ("dbg_attl", (128, 8 * 512)),
                              ("dbg_E", (128, 3 * 512)), ("dbg_ab", (128, 1280)),
                              ("dbg_loss", (1, 8)), ("dbg_g8", (1, 8)), ("dbg_GG", (128, 8)),
                              ("dbg_G2", (128, 4)), ("dbg_bvec", (128, 8)), ("dbg_lossr", (128, 8)),
                              ("dbg_a0", (128, 512)), ("dbg_h0", (128, 512)),
                              ("dbg_x2b", (D, TPC))]:
                self.dbg[nm] = nc.dram_tensor(nm, list(shape), F32, kind="ExternalOutput").ap()

    def build(self):
        self.declare_io()
        c = self
        self.resid = [c.sb(f"resid{k}", [128, TPC]) for k in range(NCH)]
        self.onesr = c.sb("onesr", [1, 128])
        self.onesc_b = c.sb("onesc_b", [128, 1], BF16)
        self.eps_ln = c.sb("eps_ln", [1, 1])

        @c.sec("sync")
        def _(e):
            for k in range(NCH):
                c.inc(e.dma_start(out=c.resid[k][:], in_=c.hb["x_f"][k * 128:(k + 1) * 128, :]), "dL")
            c.inc(e.dma_start(out=c.onesr[:], in_=c.hb["ones_row"]), "dL")

        @c.sec("vector")
        def _(v):
            c.wait(v, "dL")
            v.memset(c.onesc_b[:], 1.0)
            c.inc(v.memset(c.eps_ln[:], 1e-5), "V")

        if c.upto >= 1:
            c.relaxed = 1 in RELAX
            with ExitStack() as st_es:
                c.s1_l0attn(st_es)
        if c.upto >= 2:
            c.relaxed = 2 in RELAX
            with ExitStack() as st_es:
                c.s2_l0ffn(st_es)
        if c.upto >= 3:
            c.relaxed = 3 in RELAX
            with ExitStack() as st_es:
                c.s3_l1attn(st_es)
        if c.upto >= 4:
            c.relaxed = 4 in RELAX
            with ExitStack() as st_es:
                c.s4_l1ffn(st_es)
        c.relaxed = False

        snapT, snapV, snapS = c.n["T"], c.n["V"], c.n["SC"]

        if DEBUG and c.upto >= 3:
            @c.sec("gpsimd")
            def _(g):
                c.wcomp(g)
                c.inc(g.dma_start(out=c.dbg["dbg_vst"][0:128, 0:64], in_=self.dbgt_ref[:]), "dO")
                c.inc(g.dma_start(out=c.dbg["dbg_att"][:, 0:256], in_=self.dbg_Lsc[:]), "dO")
                c.inc(g.dma_start(out=c.dbg["dbg_att"][:, 256:512], in_=self.dbg_Cpr[:]), "dO")
                c.inc(g.dma_start(out=c.dbg["dbg_att"][:, 512:520], in_=self.dbg_Sloc[:]), "dO")
                c.inc(g.dma_start(out=c.dbg["dbg_att"][:, 520:528], in_=self.dbg_Ptot[:]), "dO")
                c.inc(g.dma_start(out=c.dbg["dbg_att"][:, 528:536], in_=self.dbg_seff0[:]), "dO")
                c.wait(g, "dO")
                c.inc(g.dma_start(out=self.dbg_hgrow[0:1, 0:64], in_=self.dbg_rech[0:1, 0:64]), "dG")
                c.inc(g.dma_start(out=self.dbg_hgrow[0:1, 64:128], in_=self.dbg_recl[0:1, 0:64]), "dG")
                c.wait(g, "dG")
                c.inc(g.dma_start(out=c.dbg["dbg_att"][0:1, 536:600], in_=self.dbg_hgrow[0:1, 0:64]), "dO")
                c.inc(g.dma_start(out=c.dbg["dbg_att"][0:1, 600:664], in_=self.dbg_hgrow[0:1, 64:128]), "dO")

        @c.sec("gpsimd")
        def _(g):
            g.wait_ge(c.h["T"], snapT)
            g.wait_ge(c.h["V"], snapV)
            g.wait_ge(c.h["SC"], snapS)
            for k in range(NCH):
                c.inc(g.dma_start(out=c.out_h[k * 128:(k + 1) * 128, :], in_=c.resid[k][:]), "dO")
            c.wait(g, "dO")

    def ln_stats(self, ntok, srcfn, nsl, dram_rows, eps=1e-5):
        c = self
        ps_sum, ps_sq = c.pb[0], c.pb[1]
        with ExitStack() as les:
            sxh = c.sb("st_sxh", [128, 512], BF16, les)
            sxl = c.sb("st_sxl", [128, 512], BF16, les)
            sq32 = c.sb("st_sq32", [128, 512], F32, les)
            sqh = c.sb("st_sqh", [128, 512], BF16, les)
            sql = c.sb("st_sql", [128, 512], BF16, les)
            srow = c.sb("st_srow", [1, ntok], F32, les)
            qrow = c.sb("st_qrow", [1, ntok], F32, les)
            arow = c.sb("st_arow", [1, ntok], F32, les)
            brow = c.sb("st_brow", [1, ntok], F32, les)
            epsv = c.sb("st_eps", [1, 1], F32, les)

            @c.sec("vector")
            def _(v):
                c.wcomp(v)
                c.inc(v.memset(epsv[:], eps), "V")

            for (j0, w) in nsl:
                for k in range(NCH):
                    @c.sec("vector")
                    def _(v, k=k, j0=j0, w=w):
                        c.wait(v, "T")
                        c.split(v, srcfn(k, j0, w), sxh[:, 0:w], sxl[:, 0:w])
                        c.inc(v.engine_nop(), "V")

                    @c.sec("scalar")
                    def _(s, k=k, j0=j0, w=w):
                        c.wait(s, "V")
                        c.wait(s, "T")
                        c.inc(s.activation(sq32[:, 0:w], srcfn(k, j0, w), AF.Square), "SC")

                    @c.sec("vector")
                    def _(v, w=w):
                        c.wait(v, "SC")
                        c.split(v, sq32[:, 0:w], sqh[:, 0:w], sql[:, 0:w])
                        c.inc(v.engine_nop(), "V")

                    @c.sec("tensor")
                    def _(t, k=k, w=w):
                        c.wait(t, "V")
                        st = (k == 0)
                        fin = (k == NCH - 1)
                        t.matmul(ps_sum[0:1, 0:w], c.onesc_b[:], sxh[:, 0:w], start=st, stop=False)
                        t.matmul(ps_sum[0:1, 0:w], c.onesc_b[:], sxl[:, 0:w], start=False, stop=fin)
                        t.matmul(ps_sq[0:1, 0:w], c.onesc_b[:], sqh[:, 0:w], start=st, stop=False)
                        c.inc(t.matmul(ps_sq[0:1, 0:w], c.onesc_b[:], sql[:, 0:w], start=False, stop=fin), "T")

                @c.sec("vector")
                def _(v, j0=j0, w=w):
                    c.wait(v, "T")
                    v.tensor_scalar_mul(srow[:, j0:j0 + w], ps_sum[0:1, 0:w], 1.0 / D)
                    c.inc(v.tensor_scalar_mul(qrow[:, j0:j0 + w], ps_sq[0:1, 0:w], 1.0 / D), "V")

            @c.sec("vector")
            def _(v):
                v.tensor_tensor(arow[:, 0:ntok], srow[:, 0:ntok], srow[:, 0:ntok], ALU.mult)
                c.inc(v.tensor_tensor(qrow[:, 0:ntok], qrow[:, 0:ntok], arow[:, 0:ntok], ALU.subtract), "V")

            @c.sec("scalar")
            def _(s):
                c.wait(s, "V")
                s.activation(arow[:, 0:ntok], qrow[:, 0:ntok], AF.Ln, bias=epsv[0:1, 0:1])
                c.inc(s.activation(arow[:, 0:ntok], arow[:, 0:ntok], AF.Exp, scale=-0.5), "SC")

            @c.sec("vector")
            def _(v):
                c.wait(v, "SC")
                v.tensor_tensor(brow[:, 0:ntok], srow[:, 0:ntok], arow[:, 0:ntok], ALU.mult)
                c.inc(v.tensor_scalar_mul(brow[:, 0:ntok], brow[:, 0:ntok], -1.0), "V")

            @c.sec("gpsimd")
            def _(g):
                c.wait(g, "V")
                c.inc(g.dma_start(out=dram_rows[0:1, 0:ntok], in_=arow[:, 0:ntok]), "dG")
                c.inc(g.dma_start(out=dram_rows[1:2, 0:ntok], in_=brow[:, 0:ntok]), "dG")

            @c.sec("vector")
            def _(v):
                c.wait(v, "dG")
                c.inc(v.engine_nop(), "V")

    def rep(self, row, so, dst, do, w):
        c = self
        pr = c.pb[2]

        @c.sec("tensor")
        def _(t):
            c.wait(t, "V")
            c.wait(t, "dL")
            c.inc(t.matmul(pr[:, 0:w], c.onesr[:], row[0:1, so:so + w], start=True, stop=True), "T")

        @c.sec("vector")
        def _(v):
            c.wait(v, "T")
            c.inc(v.tensor_copy(dst[:, do:do + w], pr[:, 0:w]), "V")

    def s1_l0attn(self, ses):
        c = self
        hb, resid = self.hb, self.resid
        NTOK = TPC + 256

        xpv = [c.sb(f"xpv{k}", [128, 256], F32, ses) for k in range(NCH)]
        abrows = self.nc.dram_tensor("abrows", [2, NTOK], F32).ap()
        arow_g = c.sb("arow_g", [1, 768], F32, ses)
        brow_g = c.sb("brow_g", [1, 768], F32, ses)
        mtri = c.sb("mtri", [128, 512], BF16, ses)
        mband = c.sb("mband", [128, 512], BF16, ses)
        mstr = c.sb("mstr", [128, 512], BF16, ses)

        @c.sec("sync")
        def _(e):
            for k in range(NCH):
                c.inc(e.dma_start(out=xpv[k][:, 0:128], in_=hb["xprev"][k * 128:(k + 1) * 128, :]), "dL")
                c.inc(e.dma_start(out=xpv[k][:, 128:256], in_=hb["xstr"][k * 128:(k + 1) * 128, :]), "dL")
            c.inc(e.dma_start(out=mtri[:], in_=hb["m_tri"]), "dL")

        @c.sec("vector")
        def _(v):
            c.wait(v, "dL")
            c.inc(v.engine_nop(), "V")

        def srcfn(k, j0, w):
            if j0 < TPC:
                return resid[k][:, j0:j0 + w]
            return xpv[k][:, j0 - TPC:j0 - TPC + w]

        nsl = [(i * 512, 512) for i in range(4)] + [(TPC, 256)]
        c.ln_stats(NTOK, srcfn, nsl, abrows)

        ab = c.sb("ab", [128, 1280], F32, ses)
        abs_ = c.sb("abs", [128, 256], F32, ses)
        U = c.sb("uarena", [128, 10240], BF16, ses)
        xng_h = [U[:, k * 640:(k + 1) * 640] for k in range(NCH)]
        xng_l = [U[:, 5120 + k * 640:5120 + (k + 1) * 640] for k in range(NCH)]
        atth = U[:, 0:4096]
        attl = U[:, 4096:8192]
        xns_h = [c.sb(f"xns_h{k}", [128, 128], BF16, ses) for k in range(NCH)]
        xns_l = [c.sb(f"xns_l{k}", [128, 128], BF16, ses) for k in range(NCH)]
        q32 = [c.sb(f"q32_{k}", [128, GQ], F32, ses) for k in range(NCH)]
        k32 = [c.sb(f"k32_{k}", [128, 640], F32, ses) for k in range(NCH)]
        v32 = [c.sb(f"v32_{t}", [128, D], F32, ses) for t in range(5)]
        kst = [c.sb(f"kst{k}", [128, 128], F32, ses) for k in range(NCH)]
        vst = c.sb("vst", [128, D], F32, ses)
        sl32 = c.sb("sl32", [128, 1024], F32, ses)
        sl_h = c.sb("sl_h", [128, 1024], BF16, ses)
        sl_l = c.sb("sl_l", [128, 1024], BF16, ses)
        Et = [c.sb(f"Et{i}", [128, 512], F32, ses) for i in range(3)]
        r2 = c.sb("r2", [2, 512], F32, ses)
        av32 = c.sb("av32", [64, 512], F32, ses)
        atmp = c.sb("atmp", [64, 1024], BF16, ses)
        ecol = c.sb("ecol", [128, 2], F32, ses)
        sel0 = c.sb("sel0", [2, 128], F32, ses)
        sel1 = c.sb("sel1", [2, 128], F32, ses)
        xn32 = c.sb("xn32", [128, 512], F32, ses)
        defr = c.sb("defr", [128, NCH * 128], F32, ses)
        @c.sec("sync")
        def _(e):
            c.inc(e.dma_start(out=sel0[:], in_=hb["sel01"][0]), "dL")
            c.inc(e.dma_start(out=sel1[:], in_=hb["sel01"][1]), "dL")

        L = dict(locals())
        for g in range(NG):
            self._s1_group(g, L)

    def _s1_group(self, g, L):
        c = self
        hb, resid = self.hb, self.resid
        G0 = g * GQ
        ab, abs_ = L["ab"], L["abs_"]
        xng_h, xng_l, xns_h, xns_l = L["xng_h"], L["xng_l"], L["xns_h"], L["xns_l"]
        q32, k32, v32, kst, vst = L["q32"], L["k32"], L["v32"], L["kst"], L["vst"]
        sl32, sl_h, sl_l = L["sl32"], L["sl_h"], L["sl_l"]
        Et, r2, av32, atmp = L["Et"], L["r2"], L["av32"], L["atmp"]
        atth, attl = L["atth"], L["attl"]
        ecol, xn32, defr = L["ecol"], L["xn32"], L["defr"]
        mtri, mband, mstr = L["mtri"], L["mband"], L["mstr"]
        xpv = L["xpv"]
        arow_g, brow_g, abrows = L["arow_g"], L["brow_g"], L["abrows"]
        pb = c.pb

        hstat0 = TPC if g == 0 else G0 - 128

        @c.sec("sync")
        def _(e):
            c.wcomp(e)
            c.inc(e.dma_start(out=mband[:], in_=hb["m_band"][:, G0:G0 + 512]), "dL")
            c.inc(e.dma_start(out=mstr[:], in_=hb["m_str"][:, G0:G0 + 512]), "dL")
            if g == 0:
                c.wait(e, "dG")
                c.inc(e.dma_start(out=arow_g[:, 0:128], in_=abrows[0:1, TPC:TPC + 128]), "dL")
                c.inc(e.dma_start(out=arow_g[:, 128:640], in_=abrows[0:1, 0:512]), "dL")
                c.inc(e.dma_start(out=arow_g[:, 640:768], in_=abrows[0:1, TPC + 128:TPC + 256]), "dL")
                c.inc(e.dma_start(out=brow_g[:, 0:128], in_=abrows[1:2, TPC:TPC + 128]), "dL")
                c.inc(e.dma_start(out=brow_g[:, 128:640], in_=abrows[1:2, 0:512]), "dL")
                c.inc(e.dma_start(out=brow_g[:, 640:768], in_=abrows[1:2, TPC + 128:TPC + 256]), "dL")
            else:
                c.inc(e.dma_start(out=arow_g[:, 0:640], in_=abrows[0:1, G0 - 128:G0 + 512]), "dL")
                c.inc(e.dma_start(out=brow_g[:, 0:640], in_=abrows[1:2, G0 - 128:G0 + 512]), "dL")

        @c.sec("vector")
        def _(v):
            c.wait(v, "dL")
            c.inc(v.engine_nop(), "V")

        c.rep(arow_g, 0, ab, 0, 128)
        c.rep(arow_g, 128, ab, 128, 512)
        c.rep(brow_g, 0, ab, 640, 128)
        c.rep(brow_g, 128, ab, 768, 512)
        if g == 0:
            c.rep(arow_g, 640, abs_, 0, 128)
            c.rep(brow_g, 640, abs_, 128, 128)

        for k in range(NCH):
            @c.sec("vector")
            def _(v, k=k):
                c.wait(v, "T")
                c.wait(v, "dG")
                hsrc = xpv[k][:, 0:128] if g == 0 else resid[k][:, G0 - 128:G0]
                v.tensor_tensor(xn32[:, 0:128], hsrc, ab[:, 0:128], ALU.mult)
                v.tensor_tensor(xn32[:, 0:128], xn32[:, 0:128], ab[:, 640:768], ALU.add)
                c.split(v, xn32[:, 0:128], xng_h[k][:, 0:128], xng_l[k][:, 0:128])
                for half in range(2):
                    j0, j1 = half * 256, (half + 1) * 256
                    v.tensor_tensor(xn32[:, 0:256], resid[k][:, G0 + j0:G0 + j1], ab[:, 128 + j0:128 + j1], ALU.mult)
                    v.tensor_tensor(xn32[:, 0:256], xn32[:, 0:256], ab[:, 768 + j0:768 + j1], ALU.add)
                    c.split(v, xn32[:, 0:256], xng_h[k][:, 128 + j0:128 + j1], xng_l[k][:, 128 + j0:128 + j1])
                if g == 0:
                    v.tensor_tensor(xn32[:, 0:128], xpv[k][:, 128:256], abs_[:, 0:128], ALU.mult)
                    v.tensor_tensor(xn32[:, 0:128], xn32[:, 0:128], abs_[:, 128:256], ALU.add)
                    c.split(v, xn32[:, 0:128], xns_h[k][:], xns_l[k][:])
                if g > 0:
                    v.tensor_tensor(resid[k][:, G0 - 128:G0], resid[k][:, G0 - 128:G0],
                                    defr[:, k * 128:(k + 1) * 128], ALU.add)
                c.inc(v.engine_nop(), "V")

        def load_slab(colskip, m, width=128, p=128):
            @c.sec("sync")
            def _(e, colskip=colskip, m=m, width=width, p=p):
                c.wcomp(e)
                src = hb["wqkvT"][:, colskip + m * width:colskip + (m + 1) * width].rearrange(
                    "(s p) m -> p s m", p=p)
                dst = sl32[:, 0:(1024 // p) * width].rearrange("p (s m) -> p s m", m=width)
                c.inc(e.dma_start(out=dst, in_=src), "dL")

            @c.sec("vector")
            def _(v, width=width, p=p):
                c.wait(v, "dL")
                w = (1024 // p) * width
                c.split(v, sl32[:, 0:w], sl_h[:, 0:w], sl_l[:, 0:w])
                c.inc(v.engine_nop(), "V")

        def mm3(t, psum_ap, wh, wl, rh, rl, start, stop, inc=False):
            t.matmul(psum_ap, wh, rh, start=start, stop=False)
            t.matmul(psum_ap, wh, rl, start=False, stop=False)
            last = t.matmul(psum_ap, wl, rh, start=False, stop=stop)
            if inc:
                c.inc(last, "T")

        for m in range(NCH):
            load_slab(0, m)

            @c.sec("tensor")
            def _(t, m=m):
                c.wait(t, "V")
                for kk in range(NCH):
                    mm3(t, pb[0][:, 0:512], sl_h[:, kk * 128:(kk + 1) * 128], sl_l[:, kk * 128:(kk + 1) * 128],
                        xng_h[kk][:, 128:640], xng_l[kk][:, 128:640], kk == 0, kk == NCH - 1, inc=(kk == NCH - 1))

            @c.sec("vector")
            def _(v, m=m):
                c.wait(v, "T")
                c.inc(v.tensor_copy(q32[m][:], pb[0][:, 0:512]), "V")

        for m in range(NCH):
            load_slab(D, m)

            @c.sec("tensor")
            def _(t, m=m):
                c.wait(t, "V")
                for kk in range(NCH):
                    mm3(t, pb[0][:, 0:512], sl_h[:, kk * 128:(kk + 1) * 128], sl_l[:, kk * 128:(kk + 1) * 128],
                        xng_h[kk][:, 0:512], xng_l[kk][:, 0:512], kk == 0, kk == NCH - 1)
                for kk in range(NCH):
                    mm3(t, pb[1][:, 0:128], sl_h[:, kk * 128:(kk + 1) * 128], sl_l[:, kk * 128:(kk + 1) * 128],
                        xng_h[kk][:, 512:640], xng_l[kk][:, 512:640], kk == 0, kk == NCH - 1,
                        inc=(g != 0 and kk == NCH - 1))
                if g == 0:
                    for kk in range(NCH):
                        mm3(t, pb[2][:, 0:128], sl_h[:, kk * 128:(kk + 1) * 128], sl_l[:, kk * 128:(kk + 1) * 128],
                            xns_h[kk][:], xns_l[kk][:], kk == 0, kk == NCH - 1, inc=(kk == NCH - 1))

            @c.sec("vector")
            def _(v, m=m):
                c.wait(v, "T")
                v.tensor_copy(k32[m][:, 0:512], pb[0][:, 0:512])
                v.tensor_copy(k32[m][:, 512:640], pb[1][:, 0:128])
                if g == 0:
                    v.tensor_copy(kst[m][:], pb[2][:, 0:128])
                c.inc(v.engine_nop(), "V")

        ntt = 6 if g == 0 else 5
        for half in range(2):
            for kk in range(NCH):
                @c.sec("sync")
                def _(e, half=half, kk=kk):
                    c.wcomp(e)
                    src = hb["wqkvT"][kk * 128:(kk + 1) * 128, 2 * D + half * 512:2 * D + (half + 1) * 512]
                    c.inc(e.dma_start(out=sl32[:, 0:512], in_=src), "dL")

                @c.sec("vector")
                def _(v):
                    c.wait(v, "dL")
                    c.split(v, sl32[:, 0:512], sl_h[:, 0:512], sl_l[:, 0:512])
                    c.inc(v.engine_nop(), "V")

                @c.sec("tensor")
                def _(t, half=half, kk=kk):
                    c.wait(t, "V")
                    for tt in range(ntt):
                        if tt < 5:
                            ah = xng_h[kk][:, tt * 128:(tt + 1) * 128]
                            al = xng_l[kk][:, tt * 128:(tt + 1) * 128]
                        else:
                            ah, al = xns_h[kk][:], xns_l[kk][:]
                        t.matmul(pb[tt][:, 0:512], ah, sl_h[:, 0:512], start=(kk == 0), stop=False)
                        t.matmul(pb[tt][:, 0:512], al, sl_h[:, 0:512], start=False, stop=False)
                        last = t.matmul(pb[tt][:, 0:512], ah, sl_l[:, 0:512], start=False, stop=(kk == NCH - 1))
                    c.inc(last, "T")

            @c.sec("vector")
            def _(v, half=half):
                c.wait(v, "T")
                for tt in range(ntt):
                    dst = v32[tt] if tt < 5 else vst
                    last = v.tensor_copy(dst[:, half * 512:(half + 1) * 512], pb[tt][:, 0:512])
                c.inc(last, "V")

        for p in range(8):
            self._s1_pair(g, p, L)

        if DEBUG and g == 0:
            @c.sec("gpsimd")
            def _(gp):
                c.wcomp(gp)
                c.wait(gp, "dG")
                for k in range(NCH):
                    c.inc(gp.dma_start(out=c.dbg["dbg_q"][k * 128:(k + 1) * 128, :], in_=q32[k][:]), "dO")
                    c.inc(gp.dma_start(out=c.dbg["dbg_k"][k * 128:(k + 1) * 128, :], in_=k32[k][:]), "dO")
                    c.inc(gp.dma_start(out=c.dbg["dbg_kst"][k * 128:(k + 1) * 128, :], in_=kst[k][:]), "dO")
                for tt in range(5):
                    c.inc(gp.dma_start(out=c.dbg["dbg_v"][tt * 128:(tt + 1) * 128, :], in_=v32[tt][:]), "dO")
                c.inc(gp.dma_start(out=c.dbg["dbg_vst"][:], in_=vst[:]), "dO")
                c.inc(gp.dma_start(out=c.dbg["dbg_ab"][:], in_=ab[:]), "dO")
                c.inc(gp.dma_start(out=c.dbg["dbg_att"][:], in_=atth[:].bitcast(F32) if False else atth[:]), "dO")
                c.inc(gp.dma_start(out=c.dbg["dbg_attl"][:], in_=attl[:]), "dO")

        for m in range(NCH):
            @c.sec("sync")
            def _(e, m=m):
                c.wcomp(e)
                src = hb["woutT"][:, m * 128:(m + 1) * 128].rearrange("(s p) m -> p s m", p=128)
                c.inc(e.dma_start(out=sl32[:].rearrange("p (s m) -> p s m", m=128), in_=src), "dL")

            @c.sec("vector")
            def _(v):
                c.wait(v, "dL")
                c.split(v, sl32[:], sl_h[:], sl_l[:])
                c.inc(v.engine_nop(), "V")

            @c.sec("tensor")
            def _(t, m=m):
                c.wait(t, "V")
                c.wait(t, "dG")
                for kk in range(NCH):
                    mm3(t, pb[0][:, 0:512], sl_h[:, kk * 128:(kk + 1) * 128], sl_l[:, kk * 128:(kk + 1) * 128],
                        atth[:, kk * 512:(kk + 1) * 512], attl[:, kk * 512:(kk + 1) * 512],
                        kk == 0, kk == NCH - 1, inc=(kk == NCH - 1))

            @c.sec("vector")
            def _(v, m=m):
                c.wait(v, "T")
                if g < NG - 1:
                    v.tensor_tensor(resid[m][:, G0:G0 + 384], resid[m][:, G0:G0 + 384],
                                    pb[0][:, 0:384], ALU.add)
                    c.inc(v.tensor_copy(defr[:, m * 128:(m + 1) * 128], pb[0][:, 384:512]), "V")
                else:
                    c.inc(v.tensor_tensor(resid[m][:, G0:G0 + 512], resid[m][:, G0:G0 + 512],
                                          pb[0][:, 0:512], ALU.add), "V")

    def _s1_pair(self, g, p, L):
        c = self
        G0 = g * GQ
        q32, k32, v32, kst, vst = L["q32"], L["k32"], L["v32"], L["kst"], L["vst"]
        Et, r2, av32, atmp = L["Et"], L["r2"], L["av32"], L["atmp"]
        atth, attl = L["atth"], L["attl"]
        ecol, sel0, sel1 = L["ecol"], L["sel0"], L["sel1"]
        mtri, mband, mstr = L["mtri"], L["mband"], L["mstr"]
        pb = c.pb
        SC0, SC1, SC2, SUM, AV0, AV1, RR, XX = pb

        for hi_ in range(2):
            h = 2 * p + hi_
            base = 64 * hi_
            kch, qch = k32[p], q32[p]

            @c.sec("tensor")
            def _(t, hi_=hi_, base=base, kch=kch, qch=qch):
                c.wait(t, "V")
                for j in range(4):
                    t.matmul(SC0[:, j * 128:(j + 1) * 128],
                             kch[base:base + 64, 128 + j * 128:256 + j * 128],
                             qch[base:base + 64, j * 128:(j + 1) * 128], start=True, stop=True)
                for j in range(4):
                    t.matmul(SC1[:, j * 128:(j + 1) * 128],
                             kch[base:base + 64, j * 128:(j + 1) * 128],
                             qch[base:base + 64, j * 128:(j + 1) * 128], start=True, stop=True)
                c.inc(t.matmul(SC2[:, 0:512], kst[p][base:base + 64, :], qch[base:base + 64, :],
                               start=True, stop=True), "T")

            @c.sec("vector")
            def _(v):
                c.wait(v, "T")
                v.scalar_tensor_tensor(Et[0][:], SC0[:, 0:512], 0.125, mtri[:], ALU.mult, ALU.add)
                v.scalar_tensor_tensor(Et[1][:], SC1[:, 0:512], 0.125, mband[:], ALU.mult, ALU.add)
                c.inc(v.scalar_tensor_tensor(Et[2][:], SC2[:, 0:512], 0.125, mstr[:],
                                             ALU.mult, ALU.add), "V")

            @c.sec("scalar")
            def _(s):
                c.wait(s, "V")
                s.activation(Et[0][:], Et[0][:], AF.Exp)
                s.activation(Et[1][:], Et[1][:], AF.Exp)
                c.inc(s.activation(Et[2][:], Et[2][:], AF.Exp), "SC")

            @c.sec("vector")
            def _(v, hi_=hi_):
                c.wait(v, "SC")
                c.wait(v, "T")
                v.memset(ecol[:, hi_:hi_ + 1], 1.0)
                c.inc(v.memset(ecol[:, 1 - hi_:2 - hi_], 0.0), "V")

            @c.sec("tensor")
            def _(t, hi_=hi_):
                c.wait(t, "V")
                st = (hi_ == 0)
                t.matmul(SUM[0:2, 0:512], ecol[:], Et[0][:], start=st, stop=False)
                t.matmul(SUM[0:2, 0:512], ecol[:], Et[1][:], start=False, stop=False)
                c.inc(t.matmul(SUM[0:2, 0:512], ecol[:], Et[2][:], start=False, stop=(hi_ == 1)), "T")

            AVb = (AV0, AV1)[hi_]

            @c.sec("tensor")
            def _(t, h=h, AVb=AVb):
                for j in range(4):
                    jc = slice(j * 128, (j + 1) * 128)
                    t.matmul(AVb[0:64, jc], v32[1 + j][:, h * 64:(h + 1) * 64], Et[0][:, jc],
                             start=True, stop=False)
                    t.matmul(AVb[0:64, jc], v32[j][:, h * 64:(h + 1) * 64], Et[1][:, jc],
                             start=False, stop=False)
                    last = t.matmul(AVb[0:64, jc], vst[:, h * 64:(h + 1) * 64], Et[2][:, jc],
                                    start=False, stop=True)
                c.inc(last, "T")

        @c.sec("scalar")
        def _(s):
            c.wait(s, "T")
            s.activation(r2[0:2, :], SUM[0:2, 0:512], AF.Ln)
            c.inc(s.activation(r2[0:2, :], r2[0:2, :], AF.Exp, scale=-1.0), "SC")

        for hi_ in range(2):
            h = 2 * p + hi_
            AVb = (AV0, AV1)[hi_]

            @c.sec("tensor")
            def _(t, hi_=hi_):
                c.wait(t, "SC")
                c.wait(t, "V")
                sel = sel0 if hi_ == 0 else sel1
                c.inc(t.matmul(RR[0:64, 0:512], sel[:, 0:64], r2[0:2, :], start=True, stop=True), "T")

            @c.sec("vector")
            def _(v):
                c.wait(v, "T")
                c.wait(v, "dG")
                c.inc(v.tensor_copy(av32[:], RR[0:64, 0:512]), "V")

            @c.sec("vector")
            def _(v, hi_=hi_, h=h, p=p, AVb=AVb):
                c.wait(v, "V")
                v.tensor_tensor(av32[:], av32[:], AVb[0:64, 0:512], ALU.mult)
                if hi_ == 0:
                    c.split(v, av32[:], atth[0:64, p * 512:(p + 1) * 512], attl[0:64, p * 512:(p + 1) * 512])
                    c.inc(v.engine_nop(), "V")
                else:
                    c.split(v, av32[:], atmp[:, 0:512], atmp[:, 512:1024])
                    c.inc(v.engine_nop(), "V")

            if hi_ == 1:
                @c.sec("gpsimd")
                def _(gp, p=p):
                    c.wait(gp, "V")
                    c.inc(gp.dma_start(out=atth[64:128, p * 512:(p + 1) * 512], in_=atmp[:, 0:512]), "dG")
                    c.inc(gp.dma_start(out=attl[64:128, p * 512:(p + 1) * 512], in_=atmp[:, 512:1024]), "dG")


    def s2_l0ffn(self, ses):
        c = self
        hb, resid = self.hb, self.resid
        core = None

        abrows = self.nc.dram_tensor("abrows2", [2, TPC], F32).ap()
        lb_in = self.nc.dram_tensor("lb_in", [8], F32)
        lb_out = self.nc.dram_tensor("lb_out", [8], F32)

        def srcfn(k, j0, w):
            return resid[k][:, j0:j0 + w]

        nsl = [(i * 512, 512) for i in range(4)]
        c.ln_stats(TPC, srcfn, nsl, abrows)

        xn32d = self.nc.dram_tensor("xn32d", [D, TPC], F32).ap()
        xn32s = c.sb("f_xn32s", [128, 512], F32, ses)
        xnh = [c.sb(f"f_xnh{k}", [128, TPC], BF16, ses) for k in range(NCH)]
        xnl = [c.sb(f"f_xnl{k}", [128, TPC], BF16, ses) for k in range(NCH)]
        abg = c.sb("f_abg", [128, 1024], F32, ses)
        abrow = c.sb("f_abrow", [1, 1024], F32, ses)
        sl32 = c.sb("f_sl32", [128, 1024], F32, ses)
        sl_h = c.sb("f_slh", [128, 1024], BF16, ses)
        sl_l = c.sb("f_sll", [128, 1024], BF16, ses)
        hh = [c.sb(f"f_hh{i}", [128, TPC], BF16, ses) for i in range(2)]
        hl = [c.sb(f"f_hl{i}", [128, TPC], BF16, ses) for i in range(2)]
        h32 = c.sb("f_h32", [128, 512], F32, ses)
        d32 = c.sb("f_d32", [128, 512], F32, ses)
        dump = c.sb("f_dump", [128, 512], F32, ses)
        lossb = c.sb("f_lossb", [128, 256], F32, ses)
        lossr = c.sb("f_lossr", [128, 8], F32, ses)
        loss8 = c.sb("f_loss8", [1, 8], F32, ses)
        lmin = c.sb("f_lmin", [1, 1], F32, ses)
        g8 = c.sb("f_g8", [1, 8], F32, ses)
        GG = c.sb("f_GG", [128, 8], F32, ses)
        G2 = c.sb("f_G2", [128, 4], F32, ses)
        onesc_f = c.sb("f_onescf", [128, 1], F32, ses)
        bvec = c.sb("f_bvec", [128, 8], F32, ses)
        bu_t = c.sb("f_bu", [128, 64], F32, ses)
        bb_t = c.sb("f_bb", [128, 8], F32, ses)
        bd1_t = c.sb("f_bd1", [128, 16], F32, ses)
        bd2_t = c.sb("f_bd2", [128, 64], F32, ses)
        badn_t = c.sb("f_badn", [128, 4], F32, ses)
        ah = [c.sb(f"f_ah{i}", [128, TPC], BF16, ses) for i in range(4)]
        al = [c.sb(f"f_al{i}", [128, TPC], BF16, ses) for i in range(4)]
        pb = c.pb

        @c.sec("sync")
        def _(e):
            c.wcomp(e)
            c.inc(e.dma_start(out=bb_t[:], in_=hb["bbase"].rearrange("(c p) -> p c", p=128), allow_slow_non_contiguous=True), "dL")
            c.inc(e.dma_start(out=bd1_t[:], in_=hb["bd1"].rearrange("(c p) -> p c", p=128), allow_slow_non_contiguous=True), "dL")
            c.inc(e.dma_start(out=bd2_t[:], in_=hb["bd2"].rearrange("(c p) -> p c", p=128), allow_slow_non_contiguous=True), "dL")
            c.inc(e.dma_start(out=badn_t[:], in_=hb["badn"].rearrange("(c p) -> p c", p=128), allow_slow_non_contiguous=True), "dL")
            c.inc(e.dma_start(out=bu_t[:], in_=hb["bup"]), "dL")

        @c.sec("vector")
        def _(v):
            c.wait(v, "dL")
            c.inc(v.memset(onesc_f[:], 1.0), "V")

        for n in range(4):
            j0 = n * 512

            @c.sec("sync")
            def _(e, j0=j0):
                c.wcomp(e)
                c.wait(e, "dG")
                c.inc(e.dma_start(out=abrow[:, 0:512], in_=abrows[0:1, j0:j0 + 512]), "dL")
                c.inc(e.dma_start(out=abrow[:, 512:1024], in_=abrows[1:2, j0:j0 + 512]), "dL")

            @c.sec("vector")
            def _(v):
                c.wait(v, "dL")
                c.inc(v.engine_nop(), "V")

            c.rep(abrow, 0, abg, 0, 512)
            c.rep(abrow, 512, abg, 512, 512)

            for k in range(NCH):
                @c.sec("vector")
                def _(v, k=k, j0=j0):
                    c.wait(v, "T")
                    c.wait(v, "dG")
                    sl = slice(j0, j0 + 512)
                    v.tensor_tensor(xn32s[:], resid[k][:, sl], abg[:, 0:512], ALU.mult)
                    v.tensor_tensor(xn32s[:], xn32s[:], abg[:, 512:1024], ALU.add)
                    c.split(v, xn32s[:], xnh[k][:, sl], xnl[k][:, sl])
                    c.inc(v.engine_nop(), "V")

                @c.sec("gpsimd")
                def _(g, k=k, j0=j0):
                    c.wait(g, "V")
                    c.inc(g.dma_start(out=xn32d[k * 128:(k + 1) * 128, j0:j0 + 512], in_=xn32s[:]), "dG")

        if S2CUT <= 1:
            return
        def mm3(t, psum_ap, wh, wl, rh, rl, start, stop, inc=False):
            t.matmul(psum_ap, wh, rh, start=start, stop=False)
            t.matmul(psum_ap, wh, rl, start=False, stop=False)
            last = t.matmul(psum_ap, wl, rh, start=False, stop=stop)
            if inc:
                c.inc(last, "T")

        def load_slab(hname, c0, m, width=128, p=128, rows=D, r0=0):
            segs = rows // p

            @c.sec("sync")
            def _(e):
                c.wcomp(e)
                src2 = hb[hname][r0:r0 + rows, c0 + m * width:c0 + (m + 1) * width].rearrange(
                    "(s p) m -> p s m", p=p)
                dst = sl32[:, 0:segs * width].rearrange("p (s m) -> p s m", m=width)
                c.inc(e.dma_start(out=dst, in_=src2), "dL")

            @c.sec("vector")
            def _(v):
                c.wait(v, "dL")
                w = segs * width
                c.split(v, sl32[:, 0:w], sl_h[:, 0:w], sl_l[:, 0:w])
                c.inc(v.engine_nop(), "V")

        for m in range(NCH):
            load_slab("wbase", 0, m)
            for n in range(4):
                @c.sec("tensor")
                def _(t, m=m, n=n):
                    c.wait(t, "V")
                    for kk in range(NCH):
                        mm3(t, pb[0][:, 0:512], sl_h[:, kk * 128:(kk + 1) * 128],
                            sl_l[:, kk * 128:(kk + 1) * 128],
                            xnh[kk][:, n * 512:(n + 1) * 512], xnl[kk][:, n * 512:(n + 1) * 512],
                            kk == 0, kk == NCH - 1, inc=(kk == NCH - 1))

                @c.sec("vector")
                def _(v, m=m, n=n):
                    c.wait(v, "T")
                    sl = slice(n * 512, (n + 1) * 512)
                    c.inc(v.scalar_tensor_tensor(resid[m][:, sl], pb[0][:, 0:512], bb_t[:, m:m + 1],
                                                 resid[m][:, sl], ALU.add, ALU.add), "V")

        if S2CUT <= 2:
            return
        for e_ in range(8):
            for mc in range(2):
                load_slab("wd1", e_ * 256, mc)
                for n in range(4):
                    @c.sec("tensor")
                    def _(t, n=n):
                        c.wait(t, "V")
                        for kk in range(NCH):
                            mm3(t, pb[0][:, 0:512], sl_h[:, kk * 128:(kk + 1) * 128],
                                sl_l[:, kk * 128:(kk + 1) * 128],
                                xnh[kk][:, n * 512:(n + 1) * 512], xnl[kk][:, n * 512:(n + 1) * 512],
                                kk == 0, kk == NCH - 1, inc=(kk == NCH - 1))

                    @c.sec("scalar")
                    def _(s, e_=e_, mc=mc, n=n):
                        c.wait(s, "T")
                        c.inc(s.activation(h32[:], pb[0][:, 0:512], AF.Relu,
                                           bias=bd1_t[:, e_ * 2 + mc:e_ * 2 + mc + 1]), "SC")

                    @c.sec("vector")
                    def _(v, mc=mc, n=n):
                        c.wait(v, "SC")
                        sl = slice(n * 512, (n + 1) * 512)
                        c.split(v, h32[:], hh[mc][:, sl], hl[mc][:, sl])
                        c.inc(v.engine_nop(), "V")

            for m in range(NCH):
                load_slab("wd2", 0, m, rows=256, r0=e_ * 256)
                for n in range(4):
                    @c.sec("sync")
                    def _(e2, m=m, n=n):
                        c.wcomp(e2)
                        c.wait(e2, "dG")
                        c.inc(e2.dma_start(out=xn32s[:], in_=xn32d[m * 128:(m + 1) * 128, n * 512:(n + 1) * 512]), "dL")

                    @c.sec("tensor")
                    def _(t, m=m, n=n):
                        c.wait(t, "V")
                        for kk in range(2):
                            mm3(t, pb[0][:, 0:512], sl_h[:, kk * 128:(kk + 1) * 128],
                                sl_l[:, kk * 128:(kk + 1) * 128],
                                hh[kk][:, n * 512:(n + 1) * 512], hl[kk][:, n * 512:(n + 1) * 512],
                                kk == 0, kk == 1, inc=(kk == 1))

                    @c.sec("vector")
                    def _(v, m=m, n=n, e_=e_):
                        c.wait(v, "T")
                        c.wait(v, "dL")
                        c.inc(v.scalar_tensor_tensor(d32[:], pb[0][:, 0:512], bd2_t[:, e_ * 8 + m:e_ * 8 + m + 1],
                                                     xn32s[:], ALU.add, ALU.subtract), "V")

                    @c.sec("scalar")
                    def _(s, e_=e_, m=m, n=n):
                        c.wait(s, "V")
                        c.inc(s.activation(dump[:], d32[:], AF.Square,
                                           accum_out=lossb[:, e_ * 32 + m * 4 + n:e_ * 32 + m * 4 + n + 1]), "SC")

        if S2CUT <= 3:
            return
        @c.sec("vector")
        def _(v):
            c.wait(v, "SC")
            for e_ in range(8):
                v.reduce_sum(lossr[:, e_:e_ + 1], lossb[:, e_ * 32:(e_ + 1) * 32], AX)
            c.inc(v.engine_nop(), "V")

        @c.sec("tensor")
        def _(t):
            c.wait(t, "V")
            c.inc(t.matmul(pb[2][0:1, 0:8], onesc_f[:], lossr[:], start=True, stop=True), "T")

        @c.sec("vector")
        def _(v):
            c.wait(v, "T")
            c.inc(v.tensor_copy(loss8[:], pb[2][0:1, 0:8]), "V")

        @c.sec("gpsimd")
        def _(g):
            c.wait(g, "V")
            c.inc(g.dma_start(out=lb_in[:].unsqueeze(0), in_=loss8[0:1, :]), "dG")
            c.wait(g, "dG")
            grp = [[0, 1, 2, 3], [4, 5, 6, 7]]
            c.inc(g.collective_compute("AllReduce", ALU.add, replica_groups=grp,
                                       ins=[lb_in.ap().opt()], outs=[lb_out.ap().opt()]), "CC")
            c.wait(g, "CC")
            c.inc(g.dma_start(out=loss8[0:1, :], in_=lb_out[:].unsqueeze(0)), "dG")

        @c.sec("vector")
        def _(v):
            c.wait(v, "dG")
            c.inc(v.tensor_reduce(lmin[:], loss8[:], AX, ALU.min), "V")

        @c.sec("vector")
        def _(v):
            c.wait(v, "V")
            c.inc(v.tensor_scalar(g8[:], loss8[:], lmin[0:1, 0:1], None, ALU.is_equal), "V")

        @c.sec("tensor")
        def _(t):
            c.wait(t, "V")
            c.inc(t.matmul(pb[1][:, 0:8], c.onesr[:], g8[:], start=True, stop=True), "T")

        @c.sec("vector")
        def _(v):
            c.wait(v, "T")
            c.inc(v.tensor_copy(GG[:], pb[1][:, 0:8]), "V")

        @c.sec("vector")
        def _(v):
            c.wait(v, "V")
            for cc in range(4):
                v.tensor_copy(G2[0:64, cc:cc + 1], GG[0:64, 2 * cc:2 * cc + 1])
                v.tensor_copy(G2[64:128, cc:cc + 1], GG[64:128, 2 * cc + 1:2 * cc + 2])
            c.inc(v.engine_nop(), "V")

        for m in range(NCH):
            @c.sec("vector")
            def _(v, m=m):
                c.wait(v, "V")
                c.inc(v.tensor_tensor(lossr[:], bu_t[:, m * 8:(m + 1) * 8], GG[:], ALU.mult), "V")

            @c.sec("vector")
            def _(v, m=m):
                c.wait(v, "V")
                c.inc(v.reduce_sum(bvec[:, m:m + 1], lossr[:], AX), "V")

        if DEBUG:
            @c.sec("gpsimd")
            def _(g):
                c.wcomp(g)
                c.inc(g.dma_start(out=c.dbg["dbg_loss"], in_=loss8[:]), "dO")
                c.inc(g.dma_start(out=c.dbg["dbg_g8"], in_=g8[:]), "dO")
                c.inc(g.dma_start(out=c.dbg["dbg_GG"], in_=GG[:]), "dO")
                c.inc(g.dma_start(out=c.dbg["dbg_G2"], in_=G2[:]), "dO")
                c.inc(g.dma_start(out=c.dbg["dbg_bvec"], in_=bvec[:]), "dO")
                c.inc(g.dma_start(out=c.dbg["dbg_lossr"], in_=lossr[:]), "dO")

        if S2CUT <= 4:
            return
        for m in range(4):
            load_slab("adn", 0, m)
            for n in range(4):
                @c.sec("tensor")
                def _(t, m=m, n=n):
                    c.wait(t, "V")
                    for kk in range(NCH):
                        mm3(t, pb[0][:, 0:512], sl_h[:, kk * 128:(kk + 1) * 128],
                            sl_l[:, kk * 128:(kk + 1) * 128],
                            xnh[kk][:, n * 512:(n + 1) * 512], xnl[kk][:, n * 512:(n + 1) * 512],
                            kk == 0, kk == NCH - 1, inc=(kk == NCH - 1))

                @c.sec("scalar")
                def _(s, m=m):
                    c.wait(s, "T")
                    c.inc(s.activation(h32[:], pb[0][:, 0:512], AF.Relu, bias=badn_t[:, m:m + 1]), "SC")

                @c.sec("vector")
                def _(v, m=m, n=n):
                    c.wait(v, "SC")
                    v.tensor_scalar_mul(h32[:], h32[:], G2[:, m:m + 1])
                    sl = slice(n * 512, (n + 1) * 512)
                    c.split(v, h32[:], ah[m][:, sl], al[m][:, sl])
                    c.inc(v.engine_nop(), "V")

        if DEBUG:
            @c.sec("gpsimd")
            def _(g):
                c.wcomp(g)
                c.inc(g.dma_start(out=c.dbg["dbg_a0"], in_=ah[0][:, 0:512]), "dO")
                c.inc(g.dma_start(out=c.dbg["dbg_h0"], in_=ah[1][:, 0:512]), "dO")
                c.inc(g.dma_start(out=c.dbg["dbg_E"][:, 0:512], in_=ah[2][:, 0:512]), "dO")
                c.inc(g.dma_start(out=c.dbg["dbg_E"][:, 512:1024], in_=ah[3][:, 0:512]), "dO")
                c.inc(g.dma_start(out=c.dbg["dbg_lossr"][:, 0:8].broadcast_to([128, 8]) if False else c.dbg["dbg_E"][:, 1024:1536], in_=al[0][:, 0:512]), "dO")

        for m in range(NCH):
            load_slab("aup", 0, m, rows=512)
            for n in range(4):
                @c.sec("tensor")
                def _(t, m=m, n=n):
                    c.wait(t, "V")
                    for kk in range(4):
                        mm3(t, pb[0][:, 0:512], sl_h[:, kk * 128:(kk + 1) * 128],
                            sl_l[:, kk * 128:(kk + 1) * 128],
                            ah[kk][:, n * 512:(n + 1) * 512], al[kk][:, n * 512:(n + 1) * 512],
                            kk == 0, kk == 3, inc=(kk == 3))

                @c.sec("vector")
                def _(v, m=m, n=n):
                    c.wait(v, "T")
                    sl = slice(n * 512, (n + 1) * 512)
                    c.inc(v.scalar_tensor_tensor(resid[m][:, sl], pb[0][:, 0:512], bvec[:, m:m + 1],
                                                 resid[m][:, sl], ALU.add, ALU.add), "V")


    def rms_stats(self, srcfn, dram_rows, eps):
        c = self
        ps_sq = c.pb[1]
        with ExitStack() as les:
            sq32 = c.sb("r_sq32", [128, 512], F32, les)
            sqh = c.sb("r_sqh", [128, 512], BF16, les)
            sql = c.sb("r_sql", [128, 512], BF16, les)
            arow = c.sb("r_arow", [1, TPC], F32, les)
            epsv = c.sb("r_eps", [1, 1], F32, les)

            @c.sec("vector")
            def _(v):
                c.wcomp(v)
                c.inc(v.memset(epsv[:], eps), "V")

            for n in range(4):
                j0, w = n * 512, 512
                for k in range(NCH):
                    @c.sec("scalar")
                    def _(s, k=k, j0=j0, w=w):
                        c.wait(s, "V")
                        c.wait(s, "T")
                        c.inc(s.activation(sq32[:, 0:w], srcfn(k, j0, w), AF.Square), "SC")

                    @c.sec("vector")
                    def _(v, w=w):
                        c.wait(v, "SC")
                        c.split(v, sq32[:, 0:w], sqh[:, 0:w], sql[:, 0:w])
                        c.inc(v.engine_nop(), "V")

                    @c.sec("tensor")
                    def _(t, k=k, w=w):
                        c.wait(t, "V")
                        t.matmul(ps_sq[0:1, 0:w], c.onesc_b[:], sqh[:, 0:w], start=(k == 0), stop=False)
                        c.inc(t.matmul(ps_sq[0:1, 0:w], c.onesc_b[:], sql[:, 0:w],
                                       start=False, stop=(k == NCH - 1)), "T")

                @c.sec("vector")
                def _(v, j0=j0, w=w):
                    c.wait(v, "T")
                    c.inc(v.tensor_scalar_mul(arow[:, j0:j0 + w], ps_sq[0:1, 0:w], 1.0 / D), "V")

            if DEBUG:
                @c.sec("gpsimd")
                def _(g):
                    c.wcomp(g)
                    c.inc(g.dma_start(out=c.dbg["dbg_h0"][1:2, 0:8], in_=arow[0:1, 0:8]), "dO")

                @c.sec("vector")
                def _(v):
                    c.wait(v, "dO")
                    c.inc(v.engine_nop(), "V")

            @c.sec("scalar")
            def _(s):
                c.wait(s, "V")
                c.inc(s.activation(arow[:], arow[:], AF.Ln, bias=epsv[0:1, 0:1]), "SC")

            @c.sec("scalar")
            def _(s):
                c.wait(s, "SC")
                c.inc(s.activation(arow[:], arow[:], AF.Exp, scale=-0.5), "SC")

            if DEBUG:
                @c.sec("gpsimd")
                def _(g):
                    c.wcomp(g)
                    c.inc(g.dma_start(out=c.dbg["dbg_h0"][2:3, 0:8], in_=arow[0:1, 0:8]), "dO")

                @c.sec("vector")
                def _(v):
                    c.wait(v, "dO")
                    c.inc(v.engine_nop(), "V")

            @c.sec("gpsimd")
            def _(g):
                c.wait(g, "SC")
                c.inc(g.dma_start(out=dram_rows[0:1, 0:TPC], in_=arow[:]), "dG")

            if DEBUG:
                @c.sec("sync")
                def _(e):
                    c.wait(e, "dG")
                    c.inc(e.dma_start(out=sq32[0:1, 0:16], in_=dram_rows[0:1, 0:16]), "dL")

                @c.sec("gpsimd")
                def _(g):
                    c.wait(g, "dL")
                    c.inc(g.dma_start(out=c.dbg["dbg_h0"][3:4, 0:16], in_=sq32[0:1, 0:16]), "dO")

                @c.sec("vector")
                def _(v):
                    c.wait(v, "dO")
                    c.inc(v.engine_nop(), "V")

            @c.sec("vector")
            def _(v):
                c.wait(v, "dG")
                c.inc(v.engine_nop(), "V")

    def s3_l1attn(self, ses):
        c = self
        hb, resid = self.hb, self.resid
        nc = self.nc
        pb = c.pb

        arows1 = nc.dram_tensor("s3_arows1", [2, TPC], F32).ap()
        abrows3 = nc.dram_tensor("s3_abrows3", [2, TPC], F32).ap()
        x2_d = nc.dram_tensor("s3_x2d", [D, TPC], F32).ap()
        arows2 = nc.dram_tensor("s3_arows2", [2, TPC], F32).ap()
        cd_d = nc.dram_tensor("s3_cd", [D, TPC], F32).ap()
        cs_d = nc.dram_tensor("s3_cs", [D, TPC], F32).ap()
        rech_d = nc.dram_tensor("s3_rech", [D, TPC], BF16).ap()
        recl_d = nc.dram_tensor("s3_recl", [D, TPC], BF16).ap()
        self.dbg_rech, self.dbg_recl = rech_d, recl_d
        hg_in = nc.dram_tensor("s3_hgin", [3 * D + 128], F32)
        hg_out = nc.dram_tensor("s3_hgout", [8, 3 * D + 128], F32)
        ag_in = nc.dram_tensor("s3_agin", [2 * D], F32)
        ag_out = nc.dram_tensor("s3_agout", [8, 2 * D], F32)
        sel_d = nc.dram_tensor("s3_seld", [3, 2 * D], F32)

        pnw_t = c.sb("s3_pnw", [128, 8], F32, ses)
        fnw_t = c.sb("s3_fnw", [128, 8], F32, ses)
        convw_t = c.sb("s3_convw", [128, 32], F32, ses)
        convb_t = c.sb("s3_convb", [128, 8], F32, ses)
        dbias_t = c.sb("s3_dbias", [128, 8], F32, ses)
        prevhot = c.sb("s3_prevhot", [8, 1], F32, ses)
        predselT = c.sb("s3_predselT", [8, 3], F32, ses)
        xh3 = [c.sb(f"s3_xh3_{k}", [128, 3], F32, ses) for k in range(NCH)]
        Lsc = c.sb("s3_Lsc", [128, 8 * 32], F32, ses)
        Cpr = c.sb("s3_Cpr", [128, 8 * 32], F32, ses)
        Sloc = c.sb("s3_Sloc", [128, 8], F32, ses)
        Ptot = c.sb("s3_Ptot", [128, 8], F32, ses)
        seff0 = c.sb("s3_seff0", [128, 8], F32, ses)
        self.dbg_Lsc, self.dbg_Cpr, self.dbg_Sloc, self.dbg_Ptot, self.dbg_seff0 = Lsc, Cpr, Sloc, Ptot, seff0
        sl32 = c.sb("s3_sl32", [128, 1024], F32, ses)
        sl_h = c.sb("s3_slh", [128, 1024], BF16, ses)
        sl_l = c.sb("s3_sll", [128, 1024], BF16, ses)

        @c.sec("sync")
        def _(e):
            c.wcomp(e)
            for nm, dst in [("pnw", pnw_t), ("fnw", fnw_t), ("convb", convb_t), ("dbias", dbias_t)]:
                c.inc(e.dma_start(out=dst[:], in_=hb[nm].rearrange("(c p) -> p c", p=128),
                                  allow_slow_non_contiguous=True), "dL")
            c.inc(e.dma_start(out=convw_t[:].rearrange("p (c k) -> p c k", k=4),
                              in_=hb["convw"].rearrange("(c p) k -> p c k", p=128),
                              allow_slow_non_contiguous=True), "dL")
            c.inc(e.dma_start(out=prevhot[:], in_=hb["prevhot"]), "dL")
            c.inc(e.dma_start(out=predselT[:], in_=hb["predselT"]), "dL")

        @c.sec("vector")
        def _(v):
            c.wait(v, "dL")
            c.inc(v.engine_nop(), "V")

        base_relax = self.relaxed
        self.relaxed = base_relax and "a" in RELAX3 if RELAX3 else base_relax
        def srcfn(k, j0, w):
            return resid[k][:, j0:j0 + w]
        c.ln_stats(TPC, srcfn, [(i * 512, 512) for i in range(4)], abrows3)

        with ExitStack() as esl:
            abg3 = c.sb("s3_abg3", [128, 1024], F32, esl)
            abrow3 = c.sb("s3_abrow3", [1, 1024], F32, esl)
            for n in range(4):
                j0 = n * 512

                @c.sec("sync")
                def _(e, j0=j0):
                    c.wcomp(e)
                    c.wait(e, "dG")
                    c.inc(e.dma_start(out=abrow3[:, 0:512], in_=abrows3[0:1, j0:j0 + 512]), "dL")
                    c.inc(e.dma_start(out=abrow3[:, 512:1024], in_=abrows3[1:2, j0:j0 + 512]), "dL")

                @c.sec("vector")
                def _(v):
                    c.wait(v, "dL")
                    c.inc(v.engine_nop(), "V")

                c.rep(abrow3, 0, abg3, 0, 512)
                c.rep(abrow3, 512, abg3, 512, 512)

                for k in range(NCH):
                    @c.sec("gpsimd")
                    def _(g, k=k, j0=j0):
                        c.wait(g, "V")
                        c.wait(g, "T")
                        c.inc(g.dma_start(out=x2_d[k * 128:(k + 1) * 128, j0:j0 + 512],
                                          in_=resid[k][:, j0:j0 + 512]), "dG")

                    @c.sec("vector")
                    def _(v, k=k, j0=j0):
                        c.wait(v, "dG")
                        sl = slice(j0, j0 + 512)
                        v.tensor_tensor(resid[k][:, sl], resid[k][:, sl], abg3[:, 0:512], ALU.mult)
                        c.inc(v.tensor_tensor(resid[k][:, sl], resid[k][:, sl], abg3[:, 512:1024], ALU.add), "V")

        self.relaxed = base_relax and ("c" in RELAX3) if RELAX3 else base_relax
        c.rms_stats(srcfn, arows1, 1e-6)

        hr3 = c.sb("s3_hr3", [1, 3], F32, ses)

        @c.sec("sync")
        def _(e):
            c.wait(e, "dG")
            c.inc(e.dma_start(out=hr3[:], in_=arows1[0:1, TPC - 3:TPC]), "dL")

        @c.sec("vector")
        def _(v):
            c.wait(v, "dL")
            c.inc(v.engine_nop(), "V")

        self.relaxed = base_relax and ("b" in RELAX3) if RELAX3 else base_relax
        @c.sec("gpsimd")
        def _(g):
            c.wait(g, "dL")
            c.wait(g, "V")
            for k in range(NCH):
                c.inc(g.dma_start(out=hg_in[k * 384:(k + 1) * 384].rearrange("(t p) -> p t", p=128),
                                  in_=resid[k][:, TPC - 3:TPC], allow_slow_non_contiguous=True), "dG")
            c.inc(g.dma_start(out=hg_in[3 * D:3 * D + 3].unsqueeze(0), in_=hr3[:]), "dG")
            c.wait(g, "dG")
            c.inc(g.collective_compute("AllGather", ALU.bypass,
                                       replica_groups=[[0, 1, 2, 3, 4, 5, 6, 7]],
                                       ins=[hg_in.ap().opt()], outs=[hg_out.ap().opt()]), "CC")

        hgall = c.sb("s3_hgall", [8, 384], F32, ses)
        hgrow = c.sb("s3_hgrow", [1, 384], F32, ses)
        dbgt = c.sb("s3_dbgt", [128, 64], F32, ses)
        self.dbgt_ref = dbgt
        self.dbg_hgrow = c.sb("s3_dbgrow", [1, 128], BF16, ses)
        for k in range(NCH):
            @c.sec("sync")
            def _(e, k=k):
                c.wait(e, "CC")
                c.wait(e, "dG")
                c.wait(e, "T")
                c.inc(e.dma_start(out=hgall[:], in_=hg_out[:, k * 384:(k + 1) * 384]), "dL")

            @c.sec("tensor")
            def _(t):
                c.wait(t, "dL")
                c.wait(t, "V")
                c.inc(t.matmul(pb[2][0:1, 0:384], prevhot[:], hgall[:], start=True, stop=True), "T")

            @c.sec("vector")
            def _(v):
                c.wait(v, "T")
                c.inc(v.tensor_copy(hgrow[:], pb[2][0:1, 0:384]), "V")

            @c.sec("gpsimd")
            def _(g):
                c.wait(g, "V")
                c.inc(g.dma_start(out=hg_in[0:384].unsqueeze(0), in_=hgrow[:]), "dG")

            @c.sec("sync")
            def _(e, k=k):
                c.wait(e, "dG")
                c.inc(e.dma_start(out=xh3[k][:], in_=hg_in[0:384].rearrange("(t p) -> p t", p=128),
                                  allow_slow_non_contiguous=True), "dL")

            @c.sec("vector")
            def _(v):
                c.wait(v, "dL")
                c.inc(v.engine_nop(), "V")

        if DEBUG:
            @c.sec("gpsimd")
            def _(g):
                c.wcomp(g)
                c.wait(g, "dL")
                for k in range(NCH):
                    c.inc(g.dma_start(out=c.dbg["dbg_h0"][k * 16:k * 16 + 3, 0:128],
                                      in_=xh3[k][:].transpose([1, 0]) if False else c.dbg and xh3[k][:]), "dO") if False else None
                c.inc(g.dma_start(out=c.dbg["dbg_h0"][0:128, 0:3], in_=xh3[0][:]), "dO")
                c.inc(g.dma_start(out=c.dbg["dbg_h0"][0:8, 16:400], in_=hgall[:]), "dO")

        def load_slab3(hname, m):
            @c.sec("sync")
            def _(e, hname=hname, m=m):
                c.wcomp(e)
                src2 = hb[hname][:, m * 128:(m + 1) * 128].rearrange("(s p) m -> p s m", p=128)
                c.inc(e.dma_start(out=sl32[:].rearrange("p (s m) -> p s m", m=128), in_=src2), "dL")

            @c.sec("vector")
            def _(v):
                c.wait(v, "dL")
                c.split(v, sl32[:], sl_h[:], sl_l[:])
                c.inc(v.engine_nop(), "V")

        def mm3(t, psum_ap, wh, wl, rh, rl, start, stop, inc=False):
            t.matmul(psum_ap, wh, rh, start=start, stop=False)
            t.matmul(psum_ap, wh, rl, start=False, stop=False)
            last = t.matmul(psum_ap, wl, rh, start=False, stop=stop)
            if inc:
                c.inc(last, "T")

        with ExitStack() as esa:
            hch = [c.sb(f"s3_hch{k}", [128, TPC], BF16, esa) for k in range(NCH)]
            hcl = [c.sb(f"s3_hcl{k}", [128, TPC], BF16, esa) for k in range(NCH)]
            es_conv = ExitStack()
            arep = c.sb("s3_arep", [128, TPC], F32, es_conv)
            arow_g = c.sb("s3_arowg", [1, TPC], F32, es_conv)
            r3rep = c.sb("s3_r3rep", [128, 3], F32, es_conv)
            hnp = c.sb("s3_hnp", [128, TPC + 3], F32, es_conv)
            hc32 = c.sb("s3_hc32", [128, TPC], F32, es_conv)
            rsh3 = c.sb("s3_rsh3", [1, 3], F32, es_conv)
            hgr3 = c.sb("s3_hgr3", [8, 3], F32, es_conv)

            @c.sec("sync")
            def _(e):
                c.wait(e, "CC")
                c.wait(e, "dG")
                c.inc(e.dma_start(out=hgr3[:], in_=hg_out[:, 3 * D:3 * D + 3]), "dL")

            @c.sec("tensor")
            def _(t):
                c.wait(t, "dL")
                c.wait(t, "V")
                c.inc(t.matmul(pb[1][0:1, 0:3], prevhot[:], hgr3[:], start=True, stop=True), "T")

            @c.sec("vector")
            def _(v):
                c.wait(v, "T")
                c.inc(v.tensor_copy(rsh3[:], pb[1][0:1, 0:3]), "V")

            @c.sec("sync")
            def _(e):
                c.wait(e, "dG")
                c.inc(e.dma_start(out=arow_g[:], in_=arows1[0:1, :]), "dL")
                if DEBUG:
                    c.inc(e.dma_start(out=hgrow[:, 0:16], in_=arows1[0:1, 0:16]), "dL")

            if DEBUG:
                @c.sec("gpsimd")
                def _(g):
                    c.wait(g, "dL")
                    c.inc(g.dma_start(out=c.dbg["dbg_h0"][4:5, 0:16], in_=hgrow[0:1, 0:16]), "dO")
                    c.inc(g.dma_start(out=c.dbg["dbg_h0"][5:6, 0:16], in_=arow_g[0:1, 0:16]), "dO")

                @c.sec("vector")
                def _(v):
                    c.wait(v, "dO")
                    c.inc(v.engine_nop(), "V")

            @c.sec("vector")
            def _(v):
                c.wait(v, "dL")
                c.inc(v.engine_nop(), "V")

            for n in range(4):
                c.rep(arow_g, n * 512, arep, n * 512, 512)

            @c.sec("tensor")
            def _(t):
                c.wait(t, "SC")
                c.wait(t, "V")
                c.inc(t.matmul(pb[2][:, 0:3], c.onesr[:], rsh3[:], start=True, stop=True), "T")

            @c.sec("vector")
            def _(v):
                c.wait(v, "T")
                c.inc(v.tensor_copy(r3rep[:], pb[2][:, 0:3]), "V")

            if DEBUG:
                @c.sec("gpsimd")
                def _(g):
                    c.wcomp(g)
                    c.inc(g.dma_start(out=c.dbg["dbg_loss"], in_=arow_g[0:1, 0:8]), "dO")
                    c.inc(g.dma_start(out=c.dbg["dbg_g8"][0:1, 0:3], in_=rsh3[0:1, 0:3]), "dO")
                    c.inc(g.dma_start(out=c.dbg["dbg_GG"], in_=arep[:, 0:8]), "dO")

                @c.sec("vector")
                def _(v):
                    c.wait(v, "dO")
                    c.inc(v.engine_nop(), "V")

            for k in range(NCH):
                @c.sec("vector")
                def _(v, k=k):
                    c.wait(v, "V")
                    v.scalar_tensor_tensor(hnp[:, 0:3], xh3[k][:], pnw_t[:, k:k + 1], r3rep[:],
                                           ALU.mult, ALU.mult)
                    v.scalar_tensor_tensor(hnp[:, 3:TPC + 3], resid[k][:, :], pnw_t[:, k:k + 1], arep[:],
                                           ALU.mult, ALU.mult)
                    if DEBUG and k == 0:
                        v.tensor_copy(dbgt[:, 16:19], hnp[:, 0:3])
                        v.tensor_copy(dbgt[:, 20:26], hnp[:, 3:9])
                        v.tensor_copy(dbgt[:, 26:29], r3rep[:])
                        v.tensor_copy(dbgt[:, 30:36], arep[:, 0:6])
                    c.inc(v.engine_nop(), "V")

                @c.sec("vector")
                def _(v, k=k):
                    c.wait(v, "V")
                    v.tensor_scalar(hc32[:], hnp[:, 0:TPC], convw_t[:, 4 * k:4 * k + 1],
                                    convb_t[:, k:k + 1], ALU.mult, ALU.add)
                    c.inc(v.engine_nop(), "V")

                for kk in range(1, 4):
                    @c.sec("vector")
                    def _(v, k=k, kk=kk):
                        c.wait(v, "V")
                        c.inc(v.scalar_tensor_tensor(hc32[:], hnp[:, kk:kk + TPC],
                                                     convw_t[:, 4 * k + kk:4 * k + kk + 1],
                                                     hc32[:], ALU.mult, ALU.add), "V")

                @c.sec("vector")
                def _(v, k=k):
                    c.wait(v, "V")
                    c.split(v, hc32[:], hch[k][:], hcl[k][:])
                    c.inc(v.engine_nop(), "V")

            es_conv.close()
            gate_t = c.sb("s3_gate", [128, TPC], F32, esa)
            ones32 = c.sb("s3_ones32", [128, 32], F32, esa)
            a32 = c.sb("s3_a32", [128, TPC], F32, esa)
            i32 = c.sb("s3_i32", [128, TPC], F32, esa)
            v32_ = c.sb("s3_v32", [128, TPC], F32, esa)
            t32a = c.sb("s3_t32a", [128, TPC], F32, esa)
            t32b = c.sb("s3_t32b", [128, TPC], F32, esa)
            Dc = c.sb("s3_Dc", [128, 32], F32, esa)
            il = c.sb("s3_il", [128, 32], F32, esa)
            rh16 = c.sb("s3_rh16", [128, TPC], BF16, esa)
            rl16 = c.sb("s3_rl16", [128, TPC], BF16, esa)
            Ssel = c.sb("s3_Ssel", [3, 512], F32, esa)
            Sk = [c.sb(f"s3_Sk{i}", [128, 8], F32, esa) for i in range(3)]
            Pk = [c.sb(f"s3_Pk{i}", [128, 8], F32, esa) for i in range(3)]
            seffm = c.sb("s3_seffm", [128, 32], F32, esa)

            @c.sec("vector")
            def _(v):
                c.wcomp(v)
                v.memset(gate_t[:], 1.0)
                c.inc(v.memset(ones32[:], 1.0), "V")

            @c.sec("vector")
            def _(v):
                c.wait(v, "V")
                c.inc(v.memset(gate_t[:].rearrange("p (c k) -> p c k", k=64)[:, :, 0:1], 0.0), "V")

            if S3CUT <= 1:
                return
            self.relaxed = base_relax and ("d" in RELAX3) if RELAX3 else base_relax
            for m in range(NCH):
                for wname, act in [("waT", "sig_b"), ("wiT", "sig"), ("wvT", "copy")]:
                    load_slab3(wname, m)
                    for n in range(4):
                        @c.sec("tensor")
                        def _(t, n=n):
                            c.wait(t, "V")
                            c.wait(t, "SC")
                            for kk in range(NCH):
                                mm3(t, pb[0][:, 0:512], sl_h[:, kk * 128:(kk + 1) * 128],
                                    sl_l[:, kk * 128:(kk + 1) * 128],
                                    hch[kk][:, n * 512:(n + 1) * 512], hcl[kk][:, n * 512:(n + 1) * 512],
                                    kk == 0, kk == NCH - 1, inc=(kk == NCH - 1))

                        if act == "copy":
                            @c.sec("vector")
                            def _(v, n=n):
                                c.wait(v, "T")
                                c.wait(v, "dG")
                                c.inc(v.tensor_copy(v32_[:, n * 512:(n + 1) * 512], pb[0][:, 0:512]), "V")
                        else:
                            @c.sec("scalar")
                            def _(s, n=n, act=act, m=m):
                                c.wait(s, "T")
                                c.wait(s, "dG")
                                dst = a32 if act == "sig_b" else i32
                                if act == "sig_b":
                                    c.inc(s.activation(dst[:, n * 512:(n + 1) * 512], pb[0][:, 0:512],
                                                       AF.Sigmoid, bias=dbias_t[:, m:m + 1]), "SC")
                                else:
                                    c.inc(s.activation(dst[:, n * 512:(n + 1) * 512], pb[0][:, 0:512],
                                                       AF.Sigmoid), "SC")

                @c.sec("vector")
                def _(v):
                    c.wait(v, "SC")
                    c.wait(v, "T")
                    c.inc(v.tensor_tensor(i32[:], i32[:], v32_[:], ALU.mult), "V")

                @c.sec("vector")
                def _(v):
                    c.wait(v, "V")
                    v.tensor_tensor(t32a[:], a32[:], a32[:], ALU.mult)
                    v.tensor_scalar(t32a[:], t32a[:], -1.0, 1.0, ALU.mult, ALU.add)
                    v.tensor_scalar_max(t32a[:], t32a[:], 1e-8)
                    c.inc(v.engine_nop(), "V")

                @c.sec("scalar")
                def _(s):
                    c.wait(s, "V")
                    c.inc(s.activation(t32a[:], t32a[:], AF.Sqrt), "SC")

                @c.sec("vector")
                def _(v):
                    c.wait(v, "SC")
                    v.tensor_tensor(i32[:], i32[:], t32a[:], ALU.mult)
                    v.tensor_scalar_max(t32b[:], a32[:], 1e-10)
                    c.inc(v.engine_nop(), "V")

                @c.sec("scalar")
                def _(s):
                    c.wait(s, "V")
                    c.inc(s.activation(t32b[:], t32b[:], AF.Ln), "SC")

                @c.sec("vector")
                def _(v):
                    c.wait(v, "SC")
                    c.inc(v.tensor_tensor_scan(t32a[:], gate_t[:], t32b[:], 0.0, ALU.mult, ALU.add), "V")

                @c.sec("vector")
                def _(v):
                    c.wait(v, "V")
                    v.tensor_scalar(t32b[:], t32a[:], -1.0, 30.0, ALU.mult, ALU.min)
                    c.inc(v.engine_nop(), "V")

                @c.sec("scalar")
                def _(s):
                    c.wait(s, "V")
                    s.activation(a32[:], t32a[:], AF.Exp)
                    c.inc(s.activation(t32b[:], t32b[:], AF.Exp), "SC")

                @c.sec("vector")
                def _(v):
                    c.wait(v, "SC")
                    v.tensor_scalar_min(t32b[:], t32b[:], 1e10)
                    v.tensor_tensor(i32[:], i32[:], t32b[:], ALU.mult)
                    c.inc(v.engine_nop(), "V")

                @c.sec("vector")
                def _(v):
                    c.wait(v, "V")
                    c.inc(v.tensor_tensor_scan(v32_[:], gate_t[:], i32[:], 0.0, ALU.mult, ALU.add), "V")

                if DEBUG and m == 0:
                    @c.sec("gpsimd")
                    def _(g):
                        c.wait(g, "V")
                        c.wait(g, "SC")
                        c.inc(g.dma_start(out=c.dbg["dbg_q"][0:128, :], in_=t32a[:, 0:512]), "dO")
                        c.inc(g.dma_start(out=c.dbg["dbg_q"][128:256, :], in_=i32[:, 0:512]), "dO")
                        c.inc(g.dma_start(out=c.dbg["dbg_q"][256:384, :], in_=v32_[:, 0:512]), "dO")
                        c.inc(g.dma_start(out=c.dbg["dbg_q"][384:512, :], in_=a32[:, 0:512]), "dO")
                        c.inc(g.dma_start(out=c.dbg["dbg_q"][512:640, :], in_=gate_t[:, 0:512]), "dO")

                    @c.sec("vector")
                    def _(v):
                        c.wait(v, "dO")
                        c.inc(v.engine_nop(), "V")

                @c.sec("vector")
                def _(v, m=m):
                    c.wait(v, "V")
                    c.wait(v, "dO")
                    cdv = a32[:].rearrange("p (c k) -> p c k", k=64)
                    csv = v32_[:].rearrange("p (c k) -> p c k", k=64)
                    v.tensor_copy(Dc[:], cdv[:, :, 63])
                    c.inc(v.tensor_tensor(il[:], cdv[:, :, 63], csv[:, :, 63], ALU.mult), "V")

                @c.sec("vector")
                def _(v, m=m):
                    c.wait(v, "V")
                    v.tensor_tensor_scan(Lsc[:, m * 32:(m + 1) * 32], Dc[:], il[:], 0.0, ALU.mult, ALU.add)
                    c.inc(v.tensor_tensor_scan(Cpr[:, m * 32:(m + 1) * 32], Dc[:], ones32[:], 1.0,
                                               ALU.mult, ALU.mult), "V")

                @c.sec("vector")
                def _(v, m=m):
                    c.wait(v, "V")
                    v.tensor_copy(Sloc[:, m:m + 1], Lsc[:, m * 32 + 31:m * 32 + 32])
                    c.inc(v.tensor_copy(Ptot[:, m:m + 1], Cpr[:, m * 32 + 31:m * 32 + 32]), "V")

                @c.sec("gpsimd")
                def _(g, m=m):
                    c.wait(g, "V")
                    c.inc(g.dma_start(out=cd_d[m * 128:(m + 1) * 128, :], in_=a32[:]), "dG")
                    c.inc(g.dma_start(out=cs_d[m * 128:(m + 1) * 128, :], in_=v32_[:]), "dG")

            self.relaxed = base_relax and ("e" in RELAX3) if RELAX3 else base_relax
            @c.sec("gpsimd")
            def _(g):
                c.wait(g, "V")
                c.wait(g, "dG")
                c.inc(g.dma_start(out=ag_in[0:1024].rearrange("(m p) -> p m", p=128), in_=Sloc[:],
                                  allow_slow_non_contiguous=True), "dG")
                c.inc(g.dma_start(out=ag_in[1024:2048].rearrange("(m p) -> p m", p=128), in_=Ptot[:],
                                  allow_slow_non_contiguous=True), "dG")
                c.wait(g, "dG")
                c.inc(g.collective_compute("AllGather", ALU.bypass,
                                           replica_groups=[[0, 1, 2, 3, 4, 5, 6, 7]],
                                           ins=[ag_in.ap().opt()], outs=[ag_out.ap().opt()]), "CC")

            agall = c.sb("s3_agall", [8, 512], F32, esa)
            for j in range(4):
                @c.sec("sync")
                def _(e, j=j):
                    c.wait(e, "CC")
                    c.wait(e, "T")
                    c.inc(e.dma_start(out=agall[:], in_=ag_out[:, j * 512:(j + 1) * 512]), "dL")

                @c.sec("tensor")
                def _(t, j=j):
                    c.wait(t, "dL")
                    c.wait(t, "V")
                    c.inc(t.matmul(pb[2][0:3, 0:512], predselT[:], agall[:], start=True, stop=True), "T")

                @c.sec("vector")
                def _(v, j=j):
                    c.wait(v, "T")
                    c.inc(v.tensor_copy(Ssel[:], pb[2][0:3, 0:512]), "V")

                @c.sec("gpsimd")
                def _(g, j=j):
                    c.wait(g, "V")
                    c.inc(g.dma_start(out=sel_d[:, j * 512:(j + 1) * 512], in_=Ssel[:]), "dG")

            @c.sec("sync")
            def _(e):
                c.wait(e, "dG")
                for i in range(3):
                    c.inc(e.dma_start(out=Sk[i][:], in_=sel_d[i, 0:1024].rearrange("(m p) -> p m", p=128),
                                      allow_slow_non_contiguous=True), "dL")
                    c.inc(e.dma_start(out=Pk[i][:], in_=sel_d[i, 1024:2048].rearrange("(m p) -> p m", p=128),
                                      allow_slow_non_contiguous=True), "dL")

            stmp = c.sb("s3_stmp", [128, 8], F32, esa)

            @c.sec("vector")
            def _(v):
                c.wait(v, "dL")
                c.inc(v.tensor_tensor(stmp[:], Pk[1][:], Sk[2][:], ALU.mult), "V")

            @c.sec("vector")
            def _(v):
                c.wait(v, "V")
                c.inc(v.tensor_tensor(seff0[:], Sk[1][:], stmp[:], ALU.add), "V")

            @c.sec("vector")
            def _(v):
                c.wait(v, "V")
                c.inc(v.tensor_tensor(stmp[:], Pk[0][:], seff0[:], ALU.mult), "V")

            @c.sec("vector")
            def _(v):
                c.wait(v, "V")
                c.inc(v.tensor_tensor(seff0[:], Sk[0][:], stmp[:], ALU.add), "V")

            if DEBUG:
                @c.sec("gpsimd")
                def _(g):
                    c.wcomp(g)
                    c.inc(g.dma_start(out=c.dbg["dbg_GG"], in_=seff0[:]), "dO")
                    c.inc(g.dma_start(out=c.dbg["dbg_bvec"], in_=Sk[0][:]), "dO")
                    c.inc(g.dma_start(out=c.dbg["dbg_lossr"], in_=Pk[0][:]), "dO")
                    c.inc(g.dma_start(out=c.dbg["dbg_G2"][:, 0:4], in_=Sloc[:, 0:4]), "dO")
                    c.inc(g.dma_start(out=c.dbg["dbg_loss"][:, 0:8], in_=Lsc[0:1, 0:8]), "dO")

            for m in range(NCH):
                @c.sec("sync")
                def _(e, m=m):
                    c.wcomp(e)
                    c.wait(e, "dG")
                    c.inc(e.dma_start(out=a32[:], in_=cd_d[m * 128:(m + 1) * 128, :]), "dL")
                    c.inc(e.dma_start(out=v32_[:], in_=cs_d[m * 128:(m + 1) * 128, :]), "dL")

                @c.sec("vector")
                def _(v, m=m):
                    c.wait(v, "dL")
                    c.inc(v.tensor_copy(seffm[:, 0:1], seff0[:, m:m + 1]), "V")

                @c.sec("vector")
                def _(v, m=m):
                    c.wait(v, "V")
                    c.inc(v.scalar_tensor_tensor(seffm[:, 1:32], Cpr[:, m * 32:m * 32 + 31],
                                                 seff0[:, m:m + 1], Lsc[:, m * 32:m * 32 + 31],
                                                 ALU.mult, ALU.add), "V")

                @c.sec("vector")
                def _(v, m=m):
                    c.wait(v, "V")
                    csv = v32_[:].rearrange("p (c k) -> p c k", k=64)
                    bc = seffm[:].unsqueeze(2).broadcast_to([128, 32, 64])
                    v.tensor_tensor(csv, csv, bc, ALU.add)
                    v.tensor_tensor(v32_[:], a32[:], v32_[:], ALU.mult)
                    c.split(v, v32_[:], rh16[:], rl16[:])
                    c.inc(v.engine_nop(), "V")

                @c.sec("gpsimd")
                def _(g, m=m):
                    c.wait(g, "V")
                    c.inc(g.dma_start(out=rech_d[m * 128:(m + 1) * 128, :], in_=rh16[:]), "dG")
                    c.inc(g.dma_start(out=recl_d[m * 128:(m + 1) * 128, :], in_=rl16[:]), "dG")
                    if DEBUG and m == 0:
                        c.inc(g.dma_start(out=c.dbg["dbg_xn"][0:128, :], in_=a32[:, 0:640]), "dO")
                        c.inc(g.dma_start(out=c.dbg["dbg_xn"][128:256, :], in_=v32_[:, 0:640]), "dO")

            if S3CUT <= 2:
                return

        self.relaxed = base_relax and ("f" in RELAX3) if RELAX3 else base_relax
        with ExitStack() as esb:
            es_op = ExitStack()
            rbh = c.sb("s3_rbh", [128, 8 * 512], BF16, es_op)
            rbl = c.sb("s3_rbl", [128, 8 * 512], BF16, es_op)

            for n in range(4):
                @c.sec("sync")
                def _(e, n=n):
                    c.wcomp(e)
                    c.wait(e, "dG")
                    for kk in range(NCH):
                        c.inc(e.dma_start(out=rbh[:, kk * 512:(kk + 1) * 512],
                                          in_=rech_d[kk * 128:(kk + 1) * 128, n * 512:(n + 1) * 512]), "dL")
                        c.inc(e.dma_start(out=rbl[:, kk * 512:(kk + 1) * 512],
                                          in_=recl_d[kk * 128:(kk + 1) * 128, n * 512:(n + 1) * 512]), "dL")

                @c.sec("vector")
                def _(v):
                    c.wait(v, "dL")
                    c.inc(v.engine_nop(), "V")

                if DEBUG:
                    @c.sec("vector")
                    def _(v, n=n):
                        c.wait(v, "dL")
                        v.tensor_reduce(dbgt[:, 40 + n:41 + n], rbh[:], AX, ALU.max, True)
                        c.inc(v.tensor_reduce(dbgt[:, 44 + n:45 + n], rbl[:], AX, ALU.max, True), "V")

                for m in range(NCH):
                    load_slab3("wopT", m)

                    @c.sec("tensor")
                    def _(t, m=m, n=n):
                        c.wait(t, "V")
                        for kk in range(NCH):
                            mm3(t, pb[0][:, 0:512], sl_h[:, kk * 128:(kk + 1) * 128],
                                sl_l[:, kk * 128:(kk + 1) * 128],
                                rbh[:, kk * 512:(kk + 1) * 512], rbl[:, kk * 512:(kk + 1) * 512],
                                kk == 0, kk == NCH - 1, inc=(kk == NCH - 1))

                    @c.sec("vector")
                    def _(v, m=m, n=n):
                        c.wait(v, "T")
                        sl = slice(n * 512, (n + 1) * 512)
                        c.inc(v.tensor_tensor(resid[m][:, sl], resid[m][:, sl], pb[0][:, 0:512],
                                              ALU.add), "V")

            if DEBUG:
                @c.sec("gpsimd")
                def _(g):
                    c.wcomp(g)
                    for k in range(NCH):
                        c.inc(g.dma_start(out=c.dbg["dbg_x2b"][k * 128:(k + 1) * 128, :], in_=resid[k][:]), "dO")

                @c.sec("vector")
                def _(v):
                    c.wait(v, "dO")
                    c.inc(v.engine_nop(), "V")

            es_op.close()
            if S3CUT <= 3:
                return
            self.relaxed = base_relax and ("g" in RELAX3) if RELAX3 else base_relax
            c.rms_stats(srcfn, arows2, 1e-6)

            hn2h = [c.sb(f"s3_hn2h{k}", [128, 512], BF16, esb) for k in range(NCH)]
            hn2l = [c.sb(f"s3_hn2l{k}", [128, 512], BF16, esb) for k in range(NCH)]
            arep2 = c.sb("s3_arep2", [128, 512], F32, esb)
            arow2 = c.sb("s3_arow2", [1, 512], F32, esb)
            hbuf = c.sb("s3_hbuf", [128, 512], F32, esb)

            dsl32 = c.sb("s3_dsl32", [128, 2048], F32, esb)
            dsl_h = c.sb("s3_dslh", [128, 2048], BF16, esb)
            dsl_l = c.sb("s3_dsll", [128, 2048], BF16, esb)
            suh = c.sb("s3_suh", [128, 32 * 512], BF16, esb)
            sul = c.sb("s3_sul", [128, 32 * 512], BF16, esb)
            g32 = c.sb("s3_g32", [128, 512], F32, esb)
            u32 = c.sb("s3_u32", [128, 512], F32, esb)

            for nq in range(4):
                sl = slice(nq * 512, (nq + 1) * 512)
                @c.sec("sync")
                def _(e, nq=nq):
                    c.wcomp(e)
                    c.wait(e, "dG")
                    c.inc(e.dma_start(out=arow2[:], in_=arows2[0:1, nq * 512:(nq + 1) * 512]), "dL")

                @c.sec("vector")
                def _(v):
                    c.wait(v, "dL")
                    c.inc(v.engine_nop(), "V")

                c.rep(arow2, 0, arep2, 0, 512)
                for k in range(NCH):
                    @c.sec("vector")
                    def _(v, k=k, nq=nq):
                        c.wait(v, "T")
                        c.wait(v, "V")
                        v.scalar_tensor_tensor(hbuf[:], resid[k][:, sl], fnw_t[:, k:k + 1], arep2[:],
                                               ALU.mult, ALU.mult)
                        c.split(v, hbuf[:], hn2h[k][:], hn2l[k][:])
                        c.inc(v.engine_nop(), "V")

                for j in range(32):
                    @c.sec("sync")
                    def _(e, j=j):
                        c.wcomp(e)
                        src2 = hb["wguT"][:, j * 256:(j + 1) * 256].rearrange("(s p) m -> p s m", p=128)
                        c.inc(e.dma_start(out=dsl32[:].rearrange("p (s m) -> p s m", m=256), in_=src2), "dL")

                    @c.sec("vector")
                    def _(v):
                        c.wait(v, "dL")
                        c.split(v, dsl32[:], dsl_h[:], dsl_l[:])
                        c.inc(v.engine_nop(), "V")

                    @c.sec("tensor")
                    def _(t, j=j, nq=nq):
                        c.wait(t, "V")
                        for kk in range(NCH):
                            mm3(t, pb[0][:, 0:512], dsl_h[:, kk * 256:kk * 256 + 128],
                                dsl_l[:, kk * 256:kk * 256 + 128],
                                hn2h[kk][:], hn2l[kk][:], kk == 0, kk == NCH - 1)
                        for kk in range(NCH):
                            mm3(t, pb[1][:, 0:512], dsl_h[:, kk * 256 + 128:kk * 256 + 256],
                                dsl_l[:, kk * 256 + 128:kk * 256 + 256],
                                hn2h[kk][:], hn2l[kk][:], kk == 0, kk == NCH - 1,
                                inc=(kk == NCH - 1))

                    @c.sec("scalar")
                    def _(s):
                        c.wait(s, "T")
                        c.inc(s.activation(g32[:], pb[0][:, 0:512], AF.Silu), "SC")

                    @c.sec("vector")
                    def _(v):
                        c.wait(v, "SC")
                        c.inc(v.tensor_copy(u32[:], pb[1][:, 0:512]), "V")

                    @c.sec("vector")
                    def _(v, j=j):
                        c.wait(v, "V")
                        v.tensor_tensor(g32[:], g32[:], u32[:], ALU.mult)
                        c.split(v, g32[:], suh[:, j * 512:(j + 1) * 512], sul[:, j * 512:(j + 1) * 512])
                        c.inc(v.engine_nop(), "V")

                for m in range(NCH):
                    for half in range(2):
                        @c.sec("sync")
                        def _(e, m=m, half=half):
                            c.wcomp(e)
                            src2 = hb["wdnT"][half * 2048:(half + 1) * 2048, m * 128:(m + 1) * 128].rearrange(
                                "(s p) m -> p s m", p=128)
                            c.inc(e.dma_start(out=dsl32[:].rearrange("p (s m) -> p s m", m=128), in_=src2), "dL")

                        @c.sec("vector")
                        def _(v):
                            c.wait(v, "dL")
                            c.split(v, dsl32[:], dsl_h[:], dsl_l[:])
                            c.inc(v.engine_nop(), "V")

                        @c.sec("tensor")
                        def _(t, m=m, half=half):
                            c.wait(t, "V")
                            for kk in range(16):
                                kj = half * 16 + kk
                                mm3(t, pb[2][:, 0:512], dsl_h[:, kk * 128:(kk + 1) * 128],
                                    dsl_l[:, kk * 128:(kk + 1) * 128],
                                    suh[:, kj * 512:(kj + 1) * 512], sul[:, kj * 512:(kj + 1) * 512],
                                    kj == 0, kj == 31, inc=(kk == 15))

                    @c.sec("vector")
                    def _(v, m=m, nq=nq):
                        c.wait(v, "T")
                        c.inc(v.tensor_tensor(resid[m][:, sl], resid[m][:, sl], pb[2][:, 0:512],
                                              ALU.add), "V")

            x2t = c.sb("s3_x2t", [128, TPC], F32, esb)
            for k in range(NCH):
                @c.sec("sync")
                def _(e, k=k):
                    c.wcomp(e)
                    c.inc(e.dma_start(out=x2t[:], in_=x2_d[k * 128:(k + 1) * 128, :]), "dL")

                @c.sec("vector")
                def _(v, k=k):
                    c.wait(v, "dL")
                    c.inc(v.tensor_tensor(resid[k][:], resid[k][:], x2t[:], ALU.add), "V")


    def s4_l1ffn(self, ses):
        c = self
        hb, resid = self.hb, self.resid
        nc = self.nc
        pb = c.pb

        abrows4 = nc.dram_tensor("s4_abrows", [2, TPC], F32).ap()
        mx_d = nc.dram_tensor("s4_mxd", [128], F32)
        mr_in = nc.dram_tensor("s4_mrin", [1], F32)
        mr_out = nc.dram_tensor("s4_mrout", [1], F32)

        def srcfn(k, j0, w):
            return resid[k][:, j0:j0 + w]
        c.ln_stats(TPC, srcfn, [(i * 512, 512) for i in range(4)], abrows4)

        xn32 = [c.sb(f"s4_xn{k}", [128, TPC], F32, ses) for k in range(NCH)]
        xqb = [c.sb(f"s4_xq{k}", [128, TPC], BF16, ses) for k in range(NCH)]
        xqi = c.sb("s4_xqi", [128, TPC], I8, ses)
        abg = c.sb("s4_abg", [128, 1024], F32, ses)
        abrow = c.sb("s4_abrow", [1, 1024], F32, ses)
        amax = c.sb("s4_amax", [128, 32], F32, ses)
        amax1 = c.sb("s4_amax1", [128, 1], F32, ses)
        arow128 = c.sb("s4_arow128", [1, 128], F32, ses)
        sc1 = c.sb("s4_sc1", [1, 1], F32, ses)
        scr = c.sb("s4_scr", [1, 1], F32, ses)
        screp = c.sb("s4_screp", [128, 1], F32, ses)
        wsc_t = c.sb("s4_wsc", [128, 8], F32, ses)
        qb_t = c.sb("s4_qb", [128, 8], F32, ses)
        cs_t = c.sb("s4_cs", [128, 8], F32, ses)
        sl32 = c.sb("s4_sl32", [128, 1024], I8, ses)
        sl_b = c.sb("s4_slb", [128, 1024], BF16, ses)
        t32 = c.sb("s4_t32", [128, 512], F32, ses)

        @c.sec("sync")
        def _(e):
            c.wcomp(e)
            c.inc(e.dma_start(out=wsc_t[:], in_=hb["wsc"].rearrange("(c p) -> p c", p=128),
                              allow_slow_non_contiguous=True), "dL")
            c.inc(e.dma_start(out=qb_t[:], in_=hb["qbias"].rearrange("(c p) -> p c", p=128),
                              allow_slow_non_contiguous=True), "dL")

        for n in range(4):
            j0 = n * 512

            @c.sec("sync")
            def _(e, j0=j0):
                c.wcomp(e)
                c.wait(e, "dG")
                c.inc(e.dma_start(out=abrow[:, 0:512], in_=abrows4[0:1, j0:j0 + 512]), "dL")
                c.inc(e.dma_start(out=abrow[:, 512:1024], in_=abrows4[1:2, j0:j0 + 512]), "dL")

            @c.sec("vector")
            def _(v):
                c.wait(v, "dL")
                c.inc(v.engine_nop(), "V")

            c.rep(abrow, 0, abg, 0, 512)
            c.rep(abrow, 512, abg, 512, 512)

            for k in range(NCH):
                @c.sec("vector")
                def _(v, k=k, j0=j0, n=n):
                    c.wait(v, "T")
                    sl = slice(j0, j0 + 512)
                    v.tensor_tensor(xn32[k][:, sl], resid[k][:, sl], abg[:, 0:512], ALU.mult)
                    v.tensor_tensor(xn32[k][:, sl], xn32[k][:, sl], abg[:, 512:1024], ALU.add)
                    c.inc(v.engine_nop(), "V")

                @c.sec("vector")
                def _(v, k=k, j0=j0, n=n):
                    c.wait(v, "V")
                    c.inc(v.tensor_reduce(amax[:, n * 8 + k:n * 8 + k + 1], xn32[k][:, j0:j0 + 512],
                                          AX, ALU.max, True), "V")

        @c.sec("vector")
        def _(v):
            c.wait(v, "V")
            c.inc(v.tensor_reduce(amax1[:], amax[:], AX, ALU.max), "V")

        @c.sec("gpsimd")
        def _(g):
            c.wait(g, "V")
            c.inc(g.dma_start(out=mx_d[:], in_=amax1[:, 0]), "dG")

        @c.sec("sync")
        def _(e):
            c.wait(e, "dG")
            c.inc(e.dma_start(out=arow128[:], in_=mx_d[:].unsqueeze(0)), "dL")

        @c.sec("vector")
        def _(v):
            c.wait(v, "dL")
            c.inc(v.tensor_reduce(sc1[:], arow128[:], AX, ALU.max), "V")

        @c.sec("gpsimd")
        def _(g):
            c.wait(g, "V")
            c.inc(g.dma_start(out=mr_in[:].unsqueeze(0), in_=sc1[:]), "dG")
            c.wait(g, "dG")
            c.inc(g.collective_compute("AllReduce", ALU.max,
                                       replica_groups=[[0, 1, 2, 3, 4, 5, 6, 7]],
                                       ins=[mr_in.ap().opt()], outs=[mr_out.ap().opt()]), "CC")
            c.wait(g, "CC")
            c.inc(g.dma_start(out=sc1[:], in_=mr_out[:].unsqueeze(0)), "dG")

        @c.sec("vector")
        def _(v):
            c.wait(v, "dG")
            v.tensor_scalar(sc1[:], sc1[:], 1.0 / 127.0, 1e-5, ALU.mult, ALU.max)
            c.inc(v.engine_nop(), "V")

        @c.sec("scalar")
        def _(s):
            c.wait(s, "V")
            c.inc(s.activation(scr[:], sc1[:], AF.Ln), "SC")

        @c.sec("scalar")
        def _(s):
            c.wait(s, "SC")
            c.inc(s.activation(scr[:], scr[:], AF.Exp, scale=-1.0), "SC")

        @c.sec("tensor")
        def _(t):
            c.wait(t, "SC")
            c.wait(t, "V")
            t.matmul(pb[2][:, 0:1], c.onesr[:], scr[:], start=True, stop=True)
            c.inc(t.matmul(pb[2][:, 1:2], c.onesr[:], sc1[:], start=True, stop=True), "T")

        @c.sec("vector")
        def _(v):
            c.wait(v, "T")
            c.inc(v.tensor_copy(screp[:], pb[2][:, 0:1]), "V")

        @c.sec("vector")
        def _(v):
            c.wait(v, "V")
            c.inc(v.tensor_scalar(cs_t[:], wsc_t[:], pb[2][:, 1:2], None, ALU.mult), "V")

        for k in range(NCH):
            @c.sec("vector")
            def _(v, k=k):
                c.wait(v, "V")
                c.inc(v.tensor_scalar(xn32[k][:], xn32[k][:], screp[:, 0:1], None, ALU.mult), "V")

            @c.sec("vector")
            def _(v, k=k):
                c.wait(v, "V")
                c.inc(v.tensor_copy(xqi[:], xn32[k][:]), "V")

            @c.sec("vector")
            def _(v, k=k):
                c.wait(v, "V")
                c.inc(v.tensor_copy(xqb[k][:], xqi[:]), "V")

        for m in range(NCH):
            @c.sec("sync")
            def _(e, m=m):
                c.wcomp(e)
                src2 = hb["wqT"][:, m * 128:(m + 1) * 128].rearrange("(s p) m -> p s m", p=128)
                c.inc(e.dma_start(out=sl32[:].rearrange("p (s m) -> p s m", m=128), in_=src2), "dL")

            @c.sec("vector")
            def _(v):
                c.wait(v, "dL")
                c.inc(v.tensor_copy(sl_b[:], sl32[:]), "V")

            for n in range(4):
                @c.sec("tensor")
                def _(t, m=m, n=n):
                    c.wait(t, "V")
                    for kk in range(NCH):
                        last = t.matmul(pb[0][:, 0:512], sl_b[:, kk * 128:(kk + 1) * 128],
                                        xqb[kk][:, n * 512:(n + 1) * 512],
                                        start=(kk == 0), stop=(kk == NCH - 1))
                    c.inc(last, "T")

                @c.sec("vector")
                def _(v, m=m, n=n):
                    c.wait(v, "T")
                    c.inc(v.scalar_tensor_tensor(t32[:], pb[0][:, 0:512], cs_t[:, m:m + 1],
                                                 resid[m][:, n * 512:(n + 1) * 512],
                                                 ALU.mult, ALU.add), "V")

                @c.sec("vector")
                def _(v, m=m, n=n):
                    c.wait(v, "V")
                    c.inc(v.tensor_scalar(resid[m][:, n * 512:(n + 1) * 512], t32[:],
                                          qb_t[:, m:m + 1], None, ALU.add), "V")



def build_program(upto=4):
    nc = bass.Bass("TRN2", target_bir_lowering=False)
    es = ExitStack()
    with es, nc.Block() as block:
        mk = MK(nc, block, es, upto)
        mk.build()
    return nc


def kernel(**inputs):
    nc = build_program(upto=4)
    maps = make_inputs(inputs)
    res = run_bass_kernel_spmd(nc, maps, core_ids=list(range(8)))
    full = np.empty((B, T, D), np.float32)
    for ci in range(8):
        b, t0 = ci // 4, (ci % 4) * TPC
        full[b, t0:t0 + TPC, :] = res.results[ci]["out"].T
    return full


# version 07 (final)
# speedup vs baseline: 1.0980x; 1.0980x over previous
import os
import numpy as np
import ml_dtypes
from contextlib import ExitStack

PARANOID = os.environ.get("MK_PARANOID", "0") == "1"
RELAX = set(int(x) for x in os.environ.get("MK_RELAX", "1,2,3,4").split(",") if x)
RELAX3 = set(os.environ.get("MK_RELAX3", ""))
DEBUG = os.environ.get("MK_DEBUG", "0") == "1"
S2CUT = int(os.environ.get("MK_S2_CUT", "9"))
S3CUT = int(os.environ.get("MK_S3_CUT", "9"))

import concourse.bass as bass
import concourse.mybir as mybir
from concourse.bass_utils import run_bass_kernel_spmd

F32 = mybir.dt.float32
BF16 = mybir.dt.bfloat16
I8 = mybir.dt.int8
AF = mybir.ActivationFunctionType
ALU = mybir.AluOpType
AX = mybir.AxisListType.X

B, T, D = 2, 8192, 1024
TPC = 2048
NCH = D // 128
NG, GQ = 4, 512
NHEADS, DH = 16, 64
NEG = -1e30



def make_inputs(inp):
    x = np.asarray(inp["L0_attn_x"])
    maps = []
    ones_row = np.ones((1, 128), np.float32)
    sel01 = np.zeros((2, 2, 128), np.float32)
    sel01[0, 0, :] = 1.0
    sel01[1, 1, :] = 1.0
    ji = np.arange(128)
    m_tri = np.tile(np.where(ji[:, None] <= ji[None, :], 0.0, NEG), (1, 4)).astype(ml_dtypes.bfloat16)
    wqkvT = np.ascontiguousarray(np.asarray(inp["L0_attn_w_qkv"]).T)
    woutT = np.ascontiguousarray(np.asarray(inp["L0_attn_w_out"]).T)
    wbase = np.ascontiguousarray(np.asarray(inp["L0_ffn_W_base"]))
    bbase = np.ascontiguousarray(np.asarray(inp["L0_ffn_b_base"]))
    wd1 = np.ascontiguousarray(np.asarray(inp["L0_ffn_Wd1"]).transpose(1, 0, 2).reshape(D, 8 * 256))
    bd1 = np.ascontiguousarray(np.asarray(inp["L0_ffn_bd1"]).reshape(-1))
    wd2 = np.ascontiguousarray(np.asarray(inp["L0_ffn_Wd2"]).reshape(8 * 256, D))
    bd2 = np.ascontiguousarray(np.asarray(inp["L0_ffn_bd2"]).reshape(-1))
    adn = np.ascontiguousarray(np.asarray(inp["L0_ffn_Wa_down"]).transpose(1, 0, 2).reshape(D, 512))
    badn = np.ascontiguousarray(np.asarray(inp["L0_ffn_ba_down"]).reshape(-1))
    aup = np.ascontiguousarray(np.asarray(inp["L0_ffn_Wa_up"]).reshape(512, D))
    pnw = np.ascontiguousarray(np.asarray(inp["L1_attn_pre_norm_w"]))
    fnw = np.ascontiguousarray(np.asarray(inp["L1_attn_ffn_norm_w"]))
    convw = np.ascontiguousarray(np.asarray(inp["L1_attn_conv_w"]).reshape(D, 4))
    convb = np.ascontiguousarray(np.asarray(inp["L1_attn_conv_b"]))
    waT = np.ascontiguousarray(np.asarray(inp["L1_attn_w_a"]).T)
    wiT = np.ascontiguousarray(np.asarray(inp["L1_attn_w_i"]).T)
    wvT = np.ascontiguousarray(np.asarray(inp["L1_attn_w_v"]).T)
    wopT = np.ascontiguousarray(np.asarray(inp["L1_attn_out_proj_w"]).T)
    dbias = np.ascontiguousarray(np.asarray(inp["L1_attn_decay_bias"]))
    wgu = np.asarray(inp["L1_attn_w_gate_up"]).T
    wguT = np.empty_like(wgu)
    for j in range(32):
        wguT[:, 256 * j:256 * j + 128] = wgu[:, 128 * j:128 * (j + 1)]
        wguT[:, 256 * j + 128:256 * j + 256] = wgu[:, 4096 + 128 * j:4096 + 128 * (j + 1)]
    wguT = np.ascontiguousarray(wguT)
    wdnT = np.ascontiguousarray(np.asarray(inp["L1_attn_w_down"]).T)
    wqT = np.ascontiguousarray(np.asarray(inp["L1_ffn_weight_int8"]).T)
    wsc = np.ascontiguousarray(np.asarray(inp["L1_ffn_weight_scale"]).reshape(-1))
    qbias = np.ascontiguousarray(np.asarray(inp["L1_ffn_bias"]))
    bup = np.ascontiguousarray(np.asarray(inp["L0_ffn_ba_up"]).reshape(8, 8, 128).transpose(2, 1, 0).reshape(128, 64))
    for c in range(8):
        b, t0 = c // 4, (c % 4) * TPC
        prevhot = np.zeros((8, 1), np.float32)
        if c % 4 > 0:
            prevhot[c - 1, 0] = 1.0
        predselT = np.zeros((8, 3), np.float32)
        for i in range(3):
            if c % 4 - 1 - i >= 0:
                predselT[c - 1 - i, i] = 1.0
        x_f = np.ascontiguousarray(x[b, t0:t0 + TPC, :].T)
        xprev = (np.ascontiguousarray(x[b, t0 - 128:t0, :].T) if t0 >= 128
                 else np.zeros((D, 128), np.float32))
        xstr = np.ascontiguousarray(x[b, 0::64, :].T)
        qg = t0 + np.arange(TPC)[None, :]
        ki = np.arange(128)[:, None]
        kg = (qg // 128) * 128 - 128 + ki
        allow_band = (kg >= 0) & (kg <= qg) & (((qg - kg) <= 128) | (kg % 64 == 0))
        m_band = np.where(allow_band, 0.0, NEG).astype(ml_dtypes.bfloat16)
        q0sub = (qg // 128) * 128
        allow_str = (64 * np.arange(128)[:, None]) < (q0sub - 128)
        m_str = np.where(allow_str, 0.0, NEG).astype(ml_dtypes.bfloat16)
        maps.append({
            "x_f": x_f, "xprev": xprev, "xstr": xstr,
            "m_tri": m_tri, "m_band": m_band, "m_str": m_str,
            "ones_row": ones_row, "sel01": sel01,
            "wqkvT": wqkvT, "woutT": woutT,
            "wbase": wbase, "bbase": bbase, "wd1": wd1, "bd1": bd1, "wd2": wd2,
            "bd2": bd2, "adn": adn, "badn": badn, "aup": aup, "bup": bup,
            "prevhot": prevhot, "predselT": predselT,
            "pnw": pnw, "fnw": fnw, "convw": convw, "convb": convb,
            "waT": waT, "wiT": wiT, "wvT": wvT, "wopT": wopT, "dbias": dbias,
            "wguT": wguT, "wdnT": wdnT,
            "wqT": wqT, "wsc": wsc, "qbias": qbias,
        })
    return maps



class MK:
    def __init__(self, nc, block, es, upto):
        self.nc, self.block, self.es, self.upto = nc, block, es, upto
        self.h, self.n = {}, {}
        for name in ("dL", "dG", "dS", "T", "V", "SC", "CC", "dO"):
            self.h[name] = es.enter_context(nc.semaphore(name))
            self.n[name] = 0
        self.pb = [es.enter_context(nc.psum_tensor(f"pb{i}", [128, 512], F32)) for i in range(8)]

    def inc(self, instr, name, amt=None):
        if amt is None:
            amt = 16 if name.startswith("d") else 1
        instr.then_inc(self.h[name], amt)
        self.n[name] += amt

    def waitv(self, eng, name, val):
        if PARANOID or not getattr(self, "relaxed", True):
            self.wait(eng, name)
            return
        eng.wait_ge(self.h[name], val)

    def wait(self, eng, name, val=None):
        if PARANOID or not getattr(self, "relaxed", True):
            for nm in self.h:
                if nm != "dO":
                    eng.wait_ge(self.h[nm], self.n[nm])
            return
        eng.wait_ge(self.h[name], self.n[name] if val is None else val)

    def wcomp(self, eng):
        for nm in ("T", "V", "SC"):
            self.wait(eng, nm)

    _uid = 0

    def sb(self, name, shape, dt=F32, es=None):
        MK._uid += 1
        return (es or self.es).enter_context(self.nc.sbuf_tensor(f"{name}_u{MK._uid}", shape, dt))

    def sec(self, eng):
        return getattr(self.block, eng)

    @staticmethod
    def split(v, src32, hi, lo):
        v.tensor_copy(hi, src32)
        v.tensor_tensor(lo, src32, hi, ALU.subtract)

    def declare_io(self):
        nc = self.nc
        self.hb = {}

        def hin(name, shape, dt=F32):
            self.hb[name] = nc.dram_tensor(name, list(shape), dt, kind="ExternalInput").ap()

        hin("x_f", (D, TPC)); hin("xprev", (D, 128)); hin("xstr", (D, 128))
        hin("m_tri", (128, 512), BF16); hin("m_band", (128, TPC), BF16); hin("m_str", (128, TPC), BF16)
        hin("ones_row", (1, 128)); hin("sel01", (2, 2, 128))
        hin("wqkvT", (D, 3 * D)); hin("woutT", (D, D))
        hin("wbase", (D, D)); hin("bbase", (D,)); hin("wd1", (D, 2048)); hin("bd1", (2048,))
        hin("wd2", (2048, D)); hin("bd2", (8192,)); hin("adn", (D, 512)); hin("badn", (512,))
        hin("aup", (512, D)); hin("bup", (128, 64))
        hin("pnw", (D,)); hin("fnw", (D,)); hin("convw", (D, 4)); hin("convb", (D,))
        hin("waT", (D, D)); hin("wiT", (D, D)); hin("wvT", (D, D)); hin("wopT", (D, D))
        hin("dbias", (D,)); hin("wguT", (D, 8192)); hin("wdnT", (4096, D))
        hin("prevhot", (8, 1)); hin("predselT", (8, 3))
        hin("wqT", (D, D), I8); hin("wsc", (D,)); hin("qbias", (D,))
        self.out_h = nc.dram_tensor("out", [D, TPC], F32, kind="ExternalOutput").ap()
        if DEBUG:
            self.dbg = {}
            for nm, shape in [("dbg_xn", (D, 640)), ("dbg_q", (D, 512)), ("dbg_k", (D, 640)),
                              ("dbg_v", (640, D)), ("dbg_kst", (D, 128)), ("dbg_vst", (128, D)),
                              ("dbg_att", (128, 8 * 512)), ("dbg_attl", (128, 8 * 512)),
                              ("dbg_E", (128, 3 * 512)), ("dbg_ab", (128, 1280)),
                              ("dbg_loss", (1, 8)), ("dbg_g8", (1, 8)), ("dbg_GG", (128, 8)),
                              ("dbg_G2", (128, 4)), ("dbg_bvec", (128, 8)), ("dbg_lossr", (128, 8)),
                              ("dbg_a0", (128, 512)), ("dbg_h0", (128, 512)),
                              ("dbg_x2b", (D, TPC))]:
                self.dbg[nm] = nc.dram_tensor(nm, list(shape), F32, kind="ExternalOutput").ap()

    def build(self):
        self.declare_io()
        c = self
        self.resid = [c.sb(f"resid{k}", [128, TPC]) for k in range(NCH)]
        self.onesr = c.sb("onesr", [1, 128])
        self.onesc_b = c.sb("onesc_b", [128, 1], BF16)
        self.eps_ln = c.sb("eps_ln", [1, 1])

        @c.sec("sync")
        def _(e):
            for k in range(NCH):
                c.inc(e.dma_start(out=c.resid[k][:], in_=c.hb["x_f"][k * 128:(k + 1) * 128, :]), "dL")
            c.inc(e.dma_start(out=c.onesr[:], in_=c.hb["ones_row"]), "dL")

        @c.sec("vector")
        def _(v):
            c.wait(v, "dL")
            v.memset(c.onesc_b[:], 1.0)
            c.inc(v.memset(c.eps_ln[:], 1e-5), "V")

        if c.upto >= 1:
            c.relaxed = 1 in RELAX
            with ExitStack() as st_es:
                c.s1_l0attn(st_es)
        if c.upto >= 2:
            c.relaxed = 2 in RELAX
            with ExitStack() as st_es:
                c.s2_l0ffn(st_es)
        if c.upto >= 3:
            c.relaxed = 3 in RELAX
            with ExitStack() as st_es:
                c.s3_l1attn(st_es)
        if c.upto >= 4:
            c.relaxed = 4 in RELAX
            with ExitStack() as st_es:
                c.s4_l1ffn(st_es)
        c.relaxed = False

        snapT, snapV, snapS = c.n["T"], c.n["V"], c.n["SC"]

        if DEBUG and c.upto >= 3:
            @c.sec("gpsimd")
            def _(g):
                c.wcomp(g)
                c.inc(g.dma_start(out=c.dbg["dbg_vst"][0:128, 0:64], in_=self.dbgt_ref[:]), "dO")
                c.inc(g.dma_start(out=c.dbg["dbg_att"][:, 0:256], in_=self.dbg_Lsc[:]), "dO")
                c.inc(g.dma_start(out=c.dbg["dbg_att"][:, 256:512], in_=self.dbg_Cpr[:]), "dO")
                c.inc(g.dma_start(out=c.dbg["dbg_att"][:, 512:520], in_=self.dbg_Sloc[:]), "dO")
                c.inc(g.dma_start(out=c.dbg["dbg_att"][:, 520:528], in_=self.dbg_Ptot[:]), "dO")
                c.inc(g.dma_start(out=c.dbg["dbg_att"][:, 528:536], in_=self.dbg_seff0[:]), "dO")
                c.wait(g, "dO")
                c.inc(g.dma_start(out=self.dbg_hgrow[0:1, 0:64], in_=self.dbg_rech[0:1, 0:64]), "dG")
                c.inc(g.dma_start(out=self.dbg_hgrow[0:1, 64:128], in_=self.dbg_recl[0:1, 0:64]), "dG")
                c.wait(g, "dG")
                c.inc(g.dma_start(out=c.dbg["dbg_att"][0:1, 536:600], in_=self.dbg_hgrow[0:1, 0:64]), "dO")
                c.inc(g.dma_start(out=c.dbg["dbg_att"][0:1, 600:664], in_=self.dbg_hgrow[0:1, 64:128]), "dO")

        @c.sec("gpsimd")
        def _(g):
            g.wait_ge(c.h["T"], snapT)
            g.wait_ge(c.h["V"], snapV)
            g.wait_ge(c.h["SC"], snapS)
            for k in range(NCH):
                c.inc(g.dma_start(out=c.out_h[k * 128:(k + 1) * 128, :], in_=c.resid[k][:]), "dO")
            c.wait(g, "dO")

    def ln_stats(self, ntok, srcfn, nsl, dram_rows, eps=1e-5):
        c = self
        ps_sum, ps_sq = c.pb[0], c.pb[1]
        with ExitStack() as les:
            sxh = c.sb("st_sxh", [128, 512], BF16, les)
            sxl = c.sb("st_sxl", [128, 512], BF16, les)
            sq32 = c.sb("st_sq32", [128, 512], F32, les)
            sqh = c.sb("st_sqh", [128, 512], BF16, les)
            sql = c.sb("st_sql", [128, 512], BF16, les)
            srow = c.sb("st_srow", [1, ntok], F32, les)
            qrow = c.sb("st_qrow", [1, ntok], F32, les)
            arow = c.sb("st_arow", [1, ntok], F32, les)
            brow = c.sb("st_brow", [1, ntok], F32, les)
            epsv = c.sb("st_eps", [1, 1], F32, les)

            @c.sec("vector")
            def _(v):
                c.wcomp(v)
                c.inc(v.memset(epsv[:], eps), "V")

            for (j0, w) in nsl:
                for k in range(NCH):
                    @c.sec("vector")
                    def _(v, k=k, j0=j0, w=w):
                        c.wait(v, "T")
                        c.split(v, srcfn(k, j0, w), sxh[:, 0:w], sxl[:, 0:w])
                        c.inc(v.engine_nop(), "V")

                    @c.sec("scalar")
                    def _(s, k=k, j0=j0, w=w):
                        c.wait(s, "V")
                        c.wait(s, "T")
                        c.inc(s.activation(sq32[:, 0:w], srcfn(k, j0, w), AF.Square), "SC")

                    @c.sec("vector")
                    def _(v, w=w):
                        c.wait(v, "SC")
                        c.split(v, sq32[:, 0:w], sqh[:, 0:w], sql[:, 0:w])
                        c.inc(v.engine_nop(), "V")

                    @c.sec("tensor")
                    def _(t, k=k, w=w):
                        c.wait(t, "V")
                        st = (k == 0)
                        fin = (k == NCH - 1)
                        t.matmul(ps_sum[0:1, 0:w], c.onesc_b[:], sxh[:, 0:w], start=st, stop=False)
                        t.matmul(ps_sum[0:1, 0:w], c.onesc_b[:], sxl[:, 0:w], start=False, stop=fin)
                        t.matmul(ps_sq[0:1, 0:w], c.onesc_b[:], sqh[:, 0:w], start=st, stop=False)
                        c.inc(t.matmul(ps_sq[0:1, 0:w], c.onesc_b[:], sql[:, 0:w], start=False, stop=fin), "T")

                @c.sec("vector")
                def _(v, j0=j0, w=w):
                    c.wait(v, "T")
                    v.tensor_scalar_mul(srow[:, j0:j0 + w], ps_sum[0:1, 0:w], 1.0 / D)
                    c.inc(v.tensor_scalar_mul(qrow[:, j0:j0 + w], ps_sq[0:1, 0:w], 1.0 / D), "V")

            @c.sec("vector")
            def _(v):
                v.tensor_tensor(arow[:, 0:ntok], srow[:, 0:ntok], srow[:, 0:ntok], ALU.mult)
                c.inc(v.tensor_tensor(qrow[:, 0:ntok], qrow[:, 0:ntok], arow[:, 0:ntok], ALU.subtract), "V")

            @c.sec("scalar")
            def _(s):
                c.wait(s, "V")
                s.activation(arow[:, 0:ntok], qrow[:, 0:ntok], AF.Ln, bias=epsv[0:1, 0:1])
                c.inc(s.activation(arow[:, 0:ntok], arow[:, 0:ntok], AF.Exp, scale=-0.5), "SC")

            @c.sec("vector")
            def _(v):
                c.wait(v, "SC")
                v.tensor_tensor(brow[:, 0:ntok], srow[:, 0:ntok], arow[:, 0:ntok], ALU.mult)
                c.inc(v.tensor_scalar_mul(brow[:, 0:ntok], brow[:, 0:ntok], -1.0), "V")

            @c.sec("gpsimd")
            def _(g):
                c.wait(g, "V")
                c.inc(g.dma_start(out=dram_rows[0:1, 0:ntok], in_=arow[:, 0:ntok]), "dG")
                c.inc(g.dma_start(out=dram_rows[1:2, 0:ntok], in_=brow[:, 0:ntok]), "dG")

            @c.sec("vector")
            def _(v):
                c.wait(v, "dG")
                c.inc(v.engine_nop(), "V")

    def rep(self, row, so, dst, do, w):
        c = self
        pr = c.pb[2]

        @c.sec("tensor")
        def _(t):
            c.wait(t, "V")
            c.wait(t, "dL")
            c.inc(t.matmul(pr[:, 0:w], c.onesr[:], row[0:1, so:so + w], start=True, stop=True), "T")

        @c.sec("vector")
        def _(v):
            c.wait(v, "T")
            c.inc(v.tensor_copy(dst[:, do:do + w], pr[:, 0:w]), "V")

    def s1_l0attn(self, ses):
        c = self
        hb, resid = self.hb, self.resid
        NTOK = TPC + 256

        xpv = [c.sb(f"xpv{k}", [128, 256], F32, ses) for k in range(NCH)]
        abrows = self.nc.dram_tensor("abrows", [2, NTOK], F32).ap()
        arow_g = c.sb("arow_g", [1, 768], F32, ses)
        brow_g = c.sb("brow_g", [1, 768], F32, ses)
        mtri = c.sb("mtri", [128, 512], BF16, ses)
        mband = c.sb("mband", [128, 512], BF16, ses)
        mstr = c.sb("mstr", [128, 512], BF16, ses)

        @c.sec("sync")
        def _(e):
            for k in range(NCH):
                c.inc(e.dma_start(out=xpv[k][:, 0:128], in_=hb["xprev"][k * 128:(k + 1) * 128, :]), "dL")
                c.inc(e.dma_start(out=xpv[k][:, 128:256], in_=hb["xstr"][k * 128:(k + 1) * 128, :]), "dL")
            c.inc(e.dma_start(out=mtri[:], in_=hb["m_tri"]), "dL")

        @c.sec("vector")
        def _(v):
            c.wait(v, "dL")
            c.inc(v.engine_nop(), "V")

        def srcfn(k, j0, w):
            if j0 < TPC:
                return resid[k][:, j0:j0 + w]
            return xpv[k][:, j0 - TPC:j0 - TPC + w]

        nsl = [(i * 512, 512) for i in range(4)] + [(TPC, 256)]
        c.ln_stats(NTOK, srcfn, nsl, abrows)

        ab = c.sb("ab", [128, 1280], F32, ses)
        abs_ = c.sb("abs", [128, 256], F32, ses)
        U = c.sb("uarena", [128, 10240], BF16, ses)
        xng_h = [U[:, k * 640:(k + 1) * 640] for k in range(NCH)]
        xng_l = [U[:, 5120 + k * 640:5120 + (k + 1) * 640] for k in range(NCH)]
        atth = U[:, 0:4096]
        attl = U[:, 4096:8192]
        xns_h = [c.sb(f"xns_h{k}", [128, 128], BF16, ses) for k in range(NCH)]
        xns_l = [c.sb(f"xns_l{k}", [128, 128], BF16, ses) for k in range(NCH)]
        q32 = [c.sb(f"q32_{k}", [128, GQ], F32, ses) for k in range(NCH)]
        k32 = [c.sb(f"k32_{k}", [128, 640], F32, ses) for k in range(NCH)]
        v32 = [c.sb(f"v32_{t}", [128, D], F32, ses) for t in range(5)]
        kst = [c.sb(f"kst{k}", [128, 128], F32, ses) for k in range(NCH)]
        vst = c.sb("vst", [128, D], F32, ses)
        sl32 = c.sb("sl32", [128, 1024], F32, ses)
        sl_h = c.sb("sl_h", [128, 1024], BF16, ses)
        sl_l = c.sb("sl_l", [128, 1024], BF16, ses)
        Et = [c.sb(f"Et{i}", [128, 512], F32, ses) for i in range(3)]
        r2 = c.sb("r2", [2, 512], F32, ses)
        av32 = c.sb("av32", [64, 512], F32, ses)
        atmp = c.sb("atmp", [64, 1024], BF16, ses)
        ecol = c.sb("ecol", [128, 2], F32, ses)
        sel0 = c.sb("sel0", [2, 128], F32, ses)
        sel1 = c.sb("sel1", [2, 128], F32, ses)
        xn32 = c.sb("xn32", [128, 512], F32, ses)
        defr = c.sb("defr", [128, NCH * 128], F32, ses)
        @c.sec("sync")
        def _(e):
            c.inc(e.dma_start(out=sel0[:], in_=hb["sel01"][0]), "dL")
            c.inc(e.dma_start(out=sel1[:], in_=hb["sel01"][1]), "dL")

        L = dict(locals())
        for g in range(NG):
            self._s1_group(g, L)

    def _s1_group(self, g, L):
        c = self
        hb, resid = self.hb, self.resid
        G0 = g * GQ
        ab, abs_ = L["ab"], L["abs_"]
        xng_h, xng_l, xns_h, xns_l = L["xng_h"], L["xng_l"], L["xns_h"], L["xns_l"]
        q32, k32, v32, kst, vst = L["q32"], L["k32"], L["v32"], L["kst"], L["vst"]
        sl32, sl_h, sl_l = L["sl32"], L["sl_h"], L["sl_l"]
        Et, r2, av32, atmp = L["Et"], L["r2"], L["av32"], L["atmp"]
        atth, attl = L["atth"], L["attl"]
        ecol, xn32, defr = L["ecol"], L["xn32"], L["defr"]
        mtri, mband, mstr = L["mtri"], L["mband"], L["mstr"]
        xpv = L["xpv"]
        arow_g, brow_g, abrows = L["arow_g"], L["brow_g"], L["abrows"]
        pb = c.pb

        hstat0 = TPC if g == 0 else G0 - 128

        @c.sec("sync")
        def _(e):
            c.wcomp(e)
            c.inc(e.dma_start(out=mband[:], in_=hb["m_band"][:, G0:G0 + 512]), "dL")
            c.inc(e.dma_start(out=mstr[:], in_=hb["m_str"][:, G0:G0 + 512]), "dL")
            if g == 0:
                c.wait(e, "dG")
                c.inc(e.dma_start(out=arow_g[:, 0:128], in_=abrows[0:1, TPC:TPC + 128]), "dL")
                c.inc(e.dma_start(out=arow_g[:, 128:640], in_=abrows[0:1, 0:512]), "dL")
                c.inc(e.dma_start(out=arow_g[:, 640:768], in_=abrows[0:1, TPC + 128:TPC + 256]), "dL")
                c.inc(e.dma_start(out=brow_g[:, 0:128], in_=abrows[1:2, TPC:TPC + 128]), "dL")
                c.inc(e.dma_start(out=brow_g[:, 128:640], in_=abrows[1:2, 0:512]), "dL")
                c.inc(e.dma_start(out=brow_g[:, 640:768], in_=abrows[1:2, TPC + 128:TPC + 256]), "dL")
            else:
                c.inc(e.dma_start(out=arow_g[:, 0:640], in_=abrows[0:1, G0 - 128:G0 + 512]), "dL")
                c.inc(e.dma_start(out=brow_g[:, 0:640], in_=abrows[1:2, G0 - 128:G0 + 512]), "dL")

        @c.sec("vector")
        def _(v):
            c.wait(v, "dL")
            c.inc(v.engine_nop(), "V")

        c.rep(arow_g, 0, ab, 0, 128)
        c.rep(arow_g, 128, ab, 128, 512)
        c.rep(brow_g, 0, ab, 640, 128)
        c.rep(brow_g, 128, ab, 768, 512)
        if g == 0:
            c.rep(arow_g, 640, abs_, 0, 128)
            c.rep(brow_g, 640, abs_, 128, 128)

        for k in range(NCH):
            @c.sec("vector")
            def _(v, k=k):
                c.wait(v, "T")
                c.wait(v, "dG")
                hsrc = xpv[k][:, 0:128] if g == 0 else resid[k][:, G0 - 128:G0]
                v.tensor_tensor(xn32[:, 0:128], hsrc, ab[:, 0:128], ALU.mult)
                v.tensor_tensor(xn32[:, 0:128], xn32[:, 0:128], ab[:, 640:768], ALU.add)
                c.split(v, xn32[:, 0:128], xng_h[k][:, 0:128], xng_l[k][:, 0:128])
                for half in range(2):
                    j0, j1 = half * 256, (half + 1) * 256
                    v.tensor_tensor(xn32[:, 0:256], resid[k][:, G0 + j0:G0 + j1], ab[:, 128 + j0:128 + j1], ALU.mult)
                    v.tensor_tensor(xn32[:, 0:256], xn32[:, 0:256], ab[:, 768 + j0:768 + j1], ALU.add)
                    c.split(v, xn32[:, 0:256], xng_h[k][:, 128 + j0:128 + j1], xng_l[k][:, 128 + j0:128 + j1])
                if g == 0:
                    v.tensor_tensor(xn32[:, 0:128], xpv[k][:, 128:256], abs_[:, 0:128], ALU.mult)
                    v.tensor_tensor(xn32[:, 0:128], xn32[:, 0:128], abs_[:, 128:256], ALU.add)
                    c.split(v, xn32[:, 0:128], xns_h[k][:], xns_l[k][:])
                if g > 0:
                    v.tensor_tensor(resid[k][:, G0 - 128:G0], resid[k][:, G0 - 128:G0],
                                    defr[:, k * 128:(k + 1) * 128], ALU.add)
                c.inc(v.engine_nop(), "V")

        def load_slab(colskip, m, width=128, p=128):
            @c.sec("sync")
            def _(e, colskip=colskip, m=m, width=width, p=p):
                c.wcomp(e)
                src = hb["wqkvT"][:, colskip + m * width:colskip + (m + 1) * width].rearrange(
                    "(s p) m -> p s m", p=p)
                dst = sl32[:, 0:(1024 // p) * width].rearrange("p (s m) -> p s m", m=width)
                c.inc(e.dma_start(out=dst, in_=src), "dL")

            @c.sec("vector")
            def _(v, width=width, p=p):
                c.wait(v, "dL")
                w = (1024 // p) * width
                c.split(v, sl32[:, 0:w], sl_h[:, 0:w], sl_l[:, 0:w])
                c.inc(v.engine_nop(), "V")

        def mm3(t, psum_ap, wh, wl, rh, rl, start, stop, inc=False):
            t.matmul(psum_ap, wh, rh, start=start, stop=False)
            t.matmul(psum_ap, wh, rl, start=False, stop=False)
            last = t.matmul(psum_ap, wl, rh, start=False, stop=stop)
            if inc:
                c.inc(last, "T")

        for m in range(NCH):
            load_slab(0, m)

            @c.sec("tensor")
            def _(t, m=m):
                c.wait(t, "V")
                for kk in range(NCH):
                    mm3(t, pb[0][:, 0:512], sl_h[:, kk * 128:(kk + 1) * 128], sl_l[:, kk * 128:(kk + 1) * 128],
                        xng_h[kk][:, 128:640], xng_l[kk][:, 128:640], kk == 0, kk == NCH - 1, inc=(kk == NCH - 1))

            @c.sec("vector")
            def _(v, m=m):
                c.wait(v, "T")
                c.inc(v.tensor_copy(q32[m][:], pb[0][:, 0:512]), "V")

        for m in range(NCH):
            load_slab(D, m)

            @c.sec("tensor")
            def _(t, m=m):
                c.wait(t, "V")
                for kk in range(NCH):
                    mm3(t, pb[0][:, 0:512], sl_h[:, kk * 128:(kk + 1) * 128], sl_l[:, kk * 128:(kk + 1) * 128],
                        xng_h[kk][:, 0:512], xng_l[kk][:, 0:512], kk == 0, kk == NCH - 1)
                for kk in range(NCH):
                    mm3(t, pb[1][:, 0:128], sl_h[:, kk * 128:(kk + 1) * 128], sl_l[:, kk * 128:(kk + 1) * 128],
                        xng_h[kk][:, 512:640], xng_l[kk][:, 512:640], kk == 0, kk == NCH - 1,
                        inc=(g != 0 and kk == NCH - 1))
                if g == 0:
                    for kk in range(NCH):
                        mm3(t, pb[2][:, 0:128], sl_h[:, kk * 128:(kk + 1) * 128], sl_l[:, kk * 128:(kk + 1) * 128],
                            xns_h[kk][:], xns_l[kk][:], kk == 0, kk == NCH - 1, inc=(kk == NCH - 1))

            @c.sec("vector")
            def _(v, m=m):
                c.wait(v, "T")
                v.tensor_copy(k32[m][:, 0:512], pb[0][:, 0:512])
                v.tensor_copy(k32[m][:, 512:640], pb[1][:, 0:128])
                if g == 0:
                    v.tensor_copy(kst[m][:], pb[2][:, 0:128])
                c.inc(v.engine_nop(), "V")

        ntt = 6 if g == 0 else 5
        for half in range(2):
            for kk in range(NCH):
                @c.sec("sync")
                def _(e, half=half, kk=kk):
                    c.wcomp(e)
                    src = hb["wqkvT"][kk * 128:(kk + 1) * 128, 2 * D + half * 512:2 * D + (half + 1) * 512]
                    c.inc(e.dma_start(out=sl32[:, 0:512], in_=src), "dL")

                @c.sec("vector")
                def _(v):
                    c.wait(v, "dL")
                    c.split(v, sl32[:, 0:512], sl_h[:, 0:512], sl_l[:, 0:512])
                    c.inc(v.engine_nop(), "V")

                @c.sec("tensor")
                def _(t, half=half, kk=kk):
                    c.wait(t, "V")
                    for tt in range(ntt):
                        if tt < 5:
                            ah = xng_h[kk][:, tt * 128:(tt + 1) * 128]
                            al = xng_l[kk][:, tt * 128:(tt + 1) * 128]
                        else:
                            ah, al = xns_h[kk][:], xns_l[kk][:]
                        t.matmul(pb[tt][:, 0:512], ah, sl_h[:, 0:512], start=(kk == 0), stop=False)
                        t.matmul(pb[tt][:, 0:512], al, sl_h[:, 0:512], start=False, stop=False)
                        last = t.matmul(pb[tt][:, 0:512], ah, sl_l[:, 0:512], start=False, stop=(kk == NCH - 1))
                    c.inc(last, "T")

            @c.sec("vector")
            def _(v, half=half):
                c.wait(v, "T")
                for tt in range(ntt):
                    dst = v32[tt] if tt < 5 else vst
                    last = v.tensor_copy(dst[:, half * 512:(half + 1) * 512], pb[tt][:, 0:512])
                c.inc(last, "V")

        for p in range(8):
            self._s1_pair(g, p, L)

        if DEBUG and g == 0:
            @c.sec("gpsimd")
            def _(gp):
                c.wcomp(gp)
                c.wait(gp, "dG")
                for k in range(NCH):
                    c.inc(gp.dma_start(out=c.dbg["dbg_q"][k * 128:(k + 1) * 128, :], in_=q32[k][:]), "dO")
                    c.inc(gp.dma_start(out=c.dbg["dbg_k"][k * 128:(k + 1) * 128, :], in_=k32[k][:]), "dO")
                    c.inc(gp.dma_start(out=c.dbg["dbg_kst"][k * 128:(k + 1) * 128, :], in_=kst[k][:]), "dO")
                for tt in range(5):
                    c.inc(gp.dma_start(out=c.dbg["dbg_v"][tt * 128:(tt + 1) * 128, :], in_=v32[tt][:]), "dO")
                c.inc(gp.dma_start(out=c.dbg["dbg_vst"][:], in_=vst[:]), "dO")
                c.inc(gp.dma_start(out=c.dbg["dbg_ab"][:], in_=ab[:]), "dO")
                c.inc(gp.dma_start(out=c.dbg["dbg_att"][:], in_=atth[:].bitcast(F32) if False else atth[:]), "dO")
                c.inc(gp.dma_start(out=c.dbg["dbg_attl"][:], in_=attl[:]), "dO")

        for m in range(NCH):
            @c.sec("sync")
            def _(e, m=m):
                c.wcomp(e)
                src = hb["woutT"][:, m * 128:(m + 1) * 128].rearrange("(s p) m -> p s m", p=128)
                c.inc(e.dma_start(out=sl32[:].rearrange("p (s m) -> p s m", m=128), in_=src), "dL")

            @c.sec("vector")
            def _(v):
                c.wait(v, "dL")
                c.split(v, sl32[:], sl_h[:], sl_l[:])
                c.inc(v.engine_nop(), "V")

            @c.sec("tensor")
            def _(t, m=m):
                c.wait(t, "V")
                c.wait(t, "dG")
                for kk in range(NCH):
                    mm3(t, pb[0][:, 0:512], sl_h[:, kk * 128:(kk + 1) * 128], sl_l[:, kk * 128:(kk + 1) * 128],
                        atth[:, kk * 512:(kk + 1) * 512], attl[:, kk * 512:(kk + 1) * 512],
                        kk == 0, kk == NCH - 1, inc=(kk == NCH - 1))

            @c.sec("vector")
            def _(v, m=m):
                c.wait(v, "T")
                if g < NG - 1:
                    v.tensor_tensor(resid[m][:, G0:G0 + 384], resid[m][:, G0:G0 + 384],
                                    pb[0][:, 0:384], ALU.add)
                    c.inc(v.tensor_copy(defr[:, m * 128:(m + 1) * 128], pb[0][:, 384:512]), "V")
                else:
                    c.inc(v.tensor_tensor(resid[m][:, G0:G0 + 512], resid[m][:, G0:G0 + 512],
                                          pb[0][:, 0:512], ALU.add), "V")

    def _s1_pair(self, g, p, L):
        c = self
        G0 = g * GQ
        q32, k32, v32, kst, vst = L["q32"], L["k32"], L["v32"], L["kst"], L["vst"]
        Et, r2, av32, atmp = L["Et"], L["r2"], L["av32"], L["atmp"]
        atth, attl = L["atth"], L["attl"]
        ecol, sel0, sel1 = L["ecol"], L["sel0"], L["sel1"]
        mtri, mband, mstr = L["mtri"], L["mband"], L["mstr"]
        pb = c.pb
        SC0, SC1, SC2, SUM, AV0, AV1, RR, XX = pb

        for hi_ in range(2):
            h = 2 * p + hi_
            base = 64 * hi_
            kch, qch = k32[p], q32[p]

            @c.sec("tensor")
            def _(t, hi_=hi_, base=base, kch=kch, qch=qch):
                c.wait(t, "V")
                for j in range(4):
                    t.matmul(SC0[:, j * 128:(j + 1) * 128],
                             kch[base:base + 64, 128 + j * 128:256 + j * 128],
                             qch[base:base + 64, j * 128:(j + 1) * 128], start=True, stop=True)
                for j in range(4):
                    t.matmul(SC1[:, j * 128:(j + 1) * 128],
                             kch[base:base + 64, j * 128:(j + 1) * 128],
                             qch[base:base + 64, j * 128:(j + 1) * 128], start=True, stop=True)
                c.inc(t.matmul(SC2[:, 0:512], kst[p][base:base + 64, :], qch[base:base + 64, :],
                               start=True, stop=True), "T")

            @c.sec("vector")
            def _(v):
                c.wait(v, "T")
                v.scalar_tensor_tensor(Et[0][:], SC0[:, 0:512], 0.125, mtri[:], ALU.mult, ALU.add)
                v.scalar_tensor_tensor(Et[1][:], SC1[:, 0:512], 0.125, mband[:], ALU.mult, ALU.add)
                c.inc(v.scalar_tensor_tensor(Et[2][:], SC2[:, 0:512], 0.125, mstr[:],
                                             ALU.mult, ALU.add), "V")

            @c.sec("scalar")
            def _(s):
                c.wait(s, "V")
                s.activation(Et[0][:], Et[0][:], AF.Exp)
                s.activation(Et[1][:], Et[1][:], AF.Exp)
                c.inc(s.activation(Et[2][:], Et[2][:], AF.Exp), "SC")

            @c.sec("vector")
            def _(v, hi_=hi_):
                c.wait(v, "SC")
                c.wait(v, "T")
                v.memset(ecol[:, hi_:hi_ + 1], 1.0)
                c.inc(v.memset(ecol[:, 1 - hi_:2 - hi_], 0.0), "V")

            @c.sec("tensor")
            def _(t, hi_=hi_):
                c.wait(t, "V")
                st = (hi_ == 0)
                t.matmul(SUM[0:2, 0:512], ecol[:], Et[0][:], start=st, stop=False)
                t.matmul(SUM[0:2, 0:512], ecol[:], Et[1][:], start=False, stop=False)
                c.inc(t.matmul(SUM[0:2, 0:512], ecol[:], Et[2][:], start=False, stop=(hi_ == 1)), "T")

            AVb = (AV0, AV1)[hi_]

            @c.sec("tensor")
            def _(t, h=h, AVb=AVb):
                for j in range(4):
                    jc = slice(j * 128, (j + 1) * 128)
                    t.matmul(AVb[0:64, jc], v32[1 + j][:, h * 64:(h + 1) * 64], Et[0][:, jc],
                             start=True, stop=False)
                    t.matmul(AVb[0:64, jc], v32[j][:, h * 64:(h + 1) * 64], Et[1][:, jc],
                             start=False, stop=False)
                    last = t.matmul(AVb[0:64, jc], vst[:, h * 64:(h + 1) * 64], Et[2][:, jc],
                                    start=False, stop=True)
                c.inc(last, "T")

        @c.sec("scalar")
        def _(s):
            c.wait(s, "T")
            s.activation(r2[0:2, :], SUM[0:2, 0:512], AF.Ln)
            c.inc(s.activation(r2[0:2, :], r2[0:2, :], AF.Exp, scale=-1.0), "SC")

        for hi_ in range(2):
            h = 2 * p + hi_
            AVb = (AV0, AV1)[hi_]

            @c.sec("tensor")
            def _(t, hi_=hi_):
                c.wait(t, "SC")
                c.wait(t, "V")
                sel = sel0 if hi_ == 0 else sel1
                c.inc(t.matmul(RR[0:64, 0:512], sel[:, 0:64], r2[0:2, :], start=True, stop=True), "T")

            @c.sec("vector")
            def _(v):
                c.wait(v, "T")
                c.wait(v, "dG")
                c.inc(v.tensor_copy(av32[:], RR[0:64, 0:512]), "V")

            @c.sec("vector")
            def _(v, hi_=hi_, h=h, p=p, AVb=AVb):
                c.wait(v, "V")
                v.tensor_tensor(av32[:], av32[:], AVb[0:64, 0:512], ALU.mult)
                if hi_ == 0:
                    c.split(v, av32[:], atth[0:64, p * 512:(p + 1) * 512], attl[0:64, p * 512:(p + 1) * 512])
                    c.inc(v.engine_nop(), "V")
                else:
                    c.split(v, av32[:], atmp[:, 0:512], atmp[:, 512:1024])
                    c.inc(v.engine_nop(), "V")

            if hi_ == 1:
                @c.sec("gpsimd")
                def _(gp, p=p):
                    c.wait(gp, "V")
                    c.inc(gp.dma_start(out=atth[64:128, p * 512:(p + 1) * 512], in_=atmp[:, 0:512]), "dG")
                    c.inc(gp.dma_start(out=attl[64:128, p * 512:(p + 1) * 512], in_=atmp[:, 512:1024]), "dG")


    def s2_l0ffn(self, ses):
        c = self
        hb, resid = self.hb, self.resid
        core = None

        abrows = self.nc.dram_tensor("abrows2", [2, TPC], F32).ap()
        lb_in = self.nc.dram_tensor("lb_in", [8], F32)
        lb_out = self.nc.dram_tensor("lb_out", [8], F32)

        def srcfn(k, j0, w):
            return resid[k][:, j0:j0 + w]

        nsl = [(i * 512, 512) for i in range(4)]
        c.ln_stats(TPC, srcfn, nsl, abrows)

        xn32d = self.nc.dram_tensor("xn32d", [D, TPC], F32).ap()
        xn32s = c.sb("f_xn32s", [128, 512], F32, ses)
        xnh = [c.sb(f"f_xnh{k}", [128, TPC], BF16, ses) for k in range(NCH)]
        xnl = [c.sb(f"f_xnl{k}", [128, TPC], BF16, ses) for k in range(NCH)]
        abg = c.sb("f_abg", [128, 1024], F32, ses)
        abrow = c.sb("f_abrow", [1, 1024], F32, ses)
        sl32 = c.sb("f_sl32", [128, 1024], F32, ses)
        sl_h = c.sb("f_slh", [128, 1024], BF16, ses)
        sl_l = c.sb("f_sll", [128, 1024], BF16, ses)
        hh = [c.sb(f"f_hh{i}", [128, TPC], BF16, ses) for i in range(2)]
        hl = [c.sb(f"f_hl{i}", [128, TPC], BF16, ses) for i in range(2)]
        h32 = c.sb("f_h32", [128, 512], F32, ses)
        d32 = c.sb("f_d32", [128, 512], F32, ses)
        dump = c.sb("f_dump", [128, 512], F32, ses)
        lossb = c.sb("f_lossb", [128, 256], F32, ses)
        lossr = c.sb("f_lossr", [128, 8], F32, ses)
        loss8 = c.sb("f_loss8", [1, 8], F32, ses)
        lmin = c.sb("f_lmin", [1, 1], F32, ses)
        g8 = c.sb("f_g8", [1, 8], F32, ses)
        GG = c.sb("f_GG", [128, 8], F32, ses)
        G2 = c.sb("f_G2", [128, 4], F32, ses)
        onesc_f = c.sb("f_onescf", [128, 1], F32, ses)
        bvec = c.sb("f_bvec", [128, 8], F32, ses)
        bu_t = c.sb("f_bu", [128, 64], F32, ses)
        bb_t = c.sb("f_bb", [128, 8], F32, ses)
        bd1_t = c.sb("f_bd1", [128, 16], F32, ses)
        bd2_t = c.sb("f_bd2", [128, 64], F32, ses)
        badn_t = c.sb("f_badn", [128, 4], F32, ses)
        ah = [c.sb(f"f_ah{i}", [128, TPC], BF16, ses) for i in range(4)]
        al = [c.sb(f"f_al{i}", [128, TPC], BF16, ses) for i in range(4)]
        pb = c.pb

        @c.sec("sync")
        def _(e):
            c.wcomp(e)
            c.inc(e.dma_start(out=bb_t[:], in_=hb["bbase"].rearrange("(c p) -> p c", p=128), allow_slow_non_contiguous=True), "dL")
            c.inc(e.dma_start(out=bd1_t[:], in_=hb["bd1"].rearrange("(c p) -> p c", p=128), allow_slow_non_contiguous=True), "dL")
            c.inc(e.dma_start(out=bd2_t[:], in_=hb["bd2"].rearrange("(c p) -> p c", p=128), allow_slow_non_contiguous=True), "dL")
            c.inc(e.dma_start(out=badn_t[:], in_=hb["badn"].rearrange("(c p) -> p c", p=128), allow_slow_non_contiguous=True), "dL")
            c.inc(e.dma_start(out=bu_t[:], in_=hb["bup"]), "dL")

        @c.sec("vector")
        def _(v):
            c.wait(v, "dL")
            c.inc(v.memset(onesc_f[:], 1.0), "V")

        for n in range(4):
            j0 = n * 512

            @c.sec("sync")
            def _(e, j0=j0):
                c.wcomp(e)
                c.wait(e, "dG")
                c.inc(e.dma_start(out=abrow[:, 0:512], in_=abrows[0:1, j0:j0 + 512]), "dL")
                c.inc(e.dma_start(out=abrow[:, 512:1024], in_=abrows[1:2, j0:j0 + 512]), "dL")

            @c.sec("vector")
            def _(v):
                c.wait(v, "dL")
                c.inc(v.engine_nop(), "V")

            c.rep(abrow, 0, abg, 0, 512)
            c.rep(abrow, 512, abg, 512, 512)

            for k in range(NCH):
                @c.sec("vector")
                def _(v, k=k, j0=j0):
                    c.wait(v, "T")
                    c.wait(v, "dG")
                    sl = slice(j0, j0 + 512)
                    v.tensor_tensor(xn32s[:], resid[k][:, sl], abg[:, 0:512], ALU.mult)
                    v.tensor_tensor(xn32s[:], xn32s[:], abg[:, 512:1024], ALU.add)
                    c.split(v, xn32s[:], xnh[k][:, sl], xnl[k][:, sl])
                    c.inc(v.engine_nop(), "V")

                @c.sec("gpsimd")
                def _(g, k=k, j0=j0):
                    c.wait(g, "V")
                    c.inc(g.dma_start(out=xn32d[k * 128:(k + 1) * 128, j0:j0 + 512], in_=xn32s[:]), "dG")

        if S2CUT <= 1:
            return
        def mm3(t, psum_ap, wh, wl, rh, rl, start, stop, inc=False):
            t.matmul(psum_ap, wh, rh, start=start, stop=False)
            t.matmul(psum_ap, wh, rl, start=False, stop=False)
            last = t.matmul(psum_ap, wl, rh, start=False, stop=stop)
            if inc:
                c.inc(last, "T")

        def load_slab(hname, c0, m, width=128, p=128, rows=D, r0=0):
            segs = rows // p

            @c.sec("sync")
            def _(e):
                c.wcomp(e)
                src2 = hb[hname][r0:r0 + rows, c0 + m * width:c0 + (m + 1) * width].rearrange(
                    "(s p) m -> p s m", p=p)
                dst = sl32[:, 0:segs * width].rearrange("p (s m) -> p s m", m=width)
                c.inc(e.dma_start(out=dst, in_=src2), "dL")

            @c.sec("vector")
            def _(v):
                c.wait(v, "dL")
                w = segs * width
                c.split(v, sl32[:, 0:w], sl_h[:, 0:w], sl_l[:, 0:w])
                c.inc(v.engine_nop(), "V")

        for m in range(NCH):
            load_slab("wbase", 0, m)
            for n in range(4):
                @c.sec("tensor")
                def _(t, m=m, n=n):
                    c.wait(t, "V")
                    for kk in range(NCH):
                        mm3(t, pb[0][:, 0:512], sl_h[:, kk * 128:(kk + 1) * 128],
                            sl_l[:, kk * 128:(kk + 1) * 128],
                            xnh[kk][:, n * 512:(n + 1) * 512], xnl[kk][:, n * 512:(n + 1) * 512],
                            kk == 0, kk == NCH - 1, inc=(kk == NCH - 1))

                @c.sec("vector")
                def _(v, m=m, n=n):
                    c.wait(v, "T")
                    sl = slice(n * 512, (n + 1) * 512)
                    c.inc(v.scalar_tensor_tensor(resid[m][:, sl], pb[0][:, 0:512], bb_t[:, m:m + 1],
                                                 resid[m][:, sl], ALU.add, ALU.add), "V")

        if S2CUT <= 2:
            return
        for e_ in range(8):
            for mc in range(2):
                load_slab("wd1", e_ * 256, mc)
                for n in range(4):
                    @c.sec("tensor")
                    def _(t, n=n):
                        c.wait(t, "V")
                        for kk in range(NCH):
                            mm3(t, pb[0][:, 0:512], sl_h[:, kk * 128:(kk + 1) * 128],
                                sl_l[:, kk * 128:(kk + 1) * 128],
                                xnh[kk][:, n * 512:(n + 1) * 512], xnl[kk][:, n * 512:(n + 1) * 512],
                                kk == 0, kk == NCH - 1, inc=(kk == NCH - 1))

                    @c.sec("scalar")
                    def _(s, e_=e_, mc=mc, n=n):
                        c.wait(s, "T")
                        c.inc(s.activation(h32[:], pb[0][:, 0:512], AF.Relu,
                                           bias=bd1_t[:, e_ * 2 + mc:e_ * 2 + mc + 1]), "SC")

                    @c.sec("vector")
                    def _(v, mc=mc, n=n):
                        c.wait(v, "SC")
                        sl = slice(n * 512, (n + 1) * 512)
                        c.split(v, h32[:], hh[mc][:, sl], hl[mc][:, sl])
                        c.inc(v.engine_nop(), "V")

            for m in range(NCH):
                load_slab("wd2", 0, m, rows=256, r0=e_ * 256)
                for n in range(4):
                    @c.sec("sync")
                    def _(e2, m=m, n=n):
                        c.wcomp(e2)
                        c.wait(e2, "dG")
                        c.inc(e2.dma_start(out=xn32s[:], in_=xn32d[m * 128:(m + 1) * 128, n * 512:(n + 1) * 512]), "dL")

                    @c.sec("tensor")
                    def _(t, m=m, n=n):
                        c.wait(t, "V")
                        for kk in range(2):
                            mm3(t, pb[0][:, 0:512], sl_h[:, kk * 128:(kk + 1) * 128],
                                sl_l[:, kk * 128:(kk + 1) * 128],
                                hh[kk][:, n * 512:(n + 1) * 512], hl[kk][:, n * 512:(n + 1) * 512],
                                kk == 0, kk == 1, inc=(kk == 1))

                    @c.sec("vector")
                    def _(v, m=m, n=n, e_=e_):
                        c.wait(v, "T")
                        c.wait(v, "dL")
                        c.inc(v.scalar_tensor_tensor(d32[:], pb[0][:, 0:512], bd2_t[:, e_ * 8 + m:e_ * 8 + m + 1],
                                                     xn32s[:], ALU.add, ALU.subtract), "V")

                    @c.sec("scalar")
                    def _(s, e_=e_, m=m, n=n):
                        c.wait(s, "V")
                        c.inc(s.activation(dump[:], d32[:], AF.Square,
                                           accum_out=lossb[:, e_ * 32 + m * 4 + n:e_ * 32 + m * 4 + n + 1]), "SC")

        if S2CUT <= 3:
            return
        @c.sec("vector")
        def _(v):
            c.wait(v, "SC")
            for e_ in range(8):
                v.reduce_sum(lossr[:, e_:e_ + 1], lossb[:, e_ * 32:(e_ + 1) * 32], AX)
            c.inc(v.engine_nop(), "V")

        @c.sec("tensor")
        def _(t):
            c.wait(t, "V")
            c.inc(t.matmul(pb[2][0:1, 0:8], onesc_f[:], lossr[:], start=True, stop=True), "T")

        @c.sec("vector")
        def _(v):
            c.wait(v, "T")
            c.inc(v.tensor_copy(loss8[:], pb[2][0:1, 0:8]), "V")

        @c.sec("gpsimd")
        def _(g):
            c.wait(g, "V")
            c.inc(g.dma_start(out=lb_in[:].unsqueeze(0), in_=loss8[0:1, :]), "dG")
            c.wait(g, "dG")
            grp = [[0, 1, 2, 3], [4, 5, 6, 7]]
            c.inc(g.collective_compute("AllReduce", ALU.add, replica_groups=grp,
                                       ins=[lb_in.ap().opt()], outs=[lb_out.ap().opt()]), "CC")
            c.wait(g, "CC")
            c.inc(g.dma_start(out=loss8[0:1, :], in_=lb_out[:].unsqueeze(0)), "dG")

        @c.sec("vector")
        def _(v):
            c.wait(v, "dG")
            c.inc(v.tensor_reduce(lmin[:], loss8[:], AX, ALU.min), "V")

        @c.sec("vector")
        def _(v):
            c.wait(v, "V")
            c.inc(v.tensor_scalar(g8[:], loss8[:], lmin[0:1, 0:1], None, ALU.is_equal), "V")

        @c.sec("tensor")
        def _(t):
            c.wait(t, "V")
            c.inc(t.matmul(pb[1][:, 0:8], c.onesr[:], g8[:], start=True, stop=True), "T")

        @c.sec("vector")
        def _(v):
            c.wait(v, "T")
            c.inc(v.tensor_copy(GG[:], pb[1][:, 0:8]), "V")

        @c.sec("vector")
        def _(v):
            c.wait(v, "V")
            for cc in range(4):
                v.tensor_copy(G2[0:64, cc:cc + 1], GG[0:64, 2 * cc:2 * cc + 1])
                v.tensor_copy(G2[64:128, cc:cc + 1], GG[64:128, 2 * cc + 1:2 * cc + 2])
            c.inc(v.engine_nop(), "V")

        for m in range(NCH):
            @c.sec("vector")
            def _(v, m=m):
                c.wait(v, "V")
                c.inc(v.tensor_tensor(lossr[:], bu_t[:, m * 8:(m + 1) * 8], GG[:], ALU.mult), "V")

            @c.sec("vector")
            def _(v, m=m):
                c.wait(v, "V")
                c.inc(v.reduce_sum(bvec[:, m:m + 1], lossr[:], AX), "V")

        if DEBUG:
            @c.sec("gpsimd")
            def _(g):
                c.wcomp(g)
                c.inc(g.dma_start(out=c.dbg["dbg_loss"], in_=loss8[:]), "dO")
                c.inc(g.dma_start(out=c.dbg["dbg_g8"], in_=g8[:]), "dO")
                c.inc(g.dma_start(out=c.dbg["dbg_GG"], in_=GG[:]), "dO")
                c.inc(g.dma_start(out=c.dbg["dbg_G2"], in_=G2[:]), "dO")
                c.inc(g.dma_start(out=c.dbg["dbg_bvec"], in_=bvec[:]), "dO")
                c.inc(g.dma_start(out=c.dbg["dbg_lossr"], in_=lossr[:]), "dO")

        if S2CUT <= 4:
            return
        for m in range(4):
            load_slab("adn", 0, m)
            for n in range(4):
                @c.sec("tensor")
                def _(t, m=m, n=n):
                    c.wait(t, "V")
                    for kk in range(NCH):
                        mm3(t, pb[0][:, 0:512], sl_h[:, kk * 128:(kk + 1) * 128],
                            sl_l[:, kk * 128:(kk + 1) * 128],
                            xnh[kk][:, n * 512:(n + 1) * 512], xnl[kk][:, n * 512:(n + 1) * 512],
                            kk == 0, kk == NCH - 1, inc=(kk == NCH - 1))

                @c.sec("scalar")
                def _(s, m=m):
                    c.wait(s, "T")
                    c.inc(s.activation(h32[:], pb[0][:, 0:512], AF.Relu, bias=badn_t[:, m:m + 1]), "SC")

                @c.sec("vector")
                def _(v, m=m, n=n):
                    c.wait(v, "SC")
                    v.tensor_scalar_mul(h32[:], h32[:], G2[:, m:m + 1])
                    sl = slice(n * 512, (n + 1) * 512)
                    c.split(v, h32[:], ah[m][:, sl], al[m][:, sl])
                    c.inc(v.engine_nop(), "V")

        if DEBUG:
            @c.sec("gpsimd")
            def _(g):
                c.wcomp(g)
                c.inc(g.dma_start(out=c.dbg["dbg_a0"], in_=ah[0][:, 0:512]), "dO")
                c.inc(g.dma_start(out=c.dbg["dbg_h0"], in_=ah[1][:, 0:512]), "dO")
                c.inc(g.dma_start(out=c.dbg["dbg_E"][:, 0:512], in_=ah[2][:, 0:512]), "dO")
                c.inc(g.dma_start(out=c.dbg["dbg_E"][:, 512:1024], in_=ah[3][:, 0:512]), "dO")
                c.inc(g.dma_start(out=c.dbg["dbg_lossr"][:, 0:8].broadcast_to([128, 8]) if False else c.dbg["dbg_E"][:, 1024:1536], in_=al[0][:, 0:512]), "dO")

        for m in range(NCH):
            load_slab("aup", 0, m, rows=512)
            for n in range(4):
                @c.sec("tensor")
                def _(t, m=m, n=n):
                    c.wait(t, "V")
                    for kk in range(4):
                        mm3(t, pb[0][:, 0:512], sl_h[:, kk * 128:(kk + 1) * 128],
                            sl_l[:, kk * 128:(kk + 1) * 128],
                            ah[kk][:, n * 512:(n + 1) * 512], al[kk][:, n * 512:(n + 1) * 512],
                            kk == 0, kk == 3, inc=(kk == 3))

                @c.sec("vector")
                def _(v, m=m, n=n):
                    c.wait(v, "T")
                    sl = slice(n * 512, (n + 1) * 512)
                    c.inc(v.scalar_tensor_tensor(resid[m][:, sl], pb[0][:, 0:512], bvec[:, m:m + 1],
                                                 resid[m][:, sl], ALU.add, ALU.add), "V")


    def rms_stats(self, srcfn, dram_rows, eps):
        c = self
        ps_sq = c.pb[1]
        with ExitStack() as les:
            sq32 = c.sb("r_sq32", [128, 512], F32, les)
            sqh = c.sb("r_sqh", [128, 512], BF16, les)
            sql = c.sb("r_sql", [128, 512], BF16, les)
            arow = c.sb("r_arow", [1, TPC], F32, les)
            epsv = c.sb("r_eps", [1, 1], F32, les)

            @c.sec("vector")
            def _(v):
                c.wcomp(v)
                c.inc(v.memset(epsv[:], eps), "V")

            for n in range(4):
                j0, w = n * 512, 512
                for k in range(NCH):
                    @c.sec("scalar")
                    def _(s, k=k, j0=j0, w=w):
                        c.wait(s, "V")
                        c.wait(s, "T")
                        c.inc(s.activation(sq32[:, 0:w], srcfn(k, j0, w), AF.Square), "SC")

                    @c.sec("vector")
                    def _(v, w=w):
                        c.wait(v, "SC")
                        c.split(v, sq32[:, 0:w], sqh[:, 0:w], sql[:, 0:w])
                        c.inc(v.engine_nop(), "V")

                    @c.sec("tensor")
                    def _(t, k=k, w=w):
                        c.wait(t, "V")
                        t.matmul(ps_sq[0:1, 0:w], c.onesc_b[:], sqh[:, 0:w], start=(k == 0), stop=False)
                        c.inc(t.matmul(ps_sq[0:1, 0:w], c.onesc_b[:], sql[:, 0:w],
                                       start=False, stop=(k == NCH - 1)), "T")

                @c.sec("vector")
                def _(v, j0=j0, w=w):
                    c.wait(v, "T")
                    c.inc(v.tensor_scalar_mul(arow[:, j0:j0 + w], ps_sq[0:1, 0:w], 1.0 / D), "V")

            if DEBUG:
                @c.sec("gpsimd")
                def _(g):
                    c.wcomp(g)
                    c.inc(g.dma_start(out=c.dbg["dbg_h0"][1:2, 0:8], in_=arow[0:1, 0:8]), "dO")

                @c.sec("vector")
                def _(v):
                    c.wait(v, "dO")
                    c.inc(v.engine_nop(), "V")

            @c.sec("scalar")
            def _(s):
                c.wait(s, "V")
                c.inc(s.activation(arow[:], arow[:], AF.Ln, bias=epsv[0:1, 0:1]), "SC")

            @c.sec("scalar")
            def _(s):
                c.wait(s, "SC")
                c.inc(s.activation(arow[:], arow[:], AF.Exp, scale=-0.5), "SC")

            if DEBUG:
                @c.sec("gpsimd")
                def _(g):
                    c.wcomp(g)
                    c.inc(g.dma_start(out=c.dbg["dbg_h0"][2:3, 0:8], in_=arow[0:1, 0:8]), "dO")

                @c.sec("vector")
                def _(v):
                    c.wait(v, "dO")
                    c.inc(v.engine_nop(), "V")

            @c.sec("gpsimd")
            def _(g):
                c.wait(g, "SC")
                c.inc(g.dma_start(out=dram_rows[0:1, 0:TPC], in_=arow[:]), "dG")

            if DEBUG:
                @c.sec("sync")
                def _(e):
                    c.wait(e, "dG")
                    c.inc(e.dma_start(out=sq32[0:1, 0:16], in_=dram_rows[0:1, 0:16]), "dL")

                @c.sec("gpsimd")
                def _(g):
                    c.wait(g, "dL")
                    c.inc(g.dma_start(out=c.dbg["dbg_h0"][3:4, 0:16], in_=sq32[0:1, 0:16]), "dO")

                @c.sec("vector")
                def _(v):
                    c.wait(v, "dO")
                    c.inc(v.engine_nop(), "V")

            @c.sec("vector")
            def _(v):
                c.wait(v, "dG")
                c.inc(v.engine_nop(), "V")

    def s3_l1attn(self, ses):
        c = self
        hb, resid = self.hb, self.resid
        nc = self.nc
        pb = c.pb

        arows1 = nc.dram_tensor("s3_arows1", [2, TPC], F32).ap()
        abrows3 = nc.dram_tensor("s3_abrows3", [2, TPC], F32).ap()
        x2_d = nc.dram_tensor("s3_x2d", [D, TPC], F32).ap()
        arows2 = nc.dram_tensor("s3_arows2", [2, TPC], F32).ap()
        cd_d = nc.dram_tensor("s3_cd", [D, TPC], F32).ap()
        cs_d = nc.dram_tensor("s3_cs", [D, TPC], F32).ap()
        rech_d = nc.dram_tensor("s3_rech", [D, TPC], BF16).ap()
        recl_d = nc.dram_tensor("s3_recl", [D, TPC], BF16).ap()
        self.dbg_rech, self.dbg_recl = rech_d, recl_d
        hg_in = nc.dram_tensor("s3_hgin", [3 * D + 128], F32)
        hg_out = nc.dram_tensor("s3_hgout", [8, 3 * D + 128], F32)
        ag_in = nc.dram_tensor("s3_agin", [2 * D], F32)
        ag_out = nc.dram_tensor("s3_agout", [8, 2 * D], F32)
        sel_d = nc.dram_tensor("s3_seld", [3, 2 * D], F32)

        pnw_t = c.sb("s3_pnw", [128, 8], F32, ses)
        fnw_t = c.sb("s3_fnw", [128, 8], F32, ses)
        convw_t = c.sb("s3_convw", [128, 32], F32, ses)
        convb_t = c.sb("s3_convb", [128, 8], F32, ses)
        dbias_t = c.sb("s3_dbias", [128, 8], F32, ses)
        prevhot = c.sb("s3_prevhot", [8, 1], F32, ses)
        predselT = c.sb("s3_predselT", [8, 3], F32, ses)
        xh3 = [c.sb(f"s3_xh3_{k}", [128, 3], F32, ses) for k in range(NCH)]
        Lsc = c.sb("s3_Lsc", [128, 8 * 32], F32, ses)
        Cpr = c.sb("s3_Cpr", [128, 8 * 32], F32, ses)
        Sloc = c.sb("s3_Sloc", [128, 8], F32, ses)
        Ptot = c.sb("s3_Ptot", [128, 8], F32, ses)
        seff0 = c.sb("s3_seff0", [128, 8], F32, ses)
        self.dbg_Lsc, self.dbg_Cpr, self.dbg_Sloc, self.dbg_Ptot, self.dbg_seff0 = Lsc, Cpr, Sloc, Ptot, seff0
        sl32 = c.sb("s3_sl32", [128, 1024], F32, ses)
        sl_h = c.sb("s3_slh", [128, 1024], BF16, ses)
        sl_l = c.sb("s3_sll", [128, 1024], BF16, ses)

        @c.sec("sync")
        def _(e):
            c.wcomp(e)
            for nm, dst in [("pnw", pnw_t), ("fnw", fnw_t), ("convb", convb_t), ("dbias", dbias_t)]:
                c.inc(e.dma_start(out=dst[:], in_=hb[nm].rearrange("(c p) -> p c", p=128),
                                  allow_slow_non_contiguous=True), "dL")
            c.inc(e.dma_start(out=convw_t[:].rearrange("p (c k) -> p c k", k=4),
                              in_=hb["convw"].rearrange("(c p) k -> p c k", p=128),
                              allow_slow_non_contiguous=True), "dL")
            c.inc(e.dma_start(out=prevhot[:], in_=hb["prevhot"]), "dL")
            c.inc(e.dma_start(out=predselT[:], in_=hb["predselT"]), "dL")

        @c.sec("vector")
        def _(v):
            c.wait(v, "dL")
            c.inc(v.engine_nop(), "V")

        base_relax = self.relaxed
        self.relaxed = base_relax and "a" in RELAX3 if RELAX3 else base_relax
        def srcfn(k, j0, w):
            return resid[k][:, j0:j0 + w]
        c.ln_stats(TPC, srcfn, [(i * 512, 512) for i in range(4)], abrows3)

        with ExitStack() as esl:
            abg3 = c.sb("s3_abg3", [128, 1024], F32, esl)
            abrow3 = c.sb("s3_abrow3", [1, 1024], F32, esl)
            for n in range(4):
                j0 = n * 512

                @c.sec("sync")
                def _(e, j0=j0):
                    c.wcomp(e)
                    c.wait(e, "dG")
                    c.inc(e.dma_start(out=abrow3[:, 0:512], in_=abrows3[0:1, j0:j0 + 512]), "dL")
                    c.inc(e.dma_start(out=abrow3[:, 512:1024], in_=abrows3[1:2, j0:j0 + 512]), "dL")

                @c.sec("vector")
                def _(v):
                    c.wait(v, "dL")
                    c.inc(v.engine_nop(), "V")

                c.rep(abrow3, 0, abg3, 0, 512)
                c.rep(abrow3, 512, abg3, 512, 512)

                for k in range(NCH):
                    @c.sec("gpsimd")
                    def _(g, k=k, j0=j0):
                        c.wait(g, "V")
                        c.wait(g, "T")
                        c.inc(g.dma_start(out=x2_d[k * 128:(k + 1) * 128, j0:j0 + 512],
                                          in_=resid[k][:, j0:j0 + 512]), "dG")

                    @c.sec("vector")
                    def _(v, k=k, j0=j0):
                        c.wait(v, "dG")
                        sl = slice(j0, j0 + 512)
                        v.tensor_tensor(resid[k][:, sl], resid[k][:, sl], abg3[:, 0:512], ALU.mult)
                        c.inc(v.tensor_tensor(resid[k][:, sl], resid[k][:, sl], abg3[:, 512:1024], ALU.add), "V")

        self.relaxed = base_relax and ("c" in RELAX3) if RELAX3 else base_relax
        c.rms_stats(srcfn, arows1, 1e-6)

        hr3 = c.sb("s3_hr3", [1, 3], F32, ses)

        @c.sec("sync")
        def _(e):
            c.wait(e, "dG")
            c.inc(e.dma_start(out=hr3[:], in_=arows1[0:1, TPC - 3:TPC]), "dL")

        @c.sec("vector")
        def _(v):
            c.wait(v, "dL")
            c.inc(v.engine_nop(), "V")

        self.relaxed = base_relax and ("b" in RELAX3) if RELAX3 else base_relax
        @c.sec("gpsimd")
        def _(g):
            c.wait(g, "dL")
            c.wait(g, "V")
            for k in range(NCH):
                c.inc(g.dma_start(out=hg_in[k * 384:(k + 1) * 384].rearrange("(t p) -> p t", p=128),
                                  in_=resid[k][:, TPC - 3:TPC], allow_slow_non_contiguous=True), "dG")
            c.inc(g.dma_start(out=hg_in[3 * D:3 * D + 3].unsqueeze(0), in_=hr3[:]), "dG")
            c.wait(g, "dG")
            c.inc(g.collective_compute("AllGather", ALU.bypass,
                                       replica_groups=[[0, 1, 2, 3, 4, 5, 6, 7]],
                                       ins=[hg_in.ap().opt()], outs=[hg_out.ap().opt()]), "CC")

        hgall = c.sb("s3_hgall", [8, 384], F32, ses)
        hgrow = c.sb("s3_hgrow", [1, 384], F32, ses)
        dbgt = c.sb("s3_dbgt", [128, 64], F32, ses)
        self.dbgt_ref = dbgt
        self.dbg_hgrow = c.sb("s3_dbgrow", [1, 128], BF16, ses)
        for k in range(NCH):
            @c.sec("sync")
            def _(e, k=k):
                c.wait(e, "CC")
                c.wait(e, "dG")
                c.wait(e, "T")
                c.inc(e.dma_start(out=hgall[:], in_=hg_out[:, k * 384:(k + 1) * 384]), "dL")

            @c.sec("tensor")
            def _(t):
                c.wait(t, "dL")
                c.wait(t, "V")
                c.inc(t.matmul(pb[2][0:1, 0:384], prevhot[:], hgall[:], start=True, stop=True), "T")

            @c.sec("vector")
            def _(v):
                c.wait(v, "T")
                c.inc(v.tensor_copy(hgrow[:], pb[2][0:1, 0:384]), "V")

            @c.sec("gpsimd")
            def _(g):
                c.wait(g, "V")
                c.inc(g.dma_start(out=hg_in[0:384].unsqueeze(0), in_=hgrow[:]), "dG")

            @c.sec("sync")
            def _(e, k=k):
                c.wait(e, "dG")
                c.inc(e.dma_start(out=xh3[k][:], in_=hg_in[0:384].rearrange("(t p) -> p t", p=128),
                                  allow_slow_non_contiguous=True), "dL")

            @c.sec("vector")
            def _(v):
                c.wait(v, "dL")
                c.inc(v.engine_nop(), "V")

        if DEBUG:
            @c.sec("gpsimd")
            def _(g):
                c.wcomp(g)
                c.wait(g, "dL")
                for k in range(NCH):
                    c.inc(g.dma_start(out=c.dbg["dbg_h0"][k * 16:k * 16 + 3, 0:128],
                                      in_=xh3[k][:].transpose([1, 0]) if False else c.dbg and xh3[k][:]), "dO") if False else None
                c.inc(g.dma_start(out=c.dbg["dbg_h0"][0:128, 0:3], in_=xh3[0][:]), "dO")
                c.inc(g.dma_start(out=c.dbg["dbg_h0"][0:8, 16:400], in_=hgall[:]), "dO")

        def load_slab3(hname, m):
            @c.sec("sync")
            def _(e, hname=hname, m=m):
                c.wcomp(e)
                src2 = hb[hname][:, m * 128:(m + 1) * 128].rearrange("(s p) m -> p s m", p=128)
                c.inc(e.dma_start(out=sl32[:].rearrange("p (s m) -> p s m", m=128), in_=src2), "dL")

            @c.sec("vector")
            def _(v):
                c.wait(v, "dL")
                c.split(v, sl32[:], sl_h[:], sl_l[:])
                c.inc(v.engine_nop(), "V")

        def mm3(t, psum_ap, wh, wl, rh, rl, start, stop, inc=False):
            t.matmul(psum_ap, wh, rh, start=start, stop=False)
            t.matmul(psum_ap, wh, rl, start=False, stop=False)
            last = t.matmul(psum_ap, wl, rh, start=False, stop=stop)
            if inc:
                c.inc(last, "T")

        with ExitStack() as esa:
            hch = [c.sb(f"s3_hch{k}", [128, TPC], BF16, esa) for k in range(NCH)]
            hcl = [c.sb(f"s3_hcl{k}", [128, TPC], BF16, esa) for k in range(NCH)]
            es_conv = ExitStack()
            arep = c.sb("s3_arep", [128, TPC], F32, es_conv)
            arow_g = c.sb("s3_arowg", [1, TPC], F32, es_conv)
            r3rep = c.sb("s3_r3rep", [128, 3], F32, es_conv)
            hnp = c.sb("s3_hnp", [128, TPC + 3], F32, es_conv)
            hc32 = c.sb("s3_hc32", [128, TPC], F32, es_conv)
            rsh3 = c.sb("s3_rsh3", [1, 3], F32, es_conv)
            hgr3 = c.sb("s3_hgr3", [8, 3], F32, es_conv)

            @c.sec("sync")
            def _(e):
                c.wait(e, "CC")
                c.wait(e, "dG")
                c.inc(e.dma_start(out=hgr3[:], in_=hg_out[:, 3 * D:3 * D + 3]), "dL")

            @c.sec("tensor")
            def _(t):
                c.wait(t, "dL")
                c.wait(t, "V")
                c.inc(t.matmul(pb[1][0:1, 0:3], prevhot[:], hgr3[:], start=True, stop=True), "T")

            @c.sec("vector")
            def _(v):
                c.wait(v, "T")
                c.inc(v.tensor_copy(rsh3[:], pb[1][0:1, 0:3]), "V")

            @c.sec("sync")
            def _(e):
                c.wait(e, "dG")
                c.inc(e.dma_start(out=arow_g[:], in_=arows1[0:1, :]), "dL")
                if DEBUG:
                    c.inc(e.dma_start(out=hgrow[:, 0:16], in_=arows1[0:1, 0:16]), "dL")

            if DEBUG:
                @c.sec("gpsimd")
                def _(g):
                    c.wait(g, "dL")
                    c.inc(g.dma_start(out=c.dbg["dbg_h0"][4:5, 0:16], in_=hgrow[0:1, 0:16]), "dO")
                    c.inc(g.dma_start(out=c.dbg["dbg_h0"][5:6, 0:16], in_=arow_g[0:1, 0:16]), "dO")

                @c.sec("vector")
                def _(v):
                    c.wait(v, "dO")
                    c.inc(v.engine_nop(), "V")

            @c.sec("vector")
            def _(v):
                c.wait(v, "dL")
                c.inc(v.engine_nop(), "V")

            for n in range(4):
                c.rep(arow_g, n * 512, arep, n * 512, 512)

            @c.sec("tensor")
            def _(t):
                c.wait(t, "SC")
                c.wait(t, "V")
                c.inc(t.matmul(pb[2][:, 0:3], c.onesr[:], rsh3[:], start=True, stop=True), "T")

            @c.sec("vector")
            def _(v):
                c.wait(v, "T")
                c.inc(v.tensor_copy(r3rep[:], pb[2][:, 0:3]), "V")

            if DEBUG:
                @c.sec("gpsimd")
                def _(g):
                    c.wcomp(g)
                    c.inc(g.dma_start(out=c.dbg["dbg_loss"], in_=arow_g[0:1, 0:8]), "dO")
                    c.inc(g.dma_start(out=c.dbg["dbg_g8"][0:1, 0:3], in_=rsh3[0:1, 0:3]), "dO")
                    c.inc(g.dma_start(out=c.dbg["dbg_GG"], in_=arep[:, 0:8]), "dO")

                @c.sec("vector")
                def _(v):
                    c.wait(v, "dO")
                    c.inc(v.engine_nop(), "V")

            for k in range(NCH):
                @c.sec("vector")
                def _(v, k=k):
                    c.wait(v, "V")
                    v.scalar_tensor_tensor(hnp[:, 0:3], xh3[k][:], pnw_t[:, k:k + 1], r3rep[:],
                                           ALU.mult, ALU.mult)
                    v.scalar_tensor_tensor(hnp[:, 3:TPC + 3], resid[k][:, :], pnw_t[:, k:k + 1], arep[:],
                                           ALU.mult, ALU.mult)
                    if DEBUG and k == 0:
                        v.tensor_copy(dbgt[:, 16:19], hnp[:, 0:3])
                        v.tensor_copy(dbgt[:, 20:26], hnp[:, 3:9])
                        v.tensor_copy(dbgt[:, 26:29], r3rep[:])
                        v.tensor_copy(dbgt[:, 30:36], arep[:, 0:6])
                    c.inc(v.engine_nop(), "V")

                @c.sec("vector")
                def _(v, k=k):
                    c.wait(v, "V")
                    v.tensor_scalar(hc32[:], hnp[:, 0:TPC], convw_t[:, 4 * k:4 * k + 1],
                                    convb_t[:, k:k + 1], ALU.mult, ALU.add)
                    c.inc(v.engine_nop(), "V")

                for kk in range(1, 4):
                    @c.sec("vector")
                    def _(v, k=k, kk=kk):
                        c.wait(v, "V")
                        c.inc(v.scalar_tensor_tensor(hc32[:], hnp[:, kk:kk + TPC],
                                                     convw_t[:, 4 * k + kk:4 * k + kk + 1],
                                                     hc32[:], ALU.mult, ALU.add), "V")

                @c.sec("vector")
                def _(v, k=k):
                    c.wait(v, "V")
                    c.split(v, hc32[:], hch[k][:], hcl[k][:])
                    c.inc(v.engine_nop(), "V")

            es_conv.close()
            gate_t = c.sb("s3_gate", [128, TPC], F32, esa)
            ones32 = c.sb("s3_ones32", [128, 32], F32, esa)
            a32 = c.sb("s3_a32", [128, TPC], F32, esa)
            i32 = c.sb("s3_i32", [128, TPC], F32, esa)
            v32_ = c.sb("s3_v32", [128, TPC], F32, esa)
            t32a = c.sb("s3_t32a", [128, TPC], F32, esa)
            t32b = c.sb("s3_t32b", [128, TPC], F32, esa)
            Dc = c.sb("s3_Dc", [128, 32], F32, esa)
            il = c.sb("s3_il", [128, 32], F32, esa)
            rh16 = c.sb("s3_rh16", [128, TPC], BF16, esa)
            rl16 = c.sb("s3_rl16", [128, TPC], BF16, esa)
            Ssel = c.sb("s3_Ssel", [3, 512], F32, esa)
            Sk = [c.sb(f"s3_Sk{i}", [128, 8], F32, esa) for i in range(3)]
            Pk = [c.sb(f"s3_Pk{i}", [128, 8], F32, esa) for i in range(3)]
            seffm = c.sb("s3_seffm", [128, 32], F32, esa)

            @c.sec("vector")
            def _(v):
                c.wcomp(v)
                v.memset(gate_t[:], 1.0)
                c.inc(v.memset(ones32[:], 1.0), "V")

            @c.sec("vector")
            def _(v):
                c.wait(v, "V")
                c.inc(v.memset(gate_t[:].rearrange("p (c k) -> p c k", k=64)[:, :, 0:1], 0.0), "V")

            if S3CUT <= 1:
                return
            self.relaxed = base_relax and ("d" in RELAX3) if RELAX3 else base_relax
            for m in range(NCH):
                for wname, act in [("waT", "sig_b"), ("wiT", "sig"), ("wvT", "copy")]:
                    load_slab3(wname, m)
                    for n in range(4):
                        @c.sec("tensor")
                        def _(t, n=n):
                            c.wait(t, "V")
                            c.wait(t, "SC")
                            for kk in range(NCH):
                                mm3(t, pb[0][:, 0:512], sl_h[:, kk * 128:(kk + 1) * 128],
                                    sl_l[:, kk * 128:(kk + 1) * 128],
                                    hch[kk][:, n * 512:(n + 1) * 512], hcl[kk][:, n * 512:(n + 1) * 512],
                                    kk == 0, kk == NCH - 1, inc=(kk == NCH - 1))

                        if act == "copy":
                            @c.sec("vector")
                            def _(v, n=n):
                                c.wait(v, "T")
                                c.wait(v, "dG")
                                c.inc(v.tensor_copy(v32_[:, n * 512:(n + 1) * 512], pb[0][:, 0:512]), "V")
                        else:
                            @c.sec("scalar")
                            def _(s, n=n, act=act, m=m):
                                c.wait(s, "T")
                                c.wait(s, "dG")
                                dst = a32 if act == "sig_b" else i32
                                if act == "sig_b":
                                    c.inc(s.activation(dst[:, n * 512:(n + 1) * 512], pb[0][:, 0:512],
                                                       AF.Sigmoid, bias=dbias_t[:, m:m + 1]), "SC")
                                else:
                                    c.inc(s.activation(dst[:, n * 512:(n + 1) * 512], pb[0][:, 0:512],
                                                       AF.Sigmoid), "SC")

                @c.sec("vector")
                def _(v):
                    c.wait(v, "SC")
                    c.wait(v, "T")
                    c.inc(v.tensor_tensor(i32[:], i32[:], v32_[:], ALU.mult), "V")

                @c.sec("vector")
                def _(v):
                    c.wait(v, "V")
                    v.tensor_tensor(t32a[:], a32[:], a32[:], ALU.mult)
                    v.tensor_scalar(t32a[:], t32a[:], -1.0, 1.0, ALU.mult, ALU.add)
                    v.tensor_scalar_max(t32a[:], t32a[:], 1e-8)
                    c.inc(v.engine_nop(), "V")

                @c.sec("scalar")
                def _(s):
                    c.wait(s, "V")
                    c.inc(s.activation(t32a[:], t32a[:], AF.Sqrt), "SC")

                @c.sec("vector")
                def _(v):
                    c.wait(v, "SC")
                    v.tensor_tensor(i32[:], i32[:], t32a[:], ALU.mult)
                    v.tensor_scalar_max(t32b[:], a32[:], 1e-10)
                    c.inc(v.engine_nop(), "V")

                @c.sec("scalar")
                def _(s):
                    c.wait(s, "V")
                    c.inc(s.activation(t32b[:], t32b[:], AF.Ln), "SC")

                @c.sec("vector")
                def _(v):
                    c.wait(v, "SC")
                    c.inc(v.tensor_tensor_scan(t32a[:], gate_t[:], t32b[:], 0.0, ALU.mult, ALU.add), "V")

                @c.sec("vector")
                def _(v):
                    c.wait(v, "V")
                    v.tensor_scalar(t32b[:], t32a[:], -1.0, 30.0, ALU.mult, ALU.min)
                    c.inc(v.engine_nop(), "V")

                @c.sec("scalar")
                def _(s):
                    c.wait(s, "V")
                    s.activation(a32[:], t32a[:], AF.Exp)
                    c.inc(s.activation(t32b[:], t32b[:], AF.Exp), "SC")

                @c.sec("vector")
                def _(v):
                    c.wait(v, "SC")
                    v.tensor_scalar_min(t32b[:], t32b[:], 1e10)
                    v.tensor_tensor(i32[:], i32[:], t32b[:], ALU.mult)
                    c.inc(v.engine_nop(), "V")

                @c.sec("vector")
                def _(v):
                    c.wait(v, "V")
                    c.inc(v.tensor_tensor_scan(v32_[:], gate_t[:], i32[:], 0.0, ALU.mult, ALU.add), "V")

                if DEBUG and m == 0:
                    @c.sec("gpsimd")
                    def _(g):
                        c.wait(g, "V")
                        c.wait(g, "SC")
                        c.inc(g.dma_start(out=c.dbg["dbg_q"][0:128, :], in_=t32a[:, 0:512]), "dO")
                        c.inc(g.dma_start(out=c.dbg["dbg_q"][128:256, :], in_=i32[:, 0:512]), "dO")
                        c.inc(g.dma_start(out=c.dbg["dbg_q"][256:384, :], in_=v32_[:, 0:512]), "dO")
                        c.inc(g.dma_start(out=c.dbg["dbg_q"][384:512, :], in_=a32[:, 0:512]), "dO")
                        c.inc(g.dma_start(out=c.dbg["dbg_q"][512:640, :], in_=gate_t[:, 0:512]), "dO")

                    @c.sec("vector")
                    def _(v):
                        c.wait(v, "dO")
                        c.inc(v.engine_nop(), "V")

                @c.sec("vector")
                def _(v, m=m):
                    c.wait(v, "V")
                    c.wait(v, "dO")
                    cdv = a32[:].rearrange("p (c k) -> p c k", k=64)
                    csv = v32_[:].rearrange("p (c k) -> p c k", k=64)
                    v.tensor_copy(Dc[:], cdv[:, :, 63])
                    c.inc(v.tensor_tensor(il[:], cdv[:, :, 63], csv[:, :, 63], ALU.mult), "V")

                @c.sec("vector")
                def _(v, m=m):
                    c.wait(v, "V")
                    v.tensor_tensor_scan(Lsc[:, m * 32:(m + 1) * 32], Dc[:], il[:], 0.0, ALU.mult, ALU.add)
                    c.inc(v.tensor_tensor_scan(Cpr[:, m * 32:(m + 1) * 32], Dc[:], ones32[:], 1.0,
                                               ALU.mult, ALU.mult), "V")

                @c.sec("vector")
                def _(v, m=m):
                    c.wait(v, "V")
                    v.tensor_copy(Sloc[:, m:m + 1], Lsc[:, m * 32 + 31:m * 32 + 32])
                    c.inc(v.tensor_copy(Ptot[:, m:m + 1], Cpr[:, m * 32 + 31:m * 32 + 32]), "V")

                @c.sec("gpsimd")
                def _(g, m=m):
                    c.wait(g, "V")
                    c.inc(g.dma_start(out=cd_d[m * 128:(m + 1) * 128, :], in_=a32[:]), "dG")
                    c.inc(g.dma_start(out=cs_d[m * 128:(m + 1) * 128, :], in_=v32_[:]), "dG")

            self.relaxed = base_relax and ("e" in RELAX3) if RELAX3 else base_relax
            @c.sec("gpsimd")
            def _(g):
                c.wait(g, "V")
                c.wait(g, "dG")
                c.inc(g.dma_start(out=ag_in[0:1024].rearrange("(m p) -> p m", p=128), in_=Sloc[:],
                                  allow_slow_non_contiguous=True), "dG")
                c.inc(g.dma_start(out=ag_in[1024:2048].rearrange("(m p) -> p m", p=128), in_=Ptot[:],
                                  allow_slow_non_contiguous=True), "dG")
                c.wait(g, "dG")
                c.inc(g.collective_compute("AllGather", ALU.bypass,
                                           replica_groups=[[0, 1, 2, 3, 4, 5, 6, 7]],
                                           ins=[ag_in.ap().opt()], outs=[ag_out.ap().opt()]), "CC")

            agall = c.sb("s3_agall", [8, 512], F32, esa)
            for j in range(4):
                @c.sec("sync")
                def _(e, j=j):
                    c.wait(e, "CC")
                    c.wait(e, "T")
                    c.inc(e.dma_start(out=agall[:], in_=ag_out[:, j * 512:(j + 1) * 512]), "dL")

                @c.sec("tensor")
                def _(t, j=j):
                    c.wait(t, "dL")
                    c.wait(t, "V")
                    c.inc(t.matmul(pb[2][0:3, 0:512], predselT[:], agall[:], start=True, stop=True), "T")

                @c.sec("vector")
                def _(v, j=j):
                    c.wait(v, "T")
                    c.inc(v.tensor_copy(Ssel[:], pb[2][0:3, 0:512]), "V")

                @c.sec("gpsimd")
                def _(g, j=j):
                    c.wait(g, "V")
                    c.inc(g.dma_start(out=sel_d[:, j * 512:(j + 1) * 512], in_=Ssel[:]), "dG")

            @c.sec("sync")
            def _(e):
                c.wait(e, "dG")
                for i in range(3):
                    c.inc(e.dma_start(out=Sk[i][:], in_=sel_d[i, 0:1024].rearrange("(m p) -> p m", p=128),
                                      allow_slow_non_contiguous=True), "dL")
                    c.inc(e.dma_start(out=Pk[i][:], in_=sel_d[i, 1024:2048].rearrange("(m p) -> p m", p=128),
                                      allow_slow_non_contiguous=True), "dL")

            stmp = c.sb("s3_stmp", [128, 8], F32, esa)

            @c.sec("vector")
            def _(v):
                c.wait(v, "dL")
                c.inc(v.tensor_tensor(stmp[:], Pk[1][:], Sk[2][:], ALU.mult), "V")

            @c.sec("vector")
            def _(v):
                c.wait(v, "V")
                c.inc(v.tensor_tensor(seff0[:], Sk[1][:], stmp[:], ALU.add), "V")

            @c.sec("vector")
            def _(v):
                c.wait(v, "V")
                c.inc(v.tensor_tensor(stmp[:], Pk[0][:], seff0[:], ALU.mult), "V")

            @c.sec("vector")
            def _(v):
                c.wait(v, "V")
                c.inc(v.tensor_tensor(seff0[:], Sk[0][:], stmp[:], ALU.add), "V")

            if DEBUG:
                @c.sec("gpsimd")
                def _(g):
                    c.wcomp(g)
                    c.inc(g.dma_start(out=c.dbg["dbg_GG"], in_=seff0[:]), "dO")
                    c.inc(g.dma_start(out=c.dbg["dbg_bvec"], in_=Sk[0][:]), "dO")
                    c.inc(g.dma_start(out=c.dbg["dbg_lossr"], in_=Pk[0][:]), "dO")
                    c.inc(g.dma_start(out=c.dbg["dbg_G2"][:, 0:4], in_=Sloc[:, 0:4]), "dO")
                    c.inc(g.dma_start(out=c.dbg["dbg_loss"][:, 0:8], in_=Lsc[0:1, 0:8]), "dO")

            for m in range(NCH):
                @c.sec("sync")
                def _(e, m=m):
                    c.wcomp(e)
                    c.wait(e, "dG")
                    c.inc(e.dma_start(out=a32[:], in_=cd_d[m * 128:(m + 1) * 128, :]), "dL")
                    c.inc(e.dma_start(out=v32_[:], in_=cs_d[m * 128:(m + 1) * 128, :]), "dL")

                @c.sec("vector")
                def _(v, m=m):
                    c.wait(v, "dL")
                    c.inc(v.tensor_copy(seffm[:, 0:1], seff0[:, m:m + 1]), "V")

                @c.sec("vector")
                def _(v, m=m):
                    c.wait(v, "V")
                    c.inc(v.scalar_tensor_tensor(seffm[:, 1:32], Cpr[:, m * 32:m * 32 + 31],
                                                 seff0[:, m:m + 1], Lsc[:, m * 32:m * 32 + 31],
                                                 ALU.mult, ALU.add), "V")

                @c.sec("vector")
                def _(v, m=m):
                    c.wait(v, "V")
                    csv = v32_[:].rearrange("p (c k) -> p c k", k=64)
                    bc = seffm[:].unsqueeze(2).broadcast_to([128, 32, 64])
                    v.tensor_tensor(csv, csv, bc, ALU.add)
                    v.tensor_tensor(v32_[:], a32[:], v32_[:], ALU.mult)
                    c.split(v, v32_[:], rh16[:], rl16[:])
                    c.inc(v.engine_nop(), "V")

                @c.sec("gpsimd")
                def _(g, m=m):
                    c.wait(g, "V")
                    c.inc(g.dma_start(out=rech_d[m * 128:(m + 1) * 128, :], in_=rh16[:]), "dG")
                    c.inc(g.dma_start(out=recl_d[m * 128:(m + 1) * 128, :], in_=rl16[:]), "dG")
                    if DEBUG and m == 0:
                        c.inc(g.dma_start(out=c.dbg["dbg_xn"][0:128, :], in_=a32[:, 0:640]), "dO")
                        c.inc(g.dma_start(out=c.dbg["dbg_xn"][128:256, :], in_=v32_[:, 0:640]), "dO")

            if S3CUT <= 2:
                return

        self.relaxed = base_relax and ("f" in RELAX3) if RELAX3 else base_relax
        with ExitStack() as esb:
            es_op = ExitStack()
            rbh = c.sb("s3_rbh", [128, 8 * 512], BF16, es_op)
            rbl = c.sb("s3_rbl", [128, 8 * 512], BF16, es_op)

            for n in range(4):
                @c.sec("sync")
                def _(e, n=n):
                    c.wcomp(e)
                    c.wait(e, "dG")
                    for kk in range(NCH):
                        c.inc(e.dma_start(out=rbh[:, kk * 512:(kk + 1) * 512],
                                          in_=rech_d[kk * 128:(kk + 1) * 128, n * 512:(n + 1) * 512]), "dL")
                        c.inc(e.dma_start(out=rbl[:, kk * 512:(kk + 1) * 512],
                                          in_=recl_d[kk * 128:(kk + 1) * 128, n * 512:(n + 1) * 512]), "dL")

                @c.sec("vector")
                def _(v):
                    c.wait(v, "dL")
                    c.inc(v.engine_nop(), "V")

                if DEBUG:
                    @c.sec("vector")
                    def _(v, n=n):
                        c.wait(v, "dL")
                        v.tensor_reduce(dbgt[:, 40 + n:41 + n], rbh[:], AX, ALU.max, True)
                        c.inc(v.tensor_reduce(dbgt[:, 44 + n:45 + n], rbl[:], AX, ALU.max, True), "V")

                for m in range(NCH):
                    load_slab3("wopT", m)

                    @c.sec("tensor")
                    def _(t, m=m, n=n):
                        c.wait(t, "V")
                        for kk in range(NCH):
                            mm3(t, pb[0][:, 0:512], sl_h[:, kk * 128:(kk + 1) * 128],
                                sl_l[:, kk * 128:(kk + 1) * 128],
                                rbh[:, kk * 512:(kk + 1) * 512], rbl[:, kk * 512:(kk + 1) * 512],
                                kk == 0, kk == NCH - 1, inc=(kk == NCH - 1))

                    @c.sec("vector")
                    def _(v, m=m, n=n):
                        c.wait(v, "T")
                        sl = slice(n * 512, (n + 1) * 512)
                        c.inc(v.tensor_tensor(resid[m][:, sl], resid[m][:, sl], pb[0][:, 0:512],
                                              ALU.add), "V")

            if DEBUG:
                @c.sec("gpsimd")
                def _(g):
                    c.wcomp(g)
                    for k in range(NCH):
                        c.inc(g.dma_start(out=c.dbg["dbg_x2b"][k * 128:(k + 1) * 128, :], in_=resid[k][:]), "dO")

                @c.sec("vector")
                def _(v):
                    c.wait(v, "dO")
                    c.inc(v.engine_nop(), "V")

            es_op.close()
            if S3CUT <= 3:
                return
            self.relaxed = base_relax and ("g" in RELAX3) if RELAX3 else base_relax
            c.rms_stats(srcfn, arows2, 1e-6)

            hn2h = [c.sb(f"s3_hn2h{k}", [128, 512], BF16, esb) for k in range(NCH)]
            hn2l = [c.sb(f"s3_hn2l{k}", [128, 512], BF16, esb) for k in range(NCH)]
            arep2 = c.sb("s3_arep2", [128, 512], F32, esb)
            arow2 = c.sb("s3_arow2", [1, 512], F32, esb)
            hbuf = c.sb("s3_hbuf", [128, 512], F32, esb)

            dslA = [c.sb("s3_dsl32a", [128, 2048], F32, esb), c.sb("s3_dsl32b", [128, 2048], F32, esb)]
            dshA = [c.sb("s3_dslha", [128, 2048], BF16, esb), c.sb("s3_dslhb", [128, 2048], BF16, esb)]
            dslA_l = [c.sb("s3_dslla", [128, 2048], BF16, esb), c.sb("s3_dsllb", [128, 2048], BF16, esb)]
            suh = c.sb("s3_suh", [128, 32 * 512], BF16, esb)
            sul = c.sb("s3_sul", [128, 32 * 512], BF16, esb)
            g32 = [c.sb("s3_g32a", [128, 512], F32, esb), c.sb("s3_g32b", [128, 512], F32, esb)]
            u32 = [c.sb("s3_u32a", [128, 512], F32, esb), c.sb("s3_u32b", [128, 512], F32, esb)]

            consumed = [0, 0]
            split_v = [0, 0]
            evac_sc = [0, 0]
            evac_v = [0, 0]
            mm_t = [0, 0]
            dn_evac_v = [0, 0]
            dn_mm_t = [0, 0]
            su_guard_t = 0

            for nq in range(4):
                sl = slice(nq * 512, (nq + 1) * 512)
                @c.sec("sync")
                def _(e, nq=nq):
                    c.wcomp(e)
                    c.wait(e, "dG")
                    c.inc(e.dma_start(out=arow2[:], in_=arows2[0:1, nq * 512:(nq + 1) * 512]), "dL")

                @c.sec("vector")
                def _(v):
                    c.wait(v, "dL")
                    c.inc(v.engine_nop(), "V")

                c.rep(arow2, 0, arep2, 0, 512)
                for k in range(NCH):
                    @c.sec("vector")
                    def _(v, k=k, nq=nq):
                        c.wait(v, "T")
                        c.wait(v, "V")
                        v.scalar_tensor_tensor(hbuf[:], resid[k][:, sl], fnw_t[:, k:k + 1], arep2[:],
                                               ALU.mult, ALU.mult)
                        c.split(v, hbuf[:], hn2h[k][:], hn2l[k][:])
                        c.inc(v.engine_nop(), "V")
                hn2_v = c.n["V"]

                for j in range(32):
                    par = j % 2
                    pg, pu = (pb[0], pb[1]) if par == 0 else (pb[4], pb[5])

                    @c.sec("sync")
                    def _(e, j=j, par=par):
                        c.waitv(e, "T", consumed[par])
                        src2 = hb["wguT"][:, j * 256:(j + 1) * 256].rearrange("(s p) m -> p s m", p=128)
                        c.inc(e.dma_start(out=dslA[par][:].rearrange("p (s m) -> p s m", m=256), in_=src2), "dL")
                    dma_snap = c.n["dL"]

                    @c.sec("vector")
                    def _(v, par=par, dma_snap=dma_snap):
                        c.waitv(v, "dL", dma_snap)
                        c.waitv(v, "T", consumed[par])
                        c.split(v, dslA[par][:], dshA[par][:], dslA_l[par][:])
                        c.inc(v.engine_nop(), "V")
                    split_v[par] = c.n["V"]

                    @c.sec("tensor")
                    def _(t, j=j, par=par, nq=nq):
                        c.waitv(t, "V", max(split_v[par], evac_v[par], hn2_v))
                        c.waitv(t, "SC", evac_sc[par])
                        for kk in range(NCH):
                            mm3(t, pg[:, 0:512], dshA[par][:, kk * 256:kk * 256 + 128],
                                dslA_l[par][:, kk * 256:kk * 256 + 128],
                                hn2h[kk][:], hn2l[kk][:], kk == 0, kk == NCH - 1)
                        for kk in range(NCH):
                            mm3(t, pu[:, 0:512], dshA[par][:, kk * 256 + 128:kk * 256 + 256],
                                dslA_l[par][:, kk * 256 + 128:kk * 256 + 256],
                                hn2h[kk][:], hn2l[kk][:], kk == 0, kk == NCH - 1,
                                inc=(kk == NCH - 1))
                    mm_t[par] = c.n["T"]
                    consumed[par] = c.n["T"]

                    @c.sec("scalar")
                    def _(s, par=par):
                        c.waitv(s, "T", mm_t[par])
                        c.inc(s.activation(g32[par][:], pg[:, 0:512], AF.Silu), "SC")
                    silu_sc = c.n["SC"]

                    @c.sec("vector")
                    def _(v, par=par, silu_sc=silu_sc):
                        c.waitv(v, "T", mm_t[par])
                        c.inc(v.tensor_copy(u32[par][:], pu[:, 0:512]), "V")
                    ucopy_v = c.n["V"]

                    @c.sec("vector")
                    def _(v, j=j, par=par, silu_sc=silu_sc, ucopy_v=ucopy_v):
                        c.waitv(v, "SC", silu_sc)
                        c.waitv(v, "V", ucopy_v)
                        c.waitv(v, "T", su_guard_t)
                        v.tensor_tensor(g32[par][:], g32[par][:], u32[par][:], ALU.mult)
                        c.split(v, g32[par][:], suh[:, j * 512:(j + 1) * 512], sul[:, j * 512:(j + 1) * 512])
                        c.inc(v.engine_nop(), "V")
                    evac_v[par] = c.n["V"]
                    evac_sc[par] = c.n["SC"]
                su_v = c.n["V"]

                for m in range(NCH):
                    mp = m % 2
                    bank = pb[2] if mp == 0 else pb[3]
                    for half in range(2):
                        par = (2 * m + half) % 2

                        @c.sec("sync")
                        def _(e, m=m, half=half, par=par):
                            c.waitv(e, "T", consumed[par])
                            src2 = hb["wdnT"][half * 2048:(half + 1) * 2048, m * 128:(m + 1) * 128].rearrange(
                                "(s p) m -> p s m", p=128)
                            c.inc(e.dma_start(out=dslA[par][:].rearrange("p (s m) -> p s m", m=128), in_=src2), "dL")
                        dma_snap = c.n["dL"]

                        @c.sec("vector")
                        def _(v, par=par, dma_snap=dma_snap):
                            c.waitv(v, "dL", dma_snap)
                            c.waitv(v, "T", consumed[par])
                            c.split(v, dslA[par][:], dshA[par][:], dslA_l[par][:])
                            c.inc(v.engine_nop(), "V")
                        split_v[par] = c.n["V"]

                        @c.sec("tensor")
                        def _(t, m=m, half=half, par=par, mp=mp, bank=bank):
                            c.waitv(t, "V", max(split_v[par], su_v, dn_evac_v[mp]))
                            for kk in range(16):
                                kj = half * 16 + kk
                                mm3(t, bank[:, 0:512], dshA[par][:, kk * 128:(kk + 1) * 128],
                                    dslA_l[par][:, kk * 128:(kk + 1) * 128],
                                    suh[:, kj * 512:(kj + 1) * 512], sul[:, kj * 512:(kj + 1) * 512],
                                    kj == 0, kj == 31, inc=(kk == 15))
                        consumed[par] = c.n["T"]
                        if half == 1:
                            dn_mm_t[mp] = c.n["T"]

                    @c.sec("vector")
                    def _(v, m=m, nq=nq, mp=mp, bank=bank):
                        c.waitv(v, "T", dn_mm_t[mp])
                        c.inc(v.tensor_tensor(resid[m][:, sl], resid[m][:, sl], bank[:, 0:512],
                                              ALU.add), "V")
                    dn_evac_v[mp] = c.n["V"]
                su_guard_t = c.n["T"]

            for k in range(NCH):
                par = k % 2

                @c.sec("sync")
                def _(e, k=k, par=par):
                    c.wcomp(e)
                    c.inc(e.dma_start(out=dslA[par][:], in_=x2_d[k * 128:(k + 1) * 128, :]), "dL")

                @c.sec("vector")
                def _(v, k=k, par=par):
                    c.wait(v, "dL")
                    c.inc(v.tensor_tensor(resid[k][:], resid[k][:], dslA[par][:], ALU.add), "V")


    def s4_l1ffn(self, ses):
        c = self
        hb, resid = self.hb, self.resid
        nc = self.nc
        pb = c.pb

        abrows4 = nc.dram_tensor("s4_abrows", [2, TPC], F32).ap()
        mx_d = nc.dram_tensor("s4_mxd", [128], F32)
        mr_in = nc.dram_tensor("s4_mrin", [1], F32)
        mr_out = nc.dram_tensor("s4_mrout", [1], F32)

        def srcfn(k, j0, w):
            return resid[k][:, j0:j0 + w]
        c.ln_stats(TPC, srcfn, [(i * 512, 512) for i in range(4)], abrows4)

        xn32 = [c.sb(f"s4_xn{k}", [128, TPC], F32, ses) for k in range(NCH)]
        xqb = [c.sb(f"s4_xq{k}", [128, TPC], BF16, ses) for k in range(NCH)]
        xqi = c.sb("s4_xqi", [128, TPC], I8, ses)
        abg = c.sb("s4_abg", [128, 1024], F32, ses)
        abrow = c.sb("s4_abrow", [1, 1024], F32, ses)
        amax = c.sb("s4_amax", [128, 32], F32, ses)
        amax1 = c.sb("s4_amax1", [128, 1], F32, ses)
        arow128 = c.sb("s4_arow128", [1, 128], F32, ses)
        sc1 = c.sb("s4_sc1", [1, 1], F32, ses)
        scr = c.sb("s4_scr", [1, 1], F32, ses)
        screp = c.sb("s4_screp", [128, 1], F32, ses)
        wsc_t = c.sb("s4_wsc", [128, 8], F32, ses)
        qb_t = c.sb("s4_qb", [128, 8], F32, ses)
        cs_t = c.sb("s4_cs", [128, 8], F32, ses)
        sl32 = c.sb("s4_sl32", [128, 1024], I8, ses)
        sl_b = c.sb("s4_slb", [128, 1024], BF16, ses)
        t32 = c.sb("s4_t32", [128, 512], F32, ses)

        @c.sec("sync")
        def _(e):
            c.wcomp(e)
            c.inc(e.dma_start(out=wsc_t[:], in_=hb["wsc"].rearrange("(c p) -> p c", p=128),
                              allow_slow_non_contiguous=True), "dL")
            c.inc(e.dma_start(out=qb_t[:], in_=hb["qbias"].rearrange("(c p) -> p c", p=128),
                              allow_slow_non_contiguous=True), "dL")

        for n in range(4):
            j0 = n * 512

            @c.sec("sync")
            def _(e, j0=j0):
                c.wcomp(e)
                c.wait(e, "dG")
                c.inc(e.dma_start(out=abrow[:, 0:512], in_=abrows4[0:1, j0:j0 + 512]), "dL")
                c.inc(e.dma_start(out=abrow[:, 512:1024], in_=abrows4[1:2, j0:j0 + 512]), "dL")

            @c.sec("vector")
            def _(v):
                c.wait(v, "dL")
                c.inc(v.engine_nop(), "V")

            c.rep(abrow, 0, abg, 0, 512)
            c.rep(abrow, 512, abg, 512, 512)

            for k in range(NCH):
                @c.sec("vector")
                def _(v, k=k, j0=j0, n=n):
                    c.wait(v, "T")
                    sl = slice(j0, j0 + 512)
                    v.tensor_tensor(xn32[k][:, sl], resid[k][:, sl], abg[:, 0:512], ALU.mult)
                    v.tensor_tensor(xn32[k][:, sl], xn32[k][:, sl], abg[:, 512:1024], ALU.add)
                    c.inc(v.engine_nop(), "V")

                @c.sec("vector")
                def _(v, k=k, j0=j0, n=n):
                    c.wait(v, "V")
                    c.inc(v.tensor_reduce(amax[:, n * 8 + k:n * 8 + k + 1], xn32[k][:, j0:j0 + 512],
                                          AX, ALU.max, True), "V")

        @c.sec("vector")
        def _(v):
            c.wait(v, "V")
            c.inc(v.tensor_reduce(amax1[:], amax[:], AX, ALU.max), "V")

        @c.sec("gpsimd")
        def _(g):
            c.wait(g, "V")
            c.inc(g.dma_start(out=mx_d[:], in_=amax1[:, 0]), "dG")

        @c.sec("sync")
        def _(e):
            c.wait(e, "dG")
            c.inc(e.dma_start(out=arow128[:], in_=mx_d[:].unsqueeze(0)), "dL")

        @c.sec("vector")
        def _(v):
            c.wait(v, "dL")
            c.inc(v.tensor_reduce(sc1[:], arow128[:], AX, ALU.max), "V")

        @c.sec("gpsimd")
        def _(g):
            c.wait(g, "V")
            c.inc(g.dma_start(out=mr_in[:].unsqueeze(0), in_=sc1[:]), "dG")
            c.wait(g, "dG")
            c.inc(g.collective_compute("AllReduce", ALU.max,
                                       replica_groups=[[0, 1, 2, 3, 4, 5, 6, 7]],
                                       ins=[mr_in.ap().opt()], outs=[mr_out.ap().opt()]), "CC")
            c.wait(g, "CC")
            c.inc(g.dma_start(out=sc1[:], in_=mr_out[:].unsqueeze(0)), "dG")

        @c.sec("vector")
        def _(v):
            c.wait(v, "dG")
            v.tensor_scalar(sc1[:], sc1[:], 1.0 / 127.0, 1e-5, ALU.mult, ALU.max)
            c.inc(v.engine_nop(), "V")

        @c.sec("scalar")
        def _(s):
            c.wait(s, "V")
            c.inc(s.activation(scr[:], sc1[:], AF.Ln), "SC")

        @c.sec("scalar")
        def _(s):
            c.wait(s, "SC")
            c.inc(s.activation(scr[:], scr[:], AF.Exp, scale=-1.0), "SC")

        @c.sec("tensor")
        def _(t):
            c.wait(t, "SC")
            c.wait(t, "V")
            t.matmul(pb[2][:, 0:1], c.onesr[:], scr[:], start=True, stop=True)
            c.inc(t.matmul(pb[2][:, 1:2], c.onesr[:], sc1[:], start=True, stop=True), "T")

        @c.sec("vector")
        def _(v):
            c.wait(v, "T")
            c.inc(v.tensor_copy(screp[:], pb[2][:, 0:1]), "V")

        @c.sec("vector")
        def _(v):
            c.wait(v, "V")
            c.inc(v.tensor_scalar(cs_t[:], wsc_t[:], pb[2][:, 1:2], None, ALU.mult), "V")

        for k in range(NCH):
            @c.sec("vector")
            def _(v, k=k):
                c.wait(v, "V")
                c.inc(v.tensor_scalar(xn32[k][:], xn32[k][:], screp[:, 0:1], None, ALU.mult), "V")

            @c.sec("vector")
            def _(v, k=k):
                c.wait(v, "V")
                c.inc(v.tensor_copy(xqi[:], xn32[k][:]), "V")

            @c.sec("vector")
            def _(v, k=k):
                c.wait(v, "V")
                c.inc(v.tensor_copy(xqb[k][:], xqi[:]), "V")

        for m in range(NCH):
            @c.sec("sync")
            def _(e, m=m):
                c.wcomp(e)
                src2 = hb["wqT"][:, m * 128:(m + 1) * 128].rearrange("(s p) m -> p s m", p=128)
                c.inc(e.dma_start(out=sl32[:].rearrange("p (s m) -> p s m", m=128), in_=src2), "dL")

            @c.sec("vector")
            def _(v):
                c.wait(v, "dL")
                c.inc(v.tensor_copy(sl_b[:], sl32[:]), "V")

            for n in range(4):
                @c.sec("tensor")
                def _(t, m=m, n=n):
                    c.wait(t, "V")
                    for kk in range(NCH):
                        last = t.matmul(pb[0][:, 0:512], sl_b[:, kk * 128:(kk + 1) * 128],
                                        xqb[kk][:, n * 512:(n + 1) * 512],
                                        start=(kk == 0), stop=(kk == NCH - 1))
                    c.inc(last, "T")

                @c.sec("vector")
                def _(v, m=m, n=n):
                    c.wait(v, "T")
                    c.inc(v.scalar_tensor_tensor(t32[:], pb[0][:, 0:512], cs_t[:, m:m + 1],
                                                 resid[m][:, n * 512:(n + 1) * 512],
                                                 ALU.mult, ALU.add), "V")

                @c.sec("vector")
                def _(v, m=m, n=n):
                    c.wait(v, "V")
                    c.inc(v.tensor_scalar(resid[m][:, n * 512:(n + 1) * 512], t32[:],
                                          qb_t[:, m:m + 1], None, ALU.add), "V")



def build_program(upto=4):
    nc = bass.Bass("TRN2", target_bir_lowering=False)
    es = ExitStack()
    with es, nc.Block() as block:
        mk = MK(nc, block, es, upto)
        mk.build()
    return nc


def kernel(**inputs):
    nc = build_program(upto=4)
    maps = make_inputs(inputs)
    res = run_bass_kernel_spmd(nc, maps, core_ids=list(range(8)))
    full = np.empty((B, T, D), np.float32)
    for ci in range(8):
        b, t0 = ci // 4, (ci % 4) * TPC
        full[b, t0:t0 + TPC, :] = res.results[ci]["out"].T
    return full
